# version 03 (final)
# baseline (speedup 1.0000x reference)
import math
from contextlib import ExitStack

import numpy as np
import ml_dtypes
import concourse.bass as bass
import concourse.mybir as mybir
from concourse.bass_utils import run_bass_kernel_spmd

F32 = mybir.dt.float32
BF16 = mybir.dt.bfloat16
AF = mybir.ActivationFunctionType
ALU = mybir.AluOpType
AX = mybir.AxisListType

NCORES = 8
B, S, D = 4, 4096, 2048
TPC = 2048
NT = TPC // 128
ND = D // 128
LN_EPS = 1e-5
BIG = 30000.0

ENGINES = ("sync", "tensor", "vector", "scalar", "gpsimd")


class Ctx:

    def __init__(self, nc=None, tag=""):
        self.nc = nc or bass.Bass("TRN2", target_bir_lowering=False, debug=False)
        self.tag = tag
        self.stack = ExitStack()
        self.lanes = {e: [] for e in ENGINES}
        self.semv = {}
        self._sems = {}
        self._sbufs = {}
        self._psums = {}
        self.consts = {}

    def sem(self, name):
        if name not in self._sems:
            self._sems[name] = self.stack.enter_context(self.nc.semaphore(self.tag + name))
            self.semv[name] = 0
        return self._sems[name]

    def sbuf(self, name, shape, dtype):
        if name not in self._sbufs:
            self._sbufs[name] = self.stack.enter_context(
                self.nc.sbuf_tensor(self.tag + name, list(shape), dtype))
        return self._sbufs[name]

    def psum(self, name, shape, dtype=F32):
        if name not in self._psums:
            self._psums[name] = self.stack.enter_context(
                self.nc.psum_tensor(self.tag + name, list(shape), dtype))
        return self._psums[name]

    def dram_in(self, name, shape, dtype=F32):
        return self.nc.dram_tensor(name, list(shape), dtype, kind="ExternalInput").ap()

    def dram_out(self, name, shape, dtype=F32):
        return self.nc.dram_tensor(name, list(shape), dtype, kind="ExternalOutput").ap()

    def dram_tmp(self, name, shape, dtype=F32):
        return self.nc.dram_tensor(self.tag + name, list(shape), dtype, kind="Internal").ap()

    def const(self, name, arr):
        arr = np.ascontiguousarray(arr)
        reg = getattr(self.nc, "_const_reg", None)
        if reg is None:
            reg = self.nc._const_reg = {}
        self.consts[name] = arr
        if name in reg:
            ap, old = reg[name]
            assert old.shape == arr.shape and old.dtype == arr.dtype \
                and np.array_equal(old, arr), name
            return ap
        dt = {np.dtype(np.float32): F32, np.dtype(np.int32): mybir.dt.int32,
              np.dtype(ml_dtypes.bfloat16): BF16}[arr.dtype]
        ap = self.nc.dram_tensor(name, list(arr.shape), dt, kind="ExternalInput").ap()
        reg[name] = (ap, arr)
        return ap

    def emit(self, lane, fn, inc=None, waits=()):
        waits = tuple(waits)
        for sname, _ in waits:
            self.sem(sname)
        if inc is not None:
            self.sem(inc[0])
            self.semv[inc[0]] += inc[1]

        def thunk(eng, _fn=fn, _waits=waits, _inc=inc):
            for sname, v in _waits:
                if v > 0:
                    eng.wait_ge(self._sems[sname], v)
            ins = _fn(eng)
            if _inc is not None:
                assert ins is not None
                ins.then_inc(self._sems[_inc[0]], _inc[1])

        self.lanes[lane].append(thunk)

    def v(self, sname):
        return self.semv.get(sname, 0)

    def dma(self, lane, out, in_, inc, waits=(), **kw):
        self.emit(lane, lambda e: e.dma_start(out=out, in_=in_, **kw),
                  inc=(inc, 16), waits=waits)

    def wait(self, lane, waits):
        self.emit(lane, lambda e: None, waits=waits)

    def barrier(self, name):
        for ln in ENGINES:
            if self.lanes[ln]:
                pass
        raise NotImplementedError

    def finalize(self, close=True):
        with self.nc.Block() as blk:
            for lane in ENGINES:
                thunks = self.lanes[lane]
                getattr(blk, lane)(lambda eng, _t=tuple(thunks): [t(eng) for t in _t])
        if close:
            self.stack.close()
        return self.nc



def _io(ctx, io, kind, n, s, d=F32):
    if io and n in io:
        return io[n]
    f = {"in": ctx.dram_in, "out": ctx.dram_out}[kind]
    return f(n, s, d)


def emit_ln_to_fm(ctx, pre, x_dram, xn_fm_sb, ident_sb, n_tt=NT, nd=ND,
                  eps=LN_EPS, rms=False, extra_scale=None,
                  xn_bf_dram=None, xn32_dram=None, fm_col0=0, fm_stride=None,
                  x_row0=0, wait_extra=()):
    if fm_stride is None:
        fm_stride = n_tt * 128
    Dw = nd * 128
    xsb = ctx.sbuf(pre + "_x", [128, Dw], F32)
    xnb = ctx.sbuf(pre + "_nb", [128, Dw], BF16)
    junk = ctx.sbuf(pre + "_junk", [128, Dw], F32)
    st = ctx.sbuf(pre + "_st", [128, 8], F32)
    tp = ctx.psum(pre + "_tp", [128, 1024], BF16)
    ob0, o30 = ctx.v(pre + "ob"), ctx.v(pre + "o3")

    for tt in range(n_tt):
        ctx.dma("sync", xsb[:], x_dram[x_row0 + tt * 128:x_row0 + (tt + 1) * 128, :],
                pre + "ld", waits=[(pre + "ev", ctx.v(pre + "ev"))] + list(wait_extra))
        ldv = ctx.v(pre + "ld")
        sqw = [(pre + "ld", ldv)]
        if xn32_dram is not None and tt > 0:
            sqw.append((pre + "o3", o30 + 16 * tt))
        ctx.emit("scalar", lambda e: e.activation(out=junk[:], in_=xsb[:], func=AF.Square,
                                                  bias=0.0, scale=1.0, accum_out=st[:, 1:2]),
                 inc=(pre + "sq", 1), waits=sqw)
        if not rms:
            ctx.emit("vector", lambda e: e.reduce_sum(out=st[:, 0:1], in_=xsb[:], axis=AX.X),
                     inc=(pre + "sq", 1), waits=[(pre + "ld", ldv)])
        sqv = ctx.v(pre + "sq")
        if not rms:
            ctx.emit("vector", lambda e: e.tensor_scalar_mul(st[:, 2:3], st[:, 0:1], 1.0 / Dw),
                     inc=(pre + "s2", 1), waits=[(pre + "sq", sqv)])
            ctx.emit("vector", lambda e: e.scalar_tensor_tensor(out=st[:, 3:4], in0=st[:, 2:3],
                                                                scalar=-1.0, in1=st[:, 2:3],
                                                                op0=ALU.mult, op1=ALU.mult),
                     inc=(pre + "s2", 1), waits=[(pre + "s2", ctx.v(pre + "s2"))])
            ctx.emit("vector", lambda e: e.scalar_tensor_tensor(out=st[:, 3:4], in0=st[:, 1:2],
                                                                scalar=1.0 / Dw, in1=st[:, 3:4],
                                                                op0=ALU.mult, op1=ALU.add),
                     inc=(pre + "s2", 1), waits=[(pre + "s2", ctx.v(pre + "s2"))])
            ctx.emit("vector", lambda e: e.tensor_scalar_add(st[:, 3:4], st[:, 3:4], float(eps)),
                     inc=(pre + "s2", 1), waits=[(pre + "s2", ctx.v(pre + "s2"))])
        else:
            ctx.emit("vector", lambda e: e.tensor_scalar(st[:, 3:4], st[:, 1:2], 1.0 / Dw,
                                                         float(eps), op0=ALU.mult, op1=ALU.add),
                     inc=(pre + "s2", 1), waits=[(pre + "sq", sqv)])
        ctx.emit("scalar", lambda e: e.activation(out=st[:, 4:5], in_=st[:, 3:4], func=AF.Sqrt,
                                                  bias=0.0, scale=1.0),
                 inc=(pre + "s2", 1), waits=[(pre + "s2", ctx.v(pre + "s2"))])
        ctx.emit("vector", lambda e: e.reciprocal(out=st[:, 4:5], in_=st[:, 4:5]),
                 inc=(pre + "s2", 1), waits=[(pre + "s2", ctx.v(pre + "s2"))])
        if extra_scale is not None:
            ctx.emit("vector", lambda e: e.tensor_scalar_mul(st[:, 4:5], st[:, 4:5], float(extra_scale)),
                     inc=(pre + "s2", 1), waits=[(pre + "s2", ctx.v(pre + "s2"))])
        if not rms:
            ctx.emit("vector", lambda e: e.scalar_tensor_tensor(out=st[:, 5:6], in0=st[:, 2:3],
                                                                scalar=-1.0, in1=st[:, 4:5],
                                                                op0=ALU.mult, op1=ALU.mult),
                     inc=(pre + "s2", 1), waits=[(pre + "s2", ctx.v(pre + "s2"))])
        stv = ctx.v(pre + "s2")
        bias_ap = st[:, 5:6] if not rms else 0.0
        nmw = [(pre + "s2", stv)]
        if xn_bf_dram is not None and tt > 0:
            nmw.append((pre + "ob", ob0 + 16 * tt))
        ctx.emit("scalar", lambda e, b=bias_ap: e.activation(out=xnb[:], in_=xsb[:], func=AF.Identity,
                                                             bias=b, scale=st[:, 4:5]),
                 inc=(pre + "nm", 1), waits=nmw)
        nmv = ctx.v(pre + "nm")
        if xn_bf_dram is not None:
            ctx.dma("gpsimd", xn_bf_dram[x_row0 + tt * 128:x_row0 + (tt + 1) * 128, :],
                    xnb[:], pre + "ob", waits=[(pre + "nm", nmv)])
        if xn32_dram is not None:
            if not rms:
                ctx.emit("vector", lambda e: e.scalar_tensor_tensor(
                    out=junk[:], in0=xsb[:], scalar=st[:, 4:5],
                    in1=st[:, 5:6].to_broadcast([128, Dw]),
                    op0=ALU.mult, op1=ALU.add),
                    inc=(pre + "n32", 1), waits=[(pre + "s2", stv)])
            else:
                ctx.emit("vector", lambda e: e.tensor_scalar_mul(junk[:], xsb[:], st[:, 4:5]),
                         inc=(pre + "n32", 1), waits=[(pre + "s2", stv)])
            ctx.dma("gpsimd", xn32_dram[x_row0 + tt * 128:x_row0 + (tt + 1) * 128, :],
                    junk[:], pre + "o3", waits=[(pre + "n32", ctx.v(pre + "n32"))])
        tp2 = ctx.psum(pre + "_tq", [128, 1024], BF16)
        banks = [tp, tp2]
        for g in range(nd // 8):
            gidx = ctx.v(pre + "tp") // 8
            bank = banks[gidx % 2]
            gw = [(pre + "nm", nmv)]
            if gidx >= 2:
                gw.append((pre + "ev", 8 * (gidx - 1)))
            for i in range(8):
                d = g * 8 + i
                ctx.emit("tensor",
                         lambda e, d=d, i=i, bank=bank: e.transpose(
                             bank[:, i * 128:(i + 1) * 128],
                             xnb[:, d * 128:(d + 1) * 128], ident_sb[:]),
                         inc=(pre + "tp", 1), waits=(gw if i == 0 else []))
            tpv = ctx.v(pre + "tp")
            for i in range(8):
                d = g * 8 + i
                col = d * fm_stride + fm_col0 + tt * 128
                ctx.emit("vector",
                         lambda e, i=i, col=col, bank=bank: e.tensor_copy(
                             out=xn_fm_sb[:, col:col + 128], in_=bank[:, i * 128:(i + 1) * 128]),
                         inc=(pre + "ev", 1), waits=([(pre + "tp", tpv)] if i == 0 else []))
    return (pre + "ev", ctx.v(pre + "ev"))


def emit_proj(ctx, pre, xn_fm_sb, w_dram, out_dram, n_tok=TPC, nd=ND,
              mode="tok", F=None, scale=None, bias_dram=None, ones_sb=None,
              fm_ready=None, wait_extra=(), fm_base=0, fm_stride=None,
              w_col0=0, out_sb=None, out_col0=0, psA="pjA", psB="pjB",
              out_stage_dtype=BF16, out_row0=0, wname=None, oname=None,
              bias_fm_dram=None, w_scale_fm=None, w_scale_rdy=None,
              scale_ap=None, scale_ap_rdy=None):
    if fm_stride is None:
        fm_stride = n_tok
    nfc_w = (F // 512) if mode == "tok" else (F // 128)
    if mode == "tok":
        tchunks = [(i * 128, 128) for i in range(n_tok // 128)]
    else:
        tchunks = [(i * 512, 512) for i in range(n_tok // 512)]
        if n_tok % 512:
            assert n_tok % 128 == 0
            tchunks += [(512 * (n_tok // 512) + i * 128, 128)
                        for i in range((n_tok % 512) // 128)]
    ntc = len(tchunks)
    wW = 512 if mode == "tok" else 128
    wsb = ctx.sbuf((wname or pre + "_w"), [128, 2 * nd * wW], BF16)
    osb = ctx.sbuf((oname or pre + "_o"), [128, 2 * 512], out_stage_dtype) if out_sb is None else None
    bfm = None
    if bias_fm_dram is not None:
        bfm = ctx.sbuf(pre + "_bf", [128, F // 128], F32)
        if ctx.v(pre + "lbf") == 0:
            ctx.dma("gpsimd", bfm[:],
                    bias_fm_dram.rearrange("(fo p) -> p fo", p=128), pre + "lbf",
                    allow_slow_non_contiguous=True)
    bsb = None
    if bias_dram is not None:
        bsb = ctx.sbuf(pre + "_b", [1, F], BF16)
        if ctx.v(pre + "lb") == 0:
            ctx.dma("gpsimd", bsb[:],
                    bias_dram.rearrange("(o f) -> o f", o=1) if len(bias_dram.shape) == 1 else bias_dram,
                    pre + "lb")
    ps = [ctx.psum(psA, [128, 512]), ctx.psum(psB, [128, 512])]
    job = 0
    mm0, ev0 = ctx.v(pre + "mm"), ctx.v(pre + "ev")
    lw0 = [ctx.v(pre + "lw0"), ctx.v(pre + "lw1")]
    st0 = [ctx.v(pre + "st0"), ctx.v(pre + "st1")]
    for fc in range(nfc_w):
        par = fc % 2
        wlw = [(pre + "mm", mm0 + max(0, (fc - 1) * ntc))]
        if fc < 2:
            wlw += list(wait_extra)
        ctx.dma("gpsimd",
                wsb[:, par * nd * wW:(par + 1) * nd * wW].rearrange("p (dc f) -> p dc f", dc=nd),
                w_dram[:, w_col0 + fc * wW:w_col0 + (fc + 1) * wW]
                .rearrange("(dc p) f -> p dc f", p=128),
                pre + "lw%d" % par, waits=wlw)
        if w_scale_fm is not None:
            for d_ in range(nd):
                ws_w = [(pre + "lw%d" % par, lw0[par] + 16 * (fc // 2 + 1)),
                        (pre + "mm", mm0 + max(0, (fc - 1) * ntc))]
                if w_scale_rdy is not None:
                    ws_w.append(w_scale_rdy)
                ctx.emit("vector", lambda e, par=par, d_=d_: e.tensor_scalar_mul(
                    wsb[:, par * nd * wW + d_ * wW:par * nd * wW + (d_ + 1) * wW],
                    wsb[:, par * nd * wW + d_ * wW:par * nd * wW + (d_ + 1) * wW],
                    w_scale_fm[:, d_:d_ + 1]),
                    inc=(pre + "lws%d" % par, 1), waits=ws_w)
        for tc, (toff, twid) in enumerate(tchunks):
            bank = job % 2
            waits = [(pre + "lw%d" % par, lw0[par] + 16 * (fc // 2 + 1)),
                     (pre + "ev", ev0 + max(0, job - 1))]
            if w_scale_fm is not None:
                waits.append((pre + "lws%d" % par, ctx.v(pre + "lws%d" % par)))
            if bsb is not None:
                waits.append((pre + "lb", 16))
            if fm_ready is not None:
                waits.append(fm_ready)
            waits.extend(wait_extra)

            def mmfn(e, fc=fc, toff=toff, twid=twid, par=par, bank=bank):
                ins = None
                for d in range(nd):
                    if mode == "tok":
                        lhs = xn_fm_sb[:, d * fm_stride + fm_base + toff:
                                       d * fm_stride + fm_base + toff + twid]
                        rhs = wsb[:, par * nd * 512 + d * 512:par * nd * 512 + (d + 1) * 512]
                    else:
                        lhs = wsb[:, par * nd * 128 + d * 128:par * nd * 128 + (d + 1) * 128]
                        rhs = xn_fm_sb[:, d * fm_stride + fm_base + toff:
                                       d * fm_stride + fm_base + toff + twid]
                    ins = e.matmul(ps[bank][:, 0:(512 if mode == "tok" else twid)],
                                   lhs, rhs, start=(d == 0),
                                   stop=(d == nd - 1 and bsb is None))
                if bsb is not None:
                    ins = e.matmul(ps[bank][:, 0:512], ones_sb[0:1, 0:128],
                                   bsb[0:1, fc * 512:(fc + 1) * 512], start=False, stop=True)
                return ins

            ctx.emit("tensor", mmfn, inc=(pre + "mm", 1), waits=waits)
            ow = 512 if mode == "tok" else twid
            if out_sb is not None:
                ob = out_sb[:, (toff // 128) * F + out_col0 + fc * 512:
                            (toff // 128) * F + out_col0 + (fc + 1) * 512]
            else:
                ob = osb[:, bank * 512:bank * 512 + ow]
            evw = [(pre + "mm", ctx.v(pre + "mm"))]
            if out_sb is None and out_dram is not None:
                if job >= 2:
                    evw.append((pre + "st%d" % bank, st0[bank] + 16 * (job // 2)))
                elif st0[bank] > 0:
                    evw.append((pre + "st%d" % bank, st0[bank]))
            if job < 2:
                evw += list(wait_extra)
            if scale_ap is not None:
                if scale_ap_rdy is not None:
                    evw.append(scale_ap_rdy)
                ctx.emit("scalar", lambda e, ob=ob, bank=bank, ow=ow: e.activation(
                    out=ob, in_=ps[bank][:, 0:ow], func=AF.Identity,
                    bias=0.0, scale=scale_ap),
                    inc=(pre + "ev", 1), waits=evw)
            elif bfm is not None:
                evw.append((pre + "lbf", 16))
                ctx.emit("scalar", lambda e, ob=ob, bank=bank, ow=ow, fc=fc: e.activation(
                    out=ob, in_=ps[bank][:, 0:ow], func=AF.Identity,
                    bias=bfm[:, fc:fc + 1], scale=float(scale if scale is not None else 1.0)),
                    inc=(pre + "ev", 1), waits=evw)
            elif scale is not None:
                ctx.emit("scalar", lambda e, ob=ob, bank=bank, ow=ow: e.activation(
                    out=ob, in_=ps[bank][:, 0:ow], func=AF.Identity, bias=0.0, scale=float(scale)),
                    inc=(pre + "ev", 1), waits=evw)
            else:
                ctx.emit("vector", lambda e, ob=ob, bank=bank, ow=ow: e.tensor_copy(
                    out=ob, in_=ps[bank][:, 0:ow]),
                    inc=(pre + "ev", 1), waits=evw)
            if out_dram is not None:
                if mode == "tok":
                    dst = out_dram[out_row0 + toff:out_row0 + toff + twid,
                                   fc * 512:(fc + 1) * 512]
                else:
                    dst = out_dram[fc * 128:(fc + 1) * 128, toff:toff + twid]
                ctx.dma("sync", dst, ob, pre + "st%d" % bank, waits=[(pre + "ev", ctx.v(pre + "ev"))])
            job += 1
    nst = ctx.v(pre + "st0") + ctx.v(pre + "st1")
    if out_dram is not None:
        return [(pre + "st0", ctx.v(pre + "st0")), (pre + "st1", ctx.v(pre + "st1"))]
    return [(pre + "ev", ctx.v(pre + "ev"))]


def emit_ln_add(ctx, pre, y_dram, x_dram, out_dram, n_tt=NT, nd=ND, eps=LN_EPS,
                wait_extra=()):
    Dw = nd * 128
    ysb = ctx.sbuf(pre + "_y", [128, Dw], F32)
    xsb = ctx.sbuf(pre + "_x", [128, Dw], F32)
    osb = ctx.sbuf(pre + "_oo", [128, Dw], F32)
    st = ctx.sbuf(pre + "_st", [128, 8], F32)
    junk = ctx.sbuf(pre + "_junk", [128, Dw], F32)
    so0 = ctx.v(pre + "so")
    for tt in range(n_tt):
        ctx.dma("sync", ysb[:], y_dram[tt * 128:(tt + 1) * 128, :], pre + "ld",
                waits=[(pre + "ad", ctx.v(pre + "ad"))] + list(wait_extra))
        ctx.dma("sync", xsb[:], x_dram[tt * 128:(tt + 1) * 128, :], pre + "ld",
                waits=[(pre + "ad", ctx.v(pre + "ad"))])
        ldv = ctx.v(pre + "ld")
        ctx.emit("scalar", lambda e: e.activation(out=junk[:], in_=ysb[:], func=AF.Square,
                                                  bias=0.0, scale=1.0, accum_out=st[:, 1:2]),
                 inc=(pre + "sq", 1), waits=[(pre + "ld", ldv)])
        ctx.emit("vector", lambda e: e.reduce_sum(out=st[:, 0:1], in_=ysb[:], axis=AX.X),
                 inc=(pre + "sq", 1), waits=[(pre + "ld", ldv)])
        sqv = ctx.v(pre + "sq")
        ctx.emit("vector", lambda e: e.tensor_scalar_mul(st[:, 2:3], st[:, 0:1], 1.0 / Dw),
                 inc=(pre + "s2", 1), waits=[(pre + "sq", sqv)])
        ctx.emit("vector", lambda e: e.scalar_tensor_tensor(out=st[:, 3:4], in0=st[:, 2:3],
                                                            scalar=-1.0, in1=st[:, 2:3],
                                                            op0=ALU.mult, op1=ALU.mult),
                 inc=(pre + "s2", 1), waits=[(pre + "s2", ctx.v(pre + "s2"))])
        ctx.emit("vector", lambda e: e.scalar_tensor_tensor(out=st[:, 3:4], in0=st[:, 1:2],
                                                            scalar=1.0 / Dw, in1=st[:, 3:4],
                                                            op0=ALU.mult, op1=ALU.add),
                 inc=(pre + "s2", 1), waits=[(pre + "s2", ctx.v(pre + "s2"))])
        ctx.emit("vector", lambda e: e.tensor_scalar_add(st[:, 3:4], st[:, 3:4], float(eps)),
                 inc=(pre + "s2", 1), waits=[(pre + "s2", ctx.v(pre + "s2"))])
        ctx.emit("scalar", lambda e: e.activation(out=st[:, 4:5], in_=st[:, 3:4], func=AF.Sqrt,
                                                  bias=0.0, scale=1.0),
                 inc=(pre + "s2", 1), waits=[(pre + "s2", ctx.v(pre + "s2"))])
        ctx.emit("vector", lambda e: e.reciprocal(out=st[:, 4:5], in_=st[:, 4:5]),
                 inc=(pre + "s2", 1), waits=[(pre + "s2", ctx.v(pre + "s2"))])
        ctx.emit("vector", lambda e: e.scalar_tensor_tensor(out=st[:, 5:6], in0=st[:, 2:3],
                                                            scalar=-1.0, in1=st[:, 4:5],
                                                            op0=ALU.mult, op1=ALU.mult),
                 inc=(pre + "s2", 1), waits=[(pre + "s2", ctx.v(pre + "s2"))])
        stv = ctx.v(pre + "s2")
        ctx.emit("vector", lambda e: e.scalar_tensor_tensor(
            out=osb[:], in0=ysb[:], scalar=st[:, 4:5],
            in1=st[:, 5:6].to_broadcast([128, Dw]), op0=ALU.mult, op1=ALU.add),
            inc=(pre + "nm", 1), waits=[(pre + "s2", stv), (pre + "so", so0 + 16 * tt)])
        ctx.emit("vector", lambda e: e.tensor_add(osb[:], osb[:], xsb[:]),
                 inc=(pre + "ad", 1), waits=[(pre + "nm", ctx.v(pre + "nm"))])
        ctx.dma("gpsimd", out_dram[tt * 128:(tt + 1) * 128, :], osb[:], pre + "so",
                waits=[(pre + "ad", ctx.v(pre + "ad"))])
    return (pre + "so", ctx.v(pre + "so"))


def emit_l0_flash(ctx, pre, qt_dram, kt_own, kt_sib, v_own, v_sib, ot_dram,
                  sibmask_dram, masks_dram, ready=()):
    ksb = ctx.sbuf(pre + "_k", [128, 2 * 4096], BF16)
    vsb = ctx.sbuf(pre + "_v", [128, 2 * 4096], BF16)
    qsb = ctx.sbuf(pre + "_q", [128, 2 * 2048], BF16)
    ptb = ctx.sbuf(pre + "_pt", [128, 2 * 1024], BF16)
    mstat = ctx.sbuf(pre + "_ms", [128, 16], F32)
    mtr = ctx.sbuf(pre + "_mt", [128, 128], F32)
    mrow = ctx.sbuf(pre + "_mr", [1, 128], F32)
    rl = ctx.sbuf(pre + "_rl", [1, 512], F32)
    rlb = ctx.sbuf(pre + "_rlb", [128, 512], F32)
    biasv = ctx.sbuf(pre + "_bv", [128, 4], F32)
    sibm = ctx.sbuf(pre + "_sm", [128, 1], F32)
    masks = ctx.sbuf(pre + "_mk", [128, 4 * 512], F32)
    otb = ctx.sbuf(pre + "_otb", [128, 512], BF16)
    onesf = ctx.sbuf(pre + "_of", [1, 128], F32)
    onesk = ctx.sbuf(pre + "_ok", [128, 1], BF16)
    identf = ctx.sbuf(pre + "_idf", [128, 128], F32)

    psT = [ctx.psum(pre + "_psT0", [128, 1024]), ctx.psum(pre + "_psT1", [128, 1024])]
    psO = ctx.psum(pre + "_psO", [128, 512])
    psL = ctx.psum(pre + "_psL", [128, 512])
    psB = ctx.psum(pre + "_psB", [128, 512])

    onesd = ctx.const(pre + "_ones128f", np.ones((128, 128), dtype=np.float32))
    onesb = ctx.const(pre + "_ones128b", np.ones((128, 1), dtype=ml_dtypes.bfloat16))
    identd = ctx.const(pre + "_identf32", np.eye(128, dtype=np.float32))
    ctx.dma("sync", sibm[:], sibmask_dram, pre + "cl", waits=ready)
    for m in range(4):
        ctx.dma("sync", masks[:, m * 512:(m + 1) * 512], masks_dram[m], pre + "cl")
    ctx.dma("sync", onesf[:], onesd[0:1, :], pre + "cl")
    ctx.dma("sync", onesk[:], onesb, pre + "cl")
    ctx.dma("sync", identf[:], identd, pre + "cl")
    clv = ctx.v(pre + "cl")
    for lane in ("tensor", "scalar", "vector"):
        ctx.wait(lane, [(pre + "cl", clv)])

    def head_load(h, par):
        w = [(pre + "hd", 16 * 4 * (h - 1))] if h >= 2 else []
        ctx.dma("gpsimd", ksb[:, par * 4096:par * 4096 + 2048],
                kt_sib[h * 128:(h + 1) * 128, :], pre + "lk%d" % par, waits=w)
        ctx.dma("gpsimd", ksb[:, par * 4096 + 2048:par * 4096 + 4096],
                kt_own[h * 128:(h + 1) * 128, :], pre + "lk%d" % par)
        ctx.dma("gpsimd", vsb[:, par * 4096:par * 4096 + 2048].rearrange("p (kt f) -> p kt f", kt=16),
                v_sib[:, h * 128:(h + 1) * 128].rearrange("(kt p) f -> p kt f", p=128),
                pre + "lv%d" % par)
        ctx.dma("gpsimd", vsb[:, par * 4096 + 2048:par * 4096 + 4096].rearrange("p (kt f) -> p kt f", kt=16),
                v_own[:, h * 128:(h + 1) * 128].rearrange("(kt p) f -> p kt f", p=128),
                pre + "lv%d" % par)
        ctx.dma("gpsimd", qsb[:, par * 2048:(par + 1) * 2048],
                qt_dram[h * 128:(h + 1) * 128, :], pre + "lq%d" % par)

    head_load(0, 0)
    head_load(1, 1)
    for h in range(16):
        par = h % 2
        kof, qof = par * 4096, par * 2048
        nload = 16 * (h // 2 + 1)
        ctx.wait("tensor", [(pre + "lk%d" % par, 2 * nload), (pre + "lv%d" % par, 2 * nload),
                            (pre + "lq%d" % par, nload)])
        for qsc in range(4):
            n_own = 4 * qsc + 4
            ktiles = 16 + n_own
            qslice = qsb[:, qof + qsc * 512: qof + (qsc + 1) * 512]
            nkb = (ktiles * 128) // 512
            jobs = [(qc, kb) for kb in range(nkb) for qc in range(4)]
            for jp in range(0, len(jobs), 2):
                pj = jp // 2
                bank = psT[pj % 2]
                w = [(pre + "am", ctx.v(pre + "am") - 1)] if ctx.v(pre + "am") >= 2 else []
                if pj < 2:
                    w.append((pre + "ex", ctx.v(pre + "ex")))

                def afn(e, jobs=jobs, jp=jp, bank=bank, kof=kof, qof=qof, qsc=qsc):
                    ins = None
                    for half, (qc, kb) in enumerate(jobs[jp:jp + 2]):
                        ins = e.matmul(bank[:, half * 512:(half + 1) * 512],
                                       qsb[:, qof + (qsc * 4 + qc) * 128: qof + (qsc * 4 + qc + 1) * 128],
                                       ksb[:, kof + kb * 512: kof + (kb + 1) * 512],
                                       start=True, stop=True)
                    return ins
                ctx.emit("tensor", afn, inc=(pre + "as", 1), waits=w)
                mw = [(pre + "as", ctx.v(pre + "as"))]
                if pj == 0:
                    mw.append((pre + "mx", ctx.v(pre + "mx")))
                ctx.emit("vector", lambda e, pj=pj, bank=bank: e.reduce_max(
                    out=mstat[:, pj % 16:pj % 16 + 1], in_=bank[:, 0:1024], axis=AX.X),
                    inc=(pre + "am", 1), waits=mw)
            npairs_a = len(jobs) // 2
            amv = ctx.v(pre + "am")
            ctx.emit("vector", lambda e, n=npairs_a: e.reduce_max(
                out=mtr[:, 0:1], in_=mstat[:, 0:n], axis=AX.X),
                inc=(pre + "mx", 1), waits=[(pre + "am", amv)])
            ctx.emit("vector", lambda e: e.memset(mtr[:, 1:128], -BIG),
                     inc=(pre + "mx", 1), waits=[])
            ctx.emit("tensor", lambda e: e.transpose(psB[:, 0:128], mtr[:], identf[:]),
                     inc=(pre + "mx", 1), waits=[(pre + "mx", ctx.v(pre + "mx")),
                                                 (pre + "fin", ctx.v(pre + "fin"))])
            ctx.emit("vector", lambda e: e.reduce_max(out=mrow[0:1, 0:1], in_=psB[0:1, 0:128], axis=AX.X),
                     inc=(pre + "mx", 1), waits=[(pre + "mx", ctx.v(pre + "mx"))])
            ctx.emit("tensor", lambda e: e.matmul(psB[:, 128:129], onesf[:], mrow[0:1, 0:1],
                                                  start=True, stop=True),
                     inc=(pre + "mx", 1), waits=[(pre + "mx", ctx.v(pre + "mx"))])
            ctx.emit("vector", lambda e: e.tensor_scalar_mul(biasv[:, 0:1], psB[:, 128:129], -1.0),
                     inc=(pre + "mx", 1), waits=[(pre + "mx", ctx.v(pre + "mx"))])
            ctx.emit("vector", lambda e: e.tensor_add(biasv[:, 1:2], biasv[:, 0:1], sibm[:]),
                     inc=(pre + "mx", 1), waits=[(pre + "mx", ctx.v(pre + "mx"))])
            mxv = ctx.v(pre + "mx")
            npair = ktiles // 2
            for j in range(npair):
                bank = psT[j % 2]
                pt = ptb[:, (j % 2) * 1024:(j % 2 + 1) * 1024]
                w = [(pre + "mx", mxv)]
                if j >= 2:
                    w.append((pre + "ex", ctx.v(pre + "ex") - 1))

                def tfn(e, j=j, bank=bank, kof=kof, qslice=qslice):
                    ins = None
                    for half in range(2):
                        kt = 2 * j + half
                        ins = e.matmul(bank[:, half * 512:(half + 1) * 512],
                                       ksb[:, kof + kt * 128: kof + (kt + 1) * 128],
                                       qslice, start=True, stop=True)
                    return ins
                ctx.emit("tensor", tfn, inc=(pre + "ts", 1), waits=w)
                for half in range(2):
                    kt = 2 * j + half
                    m = kt - 16 - 4 * qsc
                    if 0 <= m <= 3:
                        ctx.emit("vector", lambda e, half=half, m=m, bank=bank: e.tensor_add(
                            bank[:, half * 512:(half + 1) * 512],
                            bank[:, half * 512:(half + 1) * 512],
                            masks[:, m * 512:(m + 1) * 512]),
                            inc=(pre + "ts", 1), waits=[(pre + "ts", ctx.v(pre + "ts"))])
                bias = biasv[:, 1:2] if (2 * j + 1) < 16 else biasv[:, 0:1]
                w = [(pre + "ts", ctx.v(pre + "ts"))]
                if j >= 2:
                    w.append((pre + "pv", ctx.v(pre + "pv") - 4))
                ctx.emit("scalar", lambda e, bank=bank, pt=pt, bias=bias: e.activation(
                    out=pt, in_=bank[:, 0:1024], func=AF.Exp, bias=bias, scale=1.0),
                    inc=(pre + "ex", 1), waits=w)
                exv = ctx.v(pre + "ex")

                def pvfn(e, j=j, pt=pt, kof=kof, npair=npair):
                    ins = None
                    for half in range(2):
                        kt = 2 * j + half
                        e.matmul(psO[:], vsb[:, kof + kt * 128: kof + (kt + 1) * 128],
                                 pt[:, half * 512:(half + 1) * 512],
                                 start=(j == 0 and half == 0), stop=(j == npair - 1 and half == 1))
                        ins = e.matmul(psL[0:1, 0:512], onesk[:],
                                       pt[:, half * 512:(half + 1) * 512],
                                       start=(j == 0 and half == 0), stop=(j == npair - 1 and half == 1))
                    return ins
                w = [(pre + "ex", exv)]
                if j == 0:
                    w.append((pre + "fin", ctx.v(pre + "fin")))
                ctx.emit("tensor", pvfn, inc=(pre + "pv", 4), waits=w)
            pvv = ctx.v(pre + "pv")
            ctx.emit("vector", lambda e: e.reciprocal(out=rl[:], in_=psL[0:1, 0:512]),
                     inc=(pre + "fz", 1), waits=[(pre + "pv", pvv)])
            ctx.emit("tensor", lambda e: e.matmul(psB[:, 0:512], onesf[:], rl[:],
                                                  start=True, stop=True),
                     inc=(pre + "fz", 1), waits=[(pre + "fz", ctx.v(pre + "fz"))])
            ctx.emit("vector", lambda e: e.tensor_copy(out=rlb[:], in_=psB[:, 0:512]),
                     inc=(pre + "fz", 1), waits=[(pre + "fz", ctx.v(pre + "fz"))])
            ctx.emit("vector", lambda e: e.scalar_tensor_tensor(
                out=otb[:], in0=psO[:], scalar=1.0, in1=rlb[:], op0=ALU.mult, op1=ALU.mult),
                inc=(pre + "fz", 1), waits=[(pre + "fz", ctx.v(pre + "fz")),
                                            (pre + "hd", 16 * (4 * h + qsc))])
            ctx.dma("gpsimd", ot_dram[h * 128:(h + 1) * 128, qsc * 512:(qsc + 1) * 512], otb[:],
                    pre + "hd", waits=[(pre + "fz", ctx.v(pre + "fz"))])
            ctx.emit("vector", lambda e: e.tensor_copy(out=mtr[:, 1:2], in_=rlb[:, 0:1]),
                     inc=(pre + "fin", 1), waits=[(pre + "fz", ctx.v(pre + "fz"))])
        if h + 2 < 16:
            head_load(h + 2, par)
    return (pre + "hd", ctx.v(pre + "hd"))



def setup_common(ctx):
    ident_sb = ctx.sbuf("ident_sb", [128, 128], BF16)
    identb = ctx.const("identb16", np.eye(128, dtype=ml_dtypes.bfloat16))
    ctx.dma("sync", ident_sb[:], identb, "ldi")
    ones_sb = ctx.sbuf("ones_sb", [1, 512], BF16)
    onesc = ctx.const("ones512", np.ones((1, 512), dtype=ml_dtypes.bfloat16))
    ctx.dma("sync", ones_sb[:], onesc, "ldo")
    for lane in ("tensor", "vector", "scalar", "gpsimd"):
        ctx.wait(lane, [("ldi", 16), ("ldo", 16)])
    return ident_sb, ones_sb


def build_l0a(parts="lnqkv", ctx=None, io=None):
    ctx = ctx or Ctx()
    io = io or {}
    x = _io(ctx, io, "in", "xin", [TPC, D])
    wq = _io(ctx, io, "in", "wq", [D, D])
    wk = _io(ctx, io, "in", "wk", [D, D])
    wv = _io(ctx, io, "in", "wv", [D, D])
    qt = _io(ctx, io, "out", "QT", [D, TPC], BF16)
    kt = _io(ctx, io, "out", "KT", [D, TPC], BF16)
    vo = _io(ctx, io, "out", "V", [TPC, D], BF16)
    ident_sb, ones_sb = setup_common(ctx)
    xn_fm = ctx.sbuf("xn_fm", [128, ND * TPC], BF16)
    fmr = emit_ln_to_fm(ctx, "ln", x, xn_fm, ident_sb)
    fin = [fmr]
    if "q" in parts:
        fin += emit_proj(ctx, "q", xn_fm, wq, qt, mode="fm", F=D, scale=1.0 / math.sqrt(128.0),
                         fm_ready=fmr, psA="p0", psB="p1")
    if "k" in parts:
        fin += emit_proj(ctx, "k", xn_fm, wk, kt, mode="fm", F=D, fm_ready=fmr, psA="p2", psB="p3")
    if "v" in parts:
        fin += emit_proj(ctx, "v", xn_fm, wv, vo, mode="tok", F=D, fm_ready=fmr, psA="p4", psB="p5")
    ctx.wait("gpsimd", fin)
    return ctx



def emit_conv_silu(ctx, pre, raw_dram, convw_dram, out_dram, F, ready=(), bufpre=None):
    nfc = F // 128
    bp = bufpre or pre
    raw = ctx.sbuf(bp + "_raw", [128, 2 * 2176], BF16)
    cw = ctx.sbuf(bp + "_cw", [128, 2 * 4], F32)
    cv = ctx.sbuf(bp + "_cv", [128, 2048], F32)
    sil = ctx.sbuf(bp + "_sil", [128, 2048], BF16)
    cd0, st0_ = ctx.v(pre + "cd"), ctx.v(pre + "st")
    for fc in range(nfc):
        par = fc % 2
        w = [(pre + "cd", cd0 + max(0, 4 * (fc - 1)))] + list(ready)
        ctx.dma("sync", raw[:, par * 2176:(par + 1) * 2176],
                raw_dram[fc * 128:(fc + 1) * 128, :], pre + "ld%d" % par, waits=w)
        ctx.dma("sync", cw[:, par * 4:(par + 1) * 4],
                convw_dram[fc * 128:(fc + 1) * 128, :], pre + "ld%d" % par)
        ldv = ctx.v(pre + "ld%d" % par)
        ctx.emit("vector", lambda e, par=par: e.tensor_scalar_mul(
            cv[:], raw[:, par * 2176 + 125: par * 2176 + 125 + 2048],
            cw[:, par * 4: par * 4 + 1]),
            inc=(pre + "cd", 1), waits=[(pre + "ld%d" % par, ldv),
                                        (pre + "sl", ctx.v(pre + "sl"))])
        for j in range(1, 4):
            ctx.emit("vector", lambda e, par=par, j=j: e.scalar_tensor_tensor(
                out=cv[:], in0=raw[:, par * 2176 + 125 + j: par * 2176 + 125 + j + 2048],
                scalar=cw[:, par * 4 + j: par * 4 + j + 1], in1=cv[:],
                op0=ALU.mult, op1=ALU.add),
                inc=(pre + "cd", 1), waits=[(pre + "cd", ctx.v(pre + "cd"))])
        ctx.emit("scalar", lambda e: e.activation(out=sil[:], in_=cv[:], func=AF.Silu,
                                                  bias=0.0, scale=1.0),
                 inc=(pre + "sl", 1), waits=[(pre + "cd", ctx.v(pre + "cd")),
                                             (pre + "st", st0_ + 16 * fc)])
        ctx.dma("gpsimd", out_dram[fc * 128:(fc + 1) * 128, :], sil[:], pre + "st",
                waits=[(pre + "sl", ctx.v(pre + "sl"))])
    return (pre + "st", ctx.v(pre + "st"))


def emit_l2norm_fm(ctx, pre, src_dram, out_dram, onesf_sb, F=1536, dh=256, scale=None,
                   eps=1e-6, ready=(), bufpre=None):
    nh = F // dh
    nfc_h = dh // 128
    bp = bufpre or pre
    sb = ctx.sbuf(bp + "_s", [128, nfc_h * 2048], BF16)
    sq = ctx.sbuf(bp + "_q", [128, 2048], F32)
    nrow = ctx.sbuf(bp + "_nr", [1, 2048], F32)
    rb = ctx.sbuf(bp + "_rb", [128, 2048], F32)
    ob = ctx.sbuf(bp + "_ob", [128, 2048], BF16)
    nps = ctx.psum(bp + "_np", [128, 512])
    so0 = ctx.v(pre + "so")
    for h in range(nh):
        for i in range(nfc_h):
            ctx.dma("sync", sb[:, i * 2048:(i + 1) * 2048],
                    src_dram[(h * nfc_h + i) * 128:(h * nfc_h + i + 1) * 128, :],
                    pre + "ld", waits=[(pre + "so", so0 + 16 * nfc_h * h)] + list(ready))
        ldv = ctx.v(pre + "ld")
        for cc in range(4):
            for i in range(nfc_h):
                ctx.emit("scalar", lambda e, cc=cc, i=i: e.activation(
                    out=sq[:, cc * 512:(cc + 1) * 512],
                    in_=sb[:, i * 2048 + cc * 512: i * 2048 + (cc + 1) * 512],
                    func=AF.Square, bias=0.0, scale=1.0),
                    inc=(pre + "sc", 1), waits=[(pre + "ld", ldv),
                                                (pre + "nm", ctx.v(pre + "nm"))])
                mw = [(pre + "sc", ctx.v(pre + "sc"))]
                if i == 0:
                    mw.append((pre + "nm", ctx.v(pre + "nm")))
                ctx.emit("tensor", lambda e, cc=cc, i=i: e.matmul(
                    nps[0:1, 0:512], onesf_sb[0:128, 0:1], sq[:, cc * 512:(cc + 1) * 512],
                    start=(i == 0), stop=(i == nfc_h - 1)),
                    inc=(pre + "nm", 1), waits=mw)
            ctx.emit("vector", lambda e, cc=cc: e.tensor_scalar_add(
                nrow[0:1, cc * 512:(cc + 1) * 512], nps[0:1, 0:512], float(eps)),
                inc=(pre + "nm", 1), waits=[(pre + "nm", ctx.v(pre + "nm"))])
        ctx.emit("scalar", lambda e: e.activation(out=nrow[:], in_=nrow[:], func=AF.Sqrt,
                                                  bias=0.0, scale=1.0),
                 inc=(pre + "nm", 1), waits=[(pre + "nm", ctx.v(pre + "nm"))])
        ctx.emit("vector", lambda e: e.reciprocal(out=nrow[:], in_=nrow[:]),
                 inc=(pre + "nm", 1), waits=[(pre + "nm", ctx.v(pre + "nm"))])
        if scale is not None:
            ctx.emit("vector", lambda e: e.tensor_scalar_mul(nrow[:], nrow[:], float(scale)),
                     inc=(pre + "nm", 1), waits=[(pre + "nm", ctx.v(pre + "nm"))])
        nmv = ctx.v(pre + "nm")
        for cc in range(4):
            ctx.emit("tensor", lambda e, cc=cc: e.matmul(
                nps[:, 0:512], onesf_sb[0:1, 0:128], nrow[0:1, cc * 512:(cc + 1) * 512],
                start=True, stop=True),
                inc=(pre + "bc", 1), waits=[(pre + "nm", nmv), (pre + "be", ctx.v(pre + "be"))])
            ctx.emit("vector", lambda e, cc=cc: e.tensor_copy(
                out=rb[:, cc * 512:(cc + 1) * 512], in_=nps[:, 0:512]),
                inc=(pre + "be", 1), waits=[(pre + "bc", ctx.v(pre + "bc"))])
        bev = ctx.v(pre + "be")
        for i in range(nfc_h):
            ctx.emit("vector", lambda e, i=i: e.tensor_mul(ob[:], sb[:, i * 2048:(i + 1) * 2048],
                                                           rb[:]),
                     inc=(pre + "ml", 1), waits=[(pre + "be", bev),
                                                 (pre + "so", so0 + 16 * (nfc_h * h + i))])
            ctx.dma("gpsimd", out_dram[(h * nfc_h + i) * 128:(h * nfc_h + i + 1) * 128, :],
                    ob[:], pre + "so", waits=[(pre + "ml", ctx.v(pre + "ml"))])
    return (pre + "so", ctx.v(pre + "so"))


def emit_fm_to_il(ctx, pre, src_dram, il_dram, parity, ident_sb, F, ready=(), bufpre=None):
    nfc = F // 128
    bp = bufpre or pre
    sb = ctx.sbuf(bp + "_s", [128, 2048], BF16)
    tk = ctx.sbuf(bp + "_t", [128, 2048], BF16)
    tpA = ctx.psum(bp + "_tA", [128, 1024], BF16)
    tpB = ctx.psum(bp + "_tB", [128, 1024], BF16)
    dst3 = il_dram
    ist0 = ctx.v(pre + "st")
    for fc in range(nfc):
        ctx.dma("sync", sb[:], src_dram[fc * 128:(fc + 1) * 128, :], pre + "ld",
                waits=[(pre + "te", ctx.v(pre + "te"))] + list(ready))
        ldv = ctx.v(pre + "ld")
        for g in range(2):
            gidx = ctx.v(pre + "tp") // 8
            bank = [tpA, tpB][gidx % 2]
            gw = [(pre + "ld", ldv)]
            if gidx >= 2:
                gw.append((pre + "te", 8 * (gidx - 1)))
            for i in range(8):
                t8 = g * 8 + i
                ctx.emit("tensor", lambda e, t8=t8, i=i, bank=bank: e.transpose(
                    bank[:, i * 128:(i + 1) * 128], sb[:, t8 * 128:(t8 + 1) * 128], ident_sb[:]),
                    inc=(pre + "tp", 1), waits=(gw if i == 0 else []))
            tpv = ctx.v(pre + "tp")
            for i in range(8):
                t8 = g * 8 + i
                ctx.emit("vector", lambda e, t8=t8, i=i, bank=bank: e.tensor_copy(
                    out=tk[:, t8 * 128:(t8 + 1) * 128], in_=bank[:, i * 128:(i + 1) * 128]),
                    inc=(pre + "te", 1),
                    waits=([(pre + "tp", tpv), (pre + "st", ist0 + 16 * 16 * fc)] if i == 0 else []))
        tev = ctx.v(pre + "te")
        for tt in range(16):
            ctx.dma("gpsimd", dst3[tt * 128:(tt + 1) * 128, fc * 128:(fc + 1) * 128],
                    tk[:, tt * 128:(tt + 1) * 128], pre + "st", waits=[(pre + "te", tev)])
    return (pre + "st", ctx.v(pre + "st"))


def build_l1a(ctx=None, io=None):
    ctx = ctx or Ctx()
    io = io or {}
    x = _io(ctx, io, "in", "xin", [TPC, D])
    xbnd = _io(ctx, io, "in", "xbnd", [128, D])
    wq = _io(ctx, io, "in", "wq", [D, 1536])
    wk0 = _io(ctx, io, "in", "wk0", [D, 1536])
    wk1 = _io(ctx, io, "in", "wk1", [D, 1536])
    wv0 = _io(ctx, io, "in", "wv0", [D, 3072])
    wv1 = _io(ctx, io, "in", "wv1", [D, 3072])
    wbab = _io(ctx, io, "in", "wbab", [D, 18])
    dtb18 = _io(ctx, io, "in", "dtb18", [18])
    wg = _io(ctx, io, "in", "wg", [D, 3072])
    cqw = _io(ctx, io, "in", "cqw", [1536, 4])
    ckw0 = _io(ctx, io, "in", "ckw0", [1536, 4])
    ckw1 = _io(ctx, io, "in", "ckw1", [1536, 4])
    cvw0 = _io(ctx, io, "in", "cvw0", [3072, 4])
    cvw1 = _io(ctx, io, "in", "cvw1", [3072, 4])
    a_log = _io(ctx, io, "in", "a_log", [6])
    qfn = _io(ctx, io, "out", "QFN", [1536, TPC], BF16)
    kfn0 = _io(ctx, io, "out", "KFN0", [1536, TPC], BF16)
    kfn1 = _io(ctx, io, "out", "KFN1", [1536, TPC], BF16)
    v0t = _io(ctx, io, "out", "V0T", [TPC, 3072], BF16)
    v1t = _io(ctx, io, "out", "V1T", [TPC, 3072], BF16)
    b0d = _io(ctx, io, "out", "B0", [TPC, 6])
    b1d = _io(ctx, io, "out", "B1", [TPC, 6])
    gd = _io(ctx, io, "out", "G", [TPC, 6])
    gg = _io(ctx, io, "out", "GG", [TPC, 3072])
    qraw = ctx.dram_tmp("QRAW", [1536, 2176], BF16)
    kraw0 = ctx.dram_tmp("KRAW0", [1536, 2176], BF16)
    kraw1 = ctx.dram_tmp("KRAW1", [1536, 2176], BF16)
    vraw0 = ctx.dram_tmp("VRAW0", [3072, 2176], BF16)
    vraw1 = ctx.dram_tmp("VRAW1", [3072, 2176], BF16)
    qcv = ctx.dram_tmp("QCV", [1536, TPC], BF16)
    kcv0 = ctx.dram_tmp("KCV0", [1536, TPC], BF16)
    kcv1 = ctx.dram_tmp("KCV1", [1536, TPC], BF16)
    vcv0 = ctx.dram_tmp("VCV0", [3072, TPC], BF16)
    vcv1 = ctx.dram_tmp("VCV1", [3072, TPC], BF16)
    ident_sb, ones_sb = setup_common(ctx)
    onesf = ctx.sbuf("onesf", [128, 128], F32)
    onesfd = ctx.const("onesf128", np.ones((128, 128), dtype=np.float32))
    ctx.dma("sync", onesf[:], onesfd, "lof")
    ctx.wait("tensor", [("lof", 16)])
    ctx.wait("vector", [("lof", 16)])
    ctx.wait("scalar", [("lof", 16)])

    xn_fm = ctx.sbuf("xn_fm", [128, ND * 2176], BF16)
    emit_ln_to_fm(ctx, "ln", xbnd, xn_fm, ident_sb, n_tt=1, fm_stride=2176, fm_col0=0)
    fmr = emit_ln_to_fm(ctx, "ln", x, xn_fm, ident_sb, n_tt=NT, fm_stride=2176, fm_col0=128)

    prev = []
    fins = {}
    for pre_, wd, od, FF in [("pq", wq, qraw, 1536), ("pk0", wk0, kraw0, 1536),
                             ("pk1", wk1, kraw1, 1536), ("pv0", wv0, vraw0, 3072),
                             ("pv1", wv1, vraw1, 3072)]:
        fin = emit_proj(ctx, "pp", xn_fm, wd, od, n_tok=2176, fm_stride=2176, mode="fm",
                        F=FF, fm_ready=fmr, wait_extra=prev, psA="pA", psB="pB",
                        wname="pwfm", oname="pofm")
        prev = [("ppev", ctx.v("ppev"))] + fin
        fins[pre_] = list(fin)
    ggfin = emit_proj(ctx, "pt", xn_fm, wg, gg, n_tok=TPC, fm_stride=2176, fm_base=128,
                      mode="tok", F=3072, fm_ready=fmr, wait_extra=prev, psA="pA", psB="pB",
                      out_stage_dtype=F32)

    bab = ctx.sbuf("bab", [128, ND * 18], BF16)
    dtb = ctx.sbuf("dtb", [1, 18], BF16)
    bps = ctx.psum("bps", [128, 512])
    bsg = ctx.sbuf("bsg", [128, 18], F32)
    gtmp = ctx.sbuf("gtmp", [128, 6], F32)
    gtm2 = ctx.sbuf("gtm2", [128, 6], F32)
    onecol = ctx.sbuf("onecol", [128, 1], F32)
    ealog = ctx.sbuf("ealog", [128, 8], F32)
    ctx.dma("gpsimd", bab[:].rearrange("p (dc f) -> p dc f", dc=ND),
            wbab.rearrange("(dc p) f -> p dc f", p=128), "lba")
    ctx.dma("gpsimd", dtb[:], dtb18.rearrange("(o f) -> o f", o=1), "lba")
    ctx.dma("gpsimd", ealog[0:1, 0:6], a_log.rearrange("(o f) -> o f", o=1), "lba")
    ctx.emit("vector", lambda e: e.memset(onecol[:], 1.0), inc=("bgp", 1), waits=[])
    ctx.emit("scalar", lambda e: e.activation(out=ealog[0:1, 0:6], in_=ealog[0:1, 0:6],
                                              func=AF.Exp, bias=0.0, scale=1.0),
             inc=("bgp", 1), waits=[("lba", 48), ("bgp", ctx.v("bgp"))])
    ctx.emit("tensor", lambda e: e.matmul(bps[:, 0:6], onesf[0:1, 0:128], ealog[0:1, 0:6],
                                          start=True, stop=True),
             inc=("bgp", 1), waits=[("bgp", ctx.v("bgp"))])
    ctx.emit("vector", lambda e: e.tensor_scalar_mul(ealog[:, 0:6], bps[:, 0:6], -1.0),
             inc=("bgp", 1), waits=[("bgp", ctx.v("bgp"))])
    bgpv = ctx.v("bgp")
    for tt in range(NT):
        def bfn(e, tt=tt):
            for d in range(ND):
                e.matmul(bps[:, 0:18], xn_fm[:, d * 2176 + 128 + tt * 128:
                                             d * 2176 + 128 + (tt + 1) * 128],
                         bab[:, d * 18:(d + 1) * 18], start=(d == 0), stop=False)
            return e.matmul(bps[:, 0:18], ones_sb[0:1, 0:128], dtb[0:1, :],
                            start=False, stop=True)
        ctx.emit("tensor", bfn, inc=("bmm", 1),
                 waits=[("lba", 48), fmr, ("bdn", 3 * tt), ("bsi", ctx.v("bsi")),
                        ("bga", ctx.v("bga")), ("bgp", bgpv)])
        bmv = ctx.v("bmm")
        ctx.emit("scalar", lambda e: e.activation(out=bsg[:, 0:12], in_=bps[:, 0:12],
                                                  func=AF.Sigmoid, bias=0.0, scale=1.0),
                 inc=("bsi", 1), waits=[("bmm", bmv), ("bst", 16 * 3 * tt)])
        ctx.emit("scalar", lambda e: e.activation(out=gtmp[:], in_=bps[:, 12:18],
                                                  func=AF.Abs, bias=0.0, scale=1.0),
                 inc=("bga", 1), waits=[("bmm", bmv), ("bdn", 3 * tt),
                                        ("bst", 16 * 3 * tt)])
        ctx.emit("scalar", lambda e: e.activation(out=gtmp[:], in_=gtmp[:],
                                                  func=AF.Exp, bias=0.0, scale=-1.0),
                 inc=("bga", 1), waits=[("bga", ctx.v("bga"))])
        ctx.emit("scalar", lambda e: e.activation(out=gtmp[:], in_=gtmp[:],
                                                  func=AF.Ln, bias=onecol[:, 0:1], scale=1.0),
                 inc=("bga", 1), waits=[("bga", ctx.v("bga")), ("bgp", bgpv)])
        ctx.emit("vector", lambda e: e.tensor_scalar_max(gtm2[:], bps[:, 12:18], 0.0),
                 inc=("bdn", 1), waits=[("bmm", bmv), ("bst", 16 * 3 * tt)])
        ctx.emit("vector", lambda e: e.tensor_add(gtmp[:], gtmp[:], gtm2[:]),
                 inc=("bdn", 1), waits=[("bdn", ctx.v("bdn")), ("bga", ctx.v("bga"))])
        ctx.emit("vector", lambda e: e.tensor_mul(gtmp[:], gtmp[:], ealog[:, 0:6]),
                 inc=("bdn", 1), waits=[("bdn", ctx.v("bdn"))])
        bdv = ctx.v("bdn")
        ctx.dma("gpsimd", b0d[tt * 128:(tt + 1) * 128, :], bsg[:, 0:6], "bst",
                waits=[("bsi", ctx.v("bsi"))])
        ctx.dma("gpsimd", b1d[tt * 128:(tt + 1) * 128, :], bsg[:, 6:12], "bst",
                waits=[("bsi", ctx.v("bsi"))])
        ctx.dma("gpsimd", gd[tt * 128:(tt + 1) * 128, :], gtmp[:], "bst",
                waits=[("bdn", bdv)])

    allproj = fins["pq"]
    cq = emit_conv_silu(ctx, "cs", qraw, cqw, qcv, 1536, bufpre="cb", ready=fins["pq"])
    ck0 = emit_conv_silu(ctx, "cs", kraw0, ckw0, kcv0, 1536, bufpre="cb", ready=fins["pk0"])
    ck1 = emit_conv_silu(ctx, "cs", kraw1, ckw1, kcv1, 1536, bufpre="cb", ready=fins["pk1"])
    cv0 = emit_conv_silu(ctx, "cs", vraw0, cvw0, vcv0, 3072, bufpre="cb", ready=fins["pv0"])
    cv1 = emit_conv_silu(ctx, "cs", vraw1, cvw1, vcv1, 3072, bufpre="cb", ready=fins["pv1"])

    nq = emit_l2norm_fm(ctx, "nr", qcv, qfn, onesf, scale=1.0 / 16.0, bufpre="nb", ready=[cq])
    nk0 = emit_l2norm_fm(ctx, "nr", kcv0, kfn0, onesf, bufpre="nb", ready=[ck0])
    nk1 = emit_l2norm_fm(ctx, "nr", kcv1, kfn1, onesf, bufpre="nb", ready=[ck1])

    v0 = emit_fm_to_il(ctx, "il", vcv0, v0t, 0, ident_sb, 3072, bufpre="ib", ready=[cv0])
    v1 = emit_fm_to_il(ctx, "il", vcv1, v1t, 1, ident_sb, 3072, bufpre="ib", ready=[cv1])

    ctx.wait("gpsimd", [nk1, v1, ggfin[0], ggfin[1], ("bst", ctx.v("bst"))])
    return ctx



def build_l1b(dbg=False, ctx=None, io=None):
    ctx = ctx or Ctx()
    io = io or {}
    qfn = _io(ctx, io, "in", "QFN", [1536, TPC], BF16)
    kfn = [_io(ctx, io, "in", "KFN0", [1536, TPC], BF16), _io(ctx, io, "in", "KFN1", [1536, TPC], BF16)]
    vt = [_io(ctx, io, "in", "V0T", [TPC, 3072], BF16), _io(ctx, io, "in", "V1T", [TPC, 3072], BF16)]
    bd = [_io(ctx, io, "in", "B0", [TPC, 6]), _io(ctx, io, "in", "B1", [TPC, 6])]
    gdr = _io(ctx, io, "in", "G", [TPC, 6])
    s0in = _io(ctx, io, "in", "S0", [6, 256, 512])
    od = _io(ctx, io, "out", "OD", [TPC, 3072])
    sf = _io(ctx, io, "out", "SF", [6, 256, 512])
    if dbg:
        dbd = {nm: _io(ctx, io, "out", "DBG_" + nm, shp)
               for nm, shp in [("cgs6", [128, 6]), ("g0", [128, 128]), ("nmat", [128, 128]),
                               ("zmat", [128, 128]), ("rr", [128, 512]), ("gq", [128, 64]),
                               ("cgrow", [1, 768])]}
        dbb = {nm: _io(ctx, io, "out", "DBB_" + nm, shp, BF16)
               for nm, shp in [("ub", [128, 512]), ("afin", [128, 64]), ("kfm", [128, 256]),
                               ("qfm", [128, 128])]}

    def dump(names_bufs):
        for nm, buf in names_bufs:
            tgt = dbd.get(nm)
            if tgt is None:
                tgt = dbb[nm]
            ctx.dma("gpsimd", tgt.rearrange("a b -> a b"), buf, "dbg",
                    waits=[("sq", ctx.v("sq"))])

    H, NCH = 6, 32
    step = np.where(np.arange(128) < 64, 2 * np.arange(128), 2 * (np.arange(128) - 64) + 1)
    slmB = np.where(step[None, :] < step[:, None], 0.0, BIG).astype(np.float32)
    tt_ = np.arange(64)[None, :]
    qmB = np.where(step[:, None] <= 2 * tt_ + 1, 0.0, BIG).astype(np.float32)
    uts_np = np.zeros((128, 128), dtype=np.float32)
    m = np.arange(64)[:, None]
    n = np.arange(128)[None, :]
    uts_np[:64] = (m <= np.where(n < 64, n, n - 64)).astype(np.float32)
    consts = [("uts", uts_np), ("slmB", slmB), ("qmB", qmB),
              ("id2s", 2.0 * np.eye(128, dtype=np.float32)),
              ("idfs", np.eye(128, dtype=np.float32)),
              ("idbs", np.eye(128, dtype=ml_dtypes.bfloat16)),
              ("onesf1", np.ones((1, 128), dtype=np.float32))]
    sbs = {}
    for nm, arr in consts:
        dr = ctx.const("l1b_" + nm, np.ascontiguousarray(arr))
        sbs[nm] = ctx.sbuf(nm, list(arr.shape), BF16 if arr.dtype == ml_dtypes.bfloat16 else F32)
        ctx.dma("sync", sbs[nm][:], dr, "cl")
    uts, slmBs, qmBs = sbs["uts"], sbs["slmB"], sbs["qmB"]
    id2s, idfs, idbs, onesf = sbs["id2s"], sbs["idfs"], sbs["idbs"], sbs["onesf1"]
    clv = ctx.v("cl")

    sst = ctx.sbuf("sst", [128, H * 2 * 512], F32)
    sstb = ctx.sbuf("sstb", [128, H * 2 * 512], BF16)
    for h in range(H):
        for dk in range(2):
            ctx.dma("sync", sst[:, (h * 2 + dk) * 512:(h * 2 + dk + 1) * 512],
                    s0in[h, dk * 128:(dk + 1) * 128, :], "cls")
    clv2 = ctx.v("cls")
    for h in range(H):
        for dk in range(2):
            ctx.emit("vector", lambda e, h=h, dk=dk: e.tensor_copy(
                out=sstb[:, (h * 2 + dk) * 512:(h * 2 + dk + 1) * 512],
                in_=sst[:, (h * 2 + dk) * 512:(h * 2 + dk + 1) * 512]),
                inc=("sq", 1), waits=[("cls", clv2), ("sq", ctx.v("sq"))])

    gc = ctx.sbuf("gc", [128, 6], F32)
    bcol = ctx.sbuf("bcol", [128, 1], F32)
    cgs6 = ctx.sbuf("cgs6", [128, 6], F32)
    cgrow = ctx.sbuf("cgrow", [1, H * 128], F32)
    kfm = ctx.sbuf("kfm", [128, 256], BF16)
    qfm = ctx.sbuf("qfm", [128, 128], BF16)
    vc = ctx.sbuf("vc", [128, 512], BF16)
    g0 = ctx.sbuf("g0", [128, 128], F32)
    nmat = ctx.sbuf("nmat", [128, 128], F32)
    zmat = ctx.sbuf("zmat", [128, 128], F32)
    zt = ctx.sbuf("zt", [128, 128], F32)
    qq = ctx.sbuf("qq", [128, 128], F32)
    rr = ctx.sbuf("rr", [128, 512], F32)
    tks = ctx.sbuf("tks", [128, 512], F32)
    ub = ctx.sbuf("ub", [128, 512], BF16)
    gq = ctx.sbuf("gq", [128, 64], F32)
    afin = ctx.sbuf("afin", [128, 64], BF16)
    gob = ctx.sbuf("gob", [128, 64], F32)
    grow = ctx.sbuf("grow", [1, 64], F32)
    qtl = ctx.sbuf("qtl", [128, 128], BF16)
    ktok = ctx.sbuf("ktok", [128, 256], BF16)
    obuf = ctx.sbuf("obuf", [128, 512], F32)
    scl = ctx.sbuf("scl", [128, 16], F32)
    pA = ctx.psum("pA", [128, 512])
    pC = ctx.psum("pC", [128, 512])
    pT = ctx.psum("pT", [128, 128])
    pTb = ctx.psum("pTb", [128, 1024], BF16)

    ctx.emit("vector", lambda e: e.memset(gc[:], 0.0), inc=("sq", 1),
             waits=[("sq", ctx.v("sq"))])

    def seq(lane, fn):
        ctx.emit(lane, fn, inc=("sq", 1), waits=[("sq", ctx.v("sq")), ("cl", clv)])

    for c in range(NCH):
        sq_c = ctx.v("sq")
        ctx.dma("sync", gc[0:64, :], gdr[c * 64:(c + 1) * 64, :], "gld",
                waits=[("sq", sq_c)])
        gldv = ctx.v("gld")
        ctx.wait("tensor", [("gld", gldv)])
        seq("tensor", lambda e: e.matmul(pT[:, 0:6], uts[:], gc[:], start=True, stop=True))
        seq("vector", lambda e: e.tensor_copy(out=cgs6[:], in_=pT[:, 0:6]))
        for h in range(H):
            seq("tensor", lambda e, h=h: e.transpose(pT[0:1, 0:128], cgs6[:, h:h + 1], idfs[:]))
            seq("vector", lambda e, h=h: e.tensor_copy(
                out=cgrow[0:1, h * 128:(h + 1) * 128], in_=pT[0:1, 0:128]))

        for h in range(H):
            u = c * H + h
            sq_u = ctx.v("sq")
            for dk in range(2):
                ctx.dma("gpsimd", kfm[:, dk * 128:dk * 128 + 64],
                        kfn[0][h * 256 + dk * 128:h * 256 + (dk + 1) * 128,
                               c * 64:(c + 1) * 64], "kld",
                        waits=([("sq", sq_u), ("ost", 16 * u)] if dk == 0 else []))
                ctx.dma("gpsimd", kfm[:, dk * 128 + 64:dk * 128 + 128],
                        kfn[1][h * 256 + dk * 128:h * 256 + (dk + 1) * 128,
                               c * 64:(c + 1) * 64], "kld")
                ctx.dma("gpsimd", qfm[:, dk * 64:(dk + 1) * 64],
                        qfn[h * 256 + dk * 128:h * 256 + (dk + 1) * 128,
                            c * 64:(c + 1) * 64], "kld")
            ctx.dma("gpsimd", vc[0:64, :], vt[0][c * 64:(c + 1) * 64, h * 512:(h + 1) * 512],
                    "kld")
            ctx.dma("gpsimd", vc[64:128, :], vt[1][c * 64:(c + 1) * 64, h * 512:(h + 1) * 512],
                    "kld")
            ctx.dma("gpsimd", bcol[0:64, :], bd[0][c * 64:(c + 1) * 64, h:h + 1], "kld",
                    allow_slow_non_contiguous=True)
            ctx.dma("gpsimd", bcol[64:128, :], bd[1][c * 64:(c + 1) * 64, h:h + 1], "kld",
                    allow_slow_non_contiguous=True)
            kldv = ctx.v("kld")
            for lane in ("tensor", "vector", "scalar"):
                ctx.wait(lane, [("kld", kldv)])

            cgc = cgs6[:, h:h + 1]
            cgr = cgrow[0:1, h * 128:(h + 1) * 128]
            cgodd = cgr[0:1, 64:128]
            sA = sst[:, h * 1024:h * 1024 + 512]
            sB = sst[:, h * 1024 + 512:h * 1024 + 1024]
            sbA = sstb[:, h * 1024:h * 1024 + 512]
            sbB = sstb[:, h * 1024 + 512:h * 1024 + 1024]

            seq("tensor", lambda e, cgr=cgr: (
                e.matmul(pA[:, 0:128], kfm[:, 0:128], kfm[:, 0:128], start=True, stop=False),
                e.matmul(pA[:, 0:128], kfm[:, 128:256], kfm[:, 128:256], start=False, stop=True),
                e.matmul(pT[:], onesf[:], cgr, start=True, stop=True))[-1])
            seq("vector", lambda e: e.tensor_add(pT[:], pT[:], slmBs[:]))
            seq("scalar", lambda e, cgc=cgc: e.activation(out=g0[:], in_=pT[:], func=AF.Exp,
                                                 bias=cgc, scale=-1.0))
            seq("vector", lambda e: e.scalar_tensor_tensor(
                out=nmat[:], in0=pA[:, 0:128], scalar=bcol[:], in1=g0[:],
                op0=ALU.mult, op1=ALU.mult))
            seq("tensor", lambda e: e.transpose(pT[:], nmat[:], idfs[:]))
            seq("vector", lambda e: e.scalar_tensor_tensor(
                out=zmat[:], in0=pT[:], scalar=-1.0, in1=idfs[:], op0=ALU.mult, op1=ALU.add))
            for it in range(6):
                seq("tensor", lambda e: (
                    e.matmul(pA[:, 0:128], nmat[:], zmat[:], start=True, stop=False),
                    e.matmul(pA[:, 0:128], idfs[:], zmat[:], start=False, stop=True))[-1])
                seq("vector", lambda e: e.scalar_tensor_tensor(
                    out=qq[:], in0=pA[:, 0:128], scalar=-1.0, in1=id2s[:],
                    op0=ALU.mult, op1=ALU.add))
                seq("tensor", lambda e: e.transpose(pT[:], zmat[:], idfs[:]))
                seq("vector", lambda e: e.tensor_copy(out=zt[:], in_=pT[:]))
                seq("tensor", lambda e: e.matmul(pA[:, 0:128], zt[:], qq[:],
                                                 start=True, stop=True))
                seq("vector", lambda e: e.tensor_copy(out=zmat[:], in_=pA[:, 0:128]))
            seq("tensor", lambda e, sbA=sbA, sbB=sbB: (
                e.matmul(pC[:, 0:512], kfm[:, 0:128], sbA, start=True, stop=False),
                e.matmul(pC[:, 0:512], kfm[:, 128:256], sbB, start=False, stop=True))[-1])
            seq("scalar", lambda e, cgc=cgc: e.activation(out=scl[:, 0:1], in_=cgc, func=AF.Exp,
                                                 bias=0.0, scale=1.0))
            seq("vector", lambda e: e.tensor_mul(scl[:, 1:2], scl[:, 0:1], bcol[:]))
            seq("vector", lambda e: e.tensor_scalar_mul(tks[:], pC[:, 0:512], scl[:, 1:2]))
            seq("vector", lambda e: e.scalar_tensor_tensor(
                out=rr[:], in0=vc[:], scalar=bcol[:], in1=tks[:],
                op0=ALU.mult, op1=ALU.subtract))
            seq("tensor", lambda e: e.matmul(pC[:, 0:512], zmat[:], rr[:],
                                             start=True, stop=True))
            seq("vector", lambda e: e.tensor_copy(out=ub[:], in_=pC[:, 0:512]))
            seq("tensor", lambda e, cgodd=cgodd: (
                e.matmul(pA[:, 0:64], kfm[:, 0:128], qfm[:, 0:64], start=True, stop=False),
                e.matmul(pA[:, 0:64], kfm[:, 128:256], qfm[:, 64:128], start=False, stop=True),
                e.matmul(pT[:, 0:64], onesf[:], cgodd, start=True, stop=True))[-1])
            seq("vector", lambda e: e.tensor_sub(pT[:, 0:64], pT[:, 0:64], qmBs[:]))
            seq("vector", lambda e, cgc=cgc: e.tensor_scalar_mul(scl[:, 2:3], cgc, -1.0))
            seq("scalar", lambda e: e.activation(out=gq[:], in_=pT[:, 0:64], func=AF.Exp,
                                                 bias=scl[:, 2:3], scale=1.0))
            seq("vector", lambda e: e.tensor_mul(afin[:], pA[:, 0:64], gq[:]))
            seq("scalar", lambda e, cgodd=cgodd: e.activation(out=grow[:], in_=cgodd,
                                                 func=AF.Exp, bias=0.0, scale=1.0))
            seq("tensor", lambda e: e.matmul(pT[:, 0:64], onesf[:], grow[:],
                                             start=True, stop=True))
            seq("vector", lambda e: e.tensor_copy(out=gob[:], in_=pT[:, 0:64]))
            seq("vector", lambda e: e.tensor_mul(qtl[:, 0:64], qfm[:, 0:64], gob[:]))
            seq("vector", lambda e: e.tensor_mul(qtl[:, 64:128], qfm[:, 64:128], gob[:]))
            seq("tensor", lambda e, sbA=sbA, sbB=sbB: (
                e.matmul(pC[0:64, 0:512], qtl[:, 0:64], sbA, start=True, stop=False),
                e.matmul(pC[0:64, 0:512], qtl[:, 64:128], sbB, start=False, stop=False),
                e.matmul(pC[0:64, 0:512], afin[:], ub[:], start=False, stop=True))[-1])
            seq("vector", lambda e: e.tensor_copy(out=obuf[0:64, :], in_=pC[0:64, 0:512]))
            ctx.dma("gpsimd", od[c * 64:(c + 1) * 64, h * 512:(h + 1) * 512],
                    obuf[0:64, :], "ost", waits=[("sq", ctx.v("sq"))])
            seq("tensor", lambda e: (
                e.transpose(pTb[:, 0:128], kfm[:, 0:128], idbs[:]),
                e.transpose(pTb[:, 128:256], kfm[:, 128:256], idbs[:]))[-1])
            seq("vector", lambda e: e.tensor_copy(out=ktok[:], in_=pTb[:, 0:256]))
            seq("tensor", lambda e, cgr=cgr: e.matmul(pT[:, 0:1], onesf[:], cgr[0:1, 127:128],
                                             start=True, stop=True))
            seq("vector", lambda e: e.tensor_copy(out=scl[:, 3:4], in_=pT[:, 0:1]))
            seq("vector", lambda e, cgc=cgc: e.tensor_sub(scl[:, 4:5], scl[:, 3:4], cgc))
            seq("scalar", lambda e: e.activation(out=scl[:, 5:6], in_=scl[:, 4:5],
                                                 func=AF.Exp, bias=0.0, scale=1.0))
            seq("scalar", lambda e: e.activation(out=scl[:, 6:7], in_=scl[:, 3:4],
                                                 func=AF.Exp, bias=0.0, scale=1.0))
            seq("vector", lambda e: e.tensor_scalar_mul(ktok[:], ktok[:], scl[:, 5:6]))
            if dbg and c == 0 and h == 0:
                dump([("cgs6", cgs6[:]), ("g0", g0[:]), ("nmat", nmat[:]),
                      ("zmat", zmat[:]), ("rr", rr[:]), ("gq", gq[:]),
                      ("cgrow", cgrow[0:1, :]), ("ub", ub[:]), ("afin", afin[:]),
                      ("kfm", kfm[:]), ("qfm", qfm[:])])
                ctx.wait("gpsimd", [("dbg", ctx.v("dbg"))])
            for dk in range(2):
                seq("tensor", lambda e, dk=dk: e.matmul(
                    pC[:, 0:512], ktok[:, dk * 128:(dk + 1) * 128], ub[:],
                    start=True, stop=True))
                seq("vector", lambda e, dk=dk, sX=(sA, sB)[dk]: e.scalar_tensor_tensor(
                    out=sX, in0=sX, scalar=scl[:, 6:7], in1=pC[:, 0:512],
                    op0=ALU.mult, op1=ALU.add))
                seq("vector", lambda e, dk=dk, sX=(sA, sB)[dk], sbX=(sbA, sbB)[dk]:
                    e.tensor_copy(out=sbX, in_=sX))
    sqf = ctx.v("sq")
    for h in range(H):
        for dk in range(2):
            ctx.dma("gpsimd", sf[h, dk * 128:(dk + 1) * 128, :],
                    sst[:, (h * 2 + dk) * 512:(h * 2 + dk + 1) * 512], "ost",
                    waits=[("sq", sqf)])
    ctx.wait("gpsimd", [("ost", ctx.v("ost"))])
    return ctx


def build_l1c(ctx=None, io=None):
    ctx = ctx or Ctx()
    io = io or {}
    odd = _io(ctx, io, "in", "OD", [TPC, 3072])
    ggd = _io(ctx, io, "in", "GG", [TPC, 3072])
    onw = _io(ctx, io, "in", "onw", [512])
    wo = _io(ctx, io, "in", "wo", [3072, D])
    x = _io(ctx, io, "in", "xin", [TPC, D])
    x3 = _io(ctx, io, "out", "X3", [TPC, D])
    o2d = ctx.dram_tmp("O2", [TPC, 3072])
    ao = ctx.dram_tmp("AO", [TPC, D])
    ident_sb, ones_sb = setup_common(ctx)
    onesf = ctx.sbuf("onesf", [1, 128], F32)
    onesfd = ctx.const("onesf_l1c", np.ones((1, 128), dtype=np.float32))
    ctx.dma("sync", onesf[:], onesfd, "lof")

    nwb = ctx.sbuf("nwb", [128, 512], F32)
    nwsb = ctx.sbuf("nwsb", [1, 512], F32)
    nps = ctx.psum("nps", [128, 512])
    ctx.dma("sync", nwsb[:], onw.rearrange("(o f) -> o f", o=1), "lof")
    ctx.emit("tensor", lambda e: e.matmul(nps[:], onesf[:], nwsb[:], start=True, stop=True),
             inc=("nwm", 1), waits=[("lof", 32)])
    ctx.emit("vector", lambda e: e.tensor_copy(out=nwb[:], in_=nps[:]),
             inc=("nwe", 1), waits=[("nwm", 1)])

    ot = ctx.sbuf("ot", [128, 3072], F32)
    gt = ctx.sbuf("gt", [128, 3072], F32)
    sg = ctx.sbuf("sg", [128, 3072], F32)
    st6 = ctx.sbuf("st6", [128, 16], F32)
    for tt in range(NT):
        ctx.dma("sync", ot[:], odd[tt * 128:(tt + 1) * 128, :], "old",
                waits=[("ost", 16 * tt)])
        ctx.dma("sync", gt[:], ggd[tt * 128:(tt + 1) * 128, :], "old")
        olv = ctx.v("old")

        def step(lane, fn):
            ctx.emit(lane, fn, inc=("oc", 1), waits=[("oc", ctx.v("oc")), ("old", olv),
                                                     ("nwe", 1)])
        step("scalar", lambda e: e.activation(out=sg[:], in_=ot[:], func=AF.Square,
                                              bias=0.0, scale=1.0))
        step("vector", lambda e: e.tensor_reduce(
            out=st6[:, 0:6], in_=sg[:].rearrange("p (h v) -> p h v", h=6),
            axis=AX.X, op=ALU.add))
        step("vector", lambda e: e.tensor_scalar(st6[:, 0:6], st6[:, 0:6], 1.0 / 512.0, 1e-5,
                                                 op0=ALU.mult, op1=ALU.add))
        step("scalar", lambda e: e.activation(out=st6[:, 0:6], in_=st6[:, 0:6], func=AF.Sqrt,
                                              bias=0.0, scale=1.0))
        step("vector", lambda e: e.reciprocal(out=st6[:, 0:6], in_=st6[:, 0:6]))
        for h in range(6):
            step("vector", lambda e, h=h: e.tensor_scalar_mul(
                ot[:, h * 512:(h + 1) * 512], ot[:, h * 512:(h + 1) * 512],
                st6[:, h:h + 1]))
            step("vector", lambda e, h=h: e.tensor_mul(
                ot[:, h * 512:(h + 1) * 512], ot[:, h * 512:(h + 1) * 512], nwb[:]))
        step("scalar", lambda e: e.activation(out=sg[:], in_=gt[:], func=AF.Sigmoid,
                                              bias=0.0, scale=1.0))
        step("vector", lambda e: e.tensor_mul(ot[:], ot[:], gt[:]))
        step("vector", lambda e: e.tensor_mul(ot[:], ot[:], sg[:]))
        ctx.dma("gpsimd", o2d[tt * 128:(tt + 1) * 128, :], ot[:], "ost",
                waits=[("oc", ctx.v("oc"))])
    ostv = ctx.v("ost")

    CH = 512
    cfm = ctx.sbuf("cfm", [128, 24 * CH], BF16)
    pfin = []
    for c in range(TPC // CH):
        ctr = emit_cast_transpose(ctx, "ct", o2d, cfm, ident_sb, n_tt=CH // 128, nd=24,
                                  src_row0=c * CH, fm_stride=CH,
                                  wait_extra=[("ost", ostv), ("pmm", ctx.v("pmm"))])
        pfin = emit_proj(ctx, "p", cfm, wo, ao, n_tok=CH, nd=24, F=D,
                         fm_ready=ctr, psA="nps", psB="pB2",
                         out_stage_dtype=F32, out_row0=c * CH)
    lfin = emit_ln_add(ctx, "le", ao, x, x3, wait_extra=pfin)
    ctx.wait("gpsimd", [lfin])
    return ctx



def l2_rope_tables(pos):
    half = 64
    inv = 1.0 / (10000.0 ** (np.arange(half, dtype=np.float64) * 2.0 / 128.0))
    ang = pos[None, :].astype(np.float64) * inv[:, None]
    cos = np.cos(ang).astype(np.float32)
    sin = np.sin(ang).astype(np.float32)
    return np.concatenate([cos, cos], 0), np.concatenate([sin, sin], 0)


def l2_klocal_pos(s):
    p = np.zeros(2688, np.int64)
    p[0:4] = np.arange(4)
    p[128:640] = (s * 2048 - 512) + np.arange(512) if s == 1 else np.arange(512)
    p[640:] = s * 2048 + np.arange(2048)
    return p


def l2_masks(s):
    out = np.zeros((16, 128, 648), np.float32)
    ql = np.arange(128)[:, None]
    for j in range(16):
        m = out[j]
        kl = np.arange(128)[None, :]
        m[:, 0:128] = np.where(kl >= ql, 0.0, -BIG)
        m[:, 512:640] = np.where(kl <= ql, 0.0, -BIG)
        w0 = 640 + (j - 4) * 128
        for sb in range(5):
            r0 = w0 + sb * 128
            if s == 0:
                valid = r0 >= 640
            else:
                valid = r0 >= 128
            if not valid:
                m[:, sb * 128:(sb + 1) * 128] = -BIG
        m[:, 644:648] = -BIG
        if s == 0:
            if j <= 3:
                m[:, 640:644] = -BIG
            elif j == 4:
                ks = np.arange(4)[None, :]
                m[:, 640:644] = np.where(ks < ql, 0.0, -BIG)
    return out.astype(ml_dtypes.bfloat16)


def build_l1d(ctx=None, io=None):
    ctx = ctx or Ctx()
    io = io or {}
    x = _io(ctx, io, "in", "xin", [TPC, D])
    wt = _io(ctx, io, "in", "wt", [D, D])
    sw = _io(ctx, io, "in", "sw", [D])
    bias = _io(ctx, io, "in", "bias", [D])
    xo = _io(ctx, io, "out", "XO", [TPC, D])
    wsc = ctx.dram_tmp("WSC", [D, D], BF16)
    ao = ctx.dram_tmp("AO", [TPC, D])
    ident_sb, ones_sb = setup_common(ctx)
    onesfd = ctx.const("onesf_l1d", np.ones((1, 128), dtype=np.float32))
    onesf = ctx.sbuf("onesf", [1, 128], F32)
    swr = ctx.sbuf("swr", [1, D], F32)
    swb = ctx.sbuf("swb", [128, D], F32)
    sps = ctx.psum("pS", [128, 512])
    ctx.dma("sync", onesf[:], onesfd, "lsc")
    ctx.dma("sync", swr[:], sw.rearrange("(o f) -> o f", o=1), "lsc")
    for i in range(4):
        ctx.emit("tensor", lambda e, i=i: e.matmul(sps[:, 0:512], onesf[:],
                                                   swr[0:1, i * 512:(i + 1) * 512],
                                                   start=True, stop=True),
                 inc=("scm", 1), waits=[("lsc", 32), ("sce", ctx.v("sce"))])
        ctx.emit("vector", lambda e, i=i: e.tensor_copy(out=swb[:, i * 512:(i + 1) * 512],
                                                        in_=sps[:, 0:512]),
                 inc=("sce", 1), waits=[("scm", ctx.v("scm"))])
    scev = ctx.v("sce")
    wf = ctx.sbuf("wf", [128, D], F32)
    wb = ctx.sbuf("wb", [128, D], BF16)
    for i in range(NT):
        ctx.dma("sync", wf[:], wt[i * 128:(i + 1) * 128, :], "wld",
                waits=[("wmu", ctx.v("wmu"))])
        ctx.emit("vector", lambda e: e.tensor_mul(wb[:], wf[:], swb[:]),
                 inc=("wmu", 1), waits=[("wld", ctx.v("wld")), ("sce", scev),
                                        ("wst", 16 * i)])
        ctx.dma("gpsimd", wsc[i * 128:(i + 1) * 128, :], wb[:], "wst",
                waits=[("wmu", ctx.v("wmu"))])
    wstv = ctx.v("wst")
    cfm = ctx.sbuf("cfm", [128, ND * 512], BF16)
    pfin = []
    for c in range(4):
        fmr = emit_ln_to_fm(ctx, "ln", x, cfm, ident_sb, n_tt=4, x_row0=c * 512,
                            fm_stride=512, wait_extra=[("pmm", ctx.v("pmm"))])
        pfin = emit_proj(ctx, "p", cfm, wsc, ao, n_tok=512, F=D, bias_dram=bias,
                         ones_sb=ones_sb, fm_ready=fmr, psA="pA", psB="pB",
                         out_stage_dtype=F32, out_row0=c * 512,
                         wait_extra=[("wst", wstv)])
    lfin = emit_ln_add(ctx, "le", ao, x, xo, wait_extra=pfin)
    ctx.wait("gpsimd", [lfin])
    return ctx


def build_l2a(ctx=None, io=None):
    ctx = ctx or Ctx()
    io = io or {}
    x = _io(ctx, io, "in", "xin", [TPC, D])
    xsib = _io(ctx, io, "in", "xsib", [640, D])
    wq = _io(ctx, io, "in", "wq", [D, D])
    bq = _io(ctx, io, "in", "bq", [D])
    wk = _io(ctx, io, "in", "wk", [D, 512])
    bk = _io(ctx, io, "in", "bk", [512])
    wv = _io(ctx, io, "in", "wv", [D, 512])
    bv = _io(ctx, io, "in", "bv", [512])
    wo = _io(ctx, io, "in", "wo", [D, D])
    bo = _io(ctx, io, "in", "bo", [D])
    msk = _io(ctx, io, "in", "l2masks", [16, 128, 648], BF16)
    qcosd = _io(ctx, io, "in", "qcos", [128, TPC])
    qsind = _io(ctx, io, "in", "qsin", [128, TPC])
    kcosd = _io(ctx, io, "in", "kcos", [128, 2688])
    ksind = _io(ctx, io, "in", "ksin", [128, 2688])
    x4 = _io(ctx, io, "out", "X4", [TPC, D])
    xnfm_d = ctx.dram_tmp("XNFM", [D, 2688], BF16)
    ktd = ctx.dram_tmp("KT", [512, 2688], BF16)
    qtd = ctx.dram_tmp("QT", [D, TPC], BF16)
    vtd = ctx.dram_tmp("VT", [2688, 512], BF16)
    atd = ctx.dram_tmp("AT", [TPC, D], BF16)
    ao = ctx.dram_tmp("AO", [TPC, D])
    ident_sb, ones_sb = setup_common(ctx)
    pwA = ctx.sbuf("pwA", [128, 2 * ND * 512], BF16)

    xn_fm_t = ctx.sbuf("xn_fm_t", [128, ND * 128], BF16)
    for tt in range(5):
        f1 = emit_ln_to_fm(ctx, "ln", xsib, xn_fm_t, ident_sb, n_tt=1, x_row0=tt * 128,
                           fm_stride=128, wait_extra=[("fmst", 16 * tt)])
        ctx.dma("sync", xnfm_d[:, tt * 128:(tt + 1) * 128].rearrange("(dc p) f -> p dc f", p=128),
                xn_fm_t[:].rearrange("p (dc f) -> p dc f", dc=ND), "fmst", waits=[f1])
    for tt in range(NT):
        f1 = emit_ln_to_fm(ctx, "ln", x, xn_fm_t, ident_sb, n_tt=1, x_row0=tt * 128,
                           fm_stride=128, wait_extra=[("fmst", 16 * (5 + tt))])
        ctx.dma("sync", xnfm_d[:, 640 + tt * 128:640 + (tt + 1) * 128]
                .rearrange("(dc p) f -> p dc f", p=128),
                xn_fm_t[:].rearrange("p (dc f) -> p dc f", dc=ND), "fmst", waits=[f1])
    fmstv = ctx.v("fmst")

    xnc = ctx.sbuf("xnc", [128, 2 * ND * 512], BF16)
    chunks = [(0, 512), (512, 512), (1024, 512), (1536, 512), (2048, 512), (2560, 128)]
    prevs = []
    for ci, (c0, cw) in enumerate(chunks):
        par = ci % 2
        ctx.dma("sync", xnc[:, par * ND * 512:(par + 1) * ND * 512]
                .rearrange("p (dc f) -> p dc f", f=512)[:, :, 0:cw],
                xnfm_d[:, c0:c0 + cw].rearrange("(dc p) f -> p dc f", p=128),
                "xcl%d" % par,
                waits=[("fmst", fmstv)] + prevs)
        xcw = [("xcl%d" % par, 16 * (ci // 2 + 1))]
        fmr = (("xcl%d" % par, 16 * (ci // 2 + 1)))
        fb = par * ND * 512
        kfin = emit_proj(ctx, "pk", xnc, wk, ktd[:, c0:c0 + cw], n_tok=cw, fm_stride=512,
                         fm_base=fb, mode="fm", F=512, fm_ready=fmr, psA="pA", psB="pB",
                         wname="pwA", oname="poA", bias_fm_dram=bk,
                         wait_extra=prevs + [("pvev", ctx.v("pvev")), ("pkev", ctx.v("pkev"))])
        vfin = emit_proj(ctx, "pv", xnc, wv, vtd, n_tok=cw, fm_stride=512, fm_base=fb,
                         mode="tok", F=512, fm_ready=fmr, psA="pA", psB="pB",
                         wname="pwA", oname="poA", bias_dram=bv, ones_sb=ones_sb,
                         out_row0=c0, wait_extra=kfin + [("pkev", ctx.v("pkev"))])
        prevs = kfin + vfin
        if c0 >= 640 or True:
            pass
    prevq = prevs
    for ci in range(4):
        par = ci % 2
        c0 = 640 + ci * 512
        ctx.dma("sync", xnc[:, par * ND * 512:(par + 1) * ND * 512]
                .rearrange("p (dc f) -> p dc f", dc=ND),
                xnfm_d[:, c0:c0 + 512].rearrange("(dc p) f -> p dc f", p=128),
                "xq%d" % par, waits=[("fmst", fmstv)] + prevq)
        fmr = ("xq%d" % par, 16 * (ci // 2 + 1))
        qfin = emit_proj(ctx, "pq", xnc, wq, qtd[:, ci * 512:(ci + 1) * 512], n_tok=512,
                         fm_stride=512, fm_base=par * ND * 512, mode="fm", F=D,
                         fm_ready=fmr, psA="pA", psB="pB", wname="pwA", oname="poA",
                         bias_fm_dram=bq,
                         wait_extra=prevq + [("pqev", ctx.v("pqev")), ("pvev", ctx.v("pvev"))])
        prevq = qfin
    projfin = prevq + prevs

    kcs = ctx.sbuf("kcs", [128, 2688], F32)
    ksn = ctx.sbuf("ksn", [128, 2688], F32)
    rbuf = ctx.sbuf("rbuf", [128, 2688], BF16)
    rtmp = ctx.sbuf("rtmp", [128, 2688], F32)
    rtm2 = ctx.sbuf("rtm2", [128, 2688], F32)
    rtm2s = ctx.sbuf("ktr", [128, 4 * 2688], BF16)[:].bitcast(F32)[:, 0:2688]
    ctx.dma("sync", kcs[:], kcosd, "rld", waits=projfin)
    ctx.dma("sync", ksn[:], ksind, "rld")

    def rope_pass(pre, src, T, nheads, sem):
        rldv = ctx.v(sem)
        for hh in range(nheads):
            ctx.dma("sync", rbuf[:, 0:T], src[hh * 128:(hh + 1) * 128, 0:T], pre + "l",
                    waits=[(sem, rldv), (pre + "w", ctx.v(pre + "w"))])
            lv = ctx.v(pre + "l")

            def rstep(fn, extra=()):
                ctx.emit("vector", fn, inc=(pre + "c", 1),
                         waits=[(pre + "l", lv), (pre + "c", ctx.v(pre + "c"))] + list(extra))
            rstep(lambda e: e.tensor_mul(rtmp[:, 0:T], rbuf[:, 0:T], kcs[:, 0:T]))
            rstep(lambda e: e.tensor_mul(rtm2[:, 0:T], rbuf[:, 0:T], ksn[:, 0:T]),
                  extra=[(pre + "x", ctx.v(pre + "x"))])
            cv = ctx.v(pre + "c")
            ctx.dma("gpsimd", rtm2s[0:64, 0:T], rtm2[64:128, 0:T], pre + "x",
                    waits=[(pre + "c", cv)])
            ctx.dma("gpsimd", rtm2s[64:128, 0:T], rtm2[0:64, 0:T], pre + "x",
                    waits=[(pre + "c", cv)])
            xv = ctx.v(pre + "x")
            rstep(lambda e: e.tensor_sub(rtmp[0:64, 0:T], rtmp[0:64, 0:T],
                                         rtm2s[0:64, 0:T]), extra=[(pre + "x", xv)])
            rstep(lambda e: e.tensor_add(rtmp[64:128, 0:T], rtmp[64:128, 0:T],
                                         rtm2s[64:128, 0:T]))
            rstep(lambda e: e.tensor_copy(out=rbuf[:, 0:T], in_=rtmp[:, 0:T]))
            ctx.dma("gpsimd", src[hh * 128:(hh + 1) * 128, 0:T], rbuf[:, 0:T], pre + "w",
                    waits=[(pre + "c", ctx.v(pre + "c"))])
        return (pre + "w", ctx.v(pre + "w"))

    rk = rope_pass("rk", ktd, 2688, 4, "rld")
    ctx.dma("sync", kcs[:, 0:2048], qcosd, "rlq", waits=[rk])
    ctx.dma("sync", ksn[:, 0:2048], qsind, "rlq")
    rq = rope_pass("rq", qtd, 2048, 16, "rlq")

    ktr = ctx.sbuf("ktr", [128, 4 * 2688], BF16)
    vtr = ctx.sbuf("vtr", [128, 21 * 512], BF16)
    mskb = ctx.sbuf("mskb", [128, 16 * 648], BF16)
    qh = ctx._sbufs["rtmp"][:].bitcast(BF16)
    _kcsb = None
    psb = ctx._sbufs["kcs"][:].bitcast(BF16)[:, 0:768]
    ptb = ctx._sbufs["kcs"][:].bitcast(BF16)[:, 1024:1792]
    osb = ctx._sbufs["rtm2"][:].bitcast(BF16)[:, 0:2048]
    lst = ctx.sbuf("lst", [128, 8], F32)
    pS = ctx.psum("pS", [128, 512])
    pS2 = ctx.psum("pS2", [128, 136])
    pO = ctx.psum("pO", [128, 128])
    pTt = ctx.psum("pTt", [128, 1024], BF16)
    for g4 in range(4):
        ctx.dma("sync", ktr[:, g4 * 2688:(g4 + 1) * 2688],
                ktd[g4 * 128:(g4 + 1) * 128, :], "akl", waits=[rq])
    for vt_ in range(21):
        ctx.dma("sync", vtr[:, vt_ * 512:(vt_ + 1) * 512],
                vtd[vt_ * 128:(vt_ + 1) * 128, :], "akl")
    for j in range(16):
        ctx.dma("sync", mskb[:, j * 648:(j + 1) * 648], msk[j], "akl")
    aklv = ctx.v("akl")
    for lane in ("tensor", "vector", "scalar"):
        ctx.wait(lane, [("akl", aklv)])

    for j in range(16):
        for h in range(16):
            u = j * 16 + h
            up = u % 2
            g4 = h // 4
            ctx.dma("gpsimd", qh[:, up * 2048:up * 2048 + 128],
                    qtd[h * 128:(h + 1) * 128, j * 128:(j + 1) * 128], "aql%d" % up,
                    waits=[("amm", max(0, ctx.v("amm") - 1))])
            aqv = ("aql%d" % up, ctx.v("aql%d" % up))
            w0 = 640 + (j - 4) * 128

            def smm(e, g4=g4, up=up, w0=w0):
                e.matmul(pS[:], qh[:, up * 2048:up * 2048 + 128],
                         ktr[:, g4 * 2688 + w0:g4 * 2688 + w0 + 512],
                         start=True, stop=True)
                e.matmul(pS2[:, 0:128], qh[:, up * 2048:up * 2048 + 128],
                         ktr[:, g4 * 2688 + w0 + 512:g4 * 2688 + w0 + 640],
                         start=True, stop=False)
                return e.matmul(pS2[:, 128:136] if False else pS2[:, 0:136],
                                qh[:, up * 2048:up * 2048 + 128],
                                ktr[:, g4 * 2688 + 0:g4 * 2688 + 8],
                                start=False, stop=True)
            def smm2(e, g4=g4, up=up, w0=w0):
                e.matmul(pS[:], qh[:, up * 2048:up * 2048 + 128],
                         ktr[:, g4 * 2688 + w0:g4 * 2688 + w0 + 512],
                         start=True, stop=True)
                e.matmul(pS2[:, 0:128], qh[:, up * 2048:up * 2048 + 128],
                         ktr[:, g4 * 2688 + w0 + 512:g4 * 2688 + w0 + 640],
                         start=True, stop=True)
                return e.matmul(pS2[:, 128:136], qh[:, up * 2048:up * 2048 + 128],
                                ktr[:, g4 * 2688 + 0:g4 * 2688 + 8],
                                start=True, stop=True)
            ctx.emit("tensor", smm2, inc=("amm", 1),
                     waits=[aqv, ("aex", ctx.v("aex"))])
            amv = ctx.v("amm")

            def astep(fn, lane="vector"):
                ctx.emit(lane, fn, inc=("aex", 1), waits=[("aex", ctx.v("aex")),
                                                          ("amm", amv)])
            ctx.emit("vector", lambda e, j=j: e.tensor_add(
                pS[:], pS[:], mskb[:, j * 648:j * 648 + 512]),
                inc=("aex", 1), waits=[("aex", ctx.v("aex")), ("amm", amv),
                                       ("aoc", ctx.v("aoc"))])
            astep(lambda e, j=j: e.tensor_add(pS2[:, 0:136], pS2[:, 0:136],
                                              mskb[:, j * 648 + 512:(j + 1) * 648]))
            astep(lambda e: e.reduce_max(out=lst[:, 0:1], in_=pS[:], axis=AX.X))
            astep(lambda e: e.reduce_max(out=lst[:, 1:2], in_=pS2[:, 0:136], axis=AX.X))
            astep(lambda e: e.tensor_max(lst[:, 0:1], lst[:, 0:1], lst[:, 1:2]))
            astep(lambda e: e.tensor_scalar_mul(lst[:, 1:2], lst[:, 0:1], -0.08838834764831845))
            astep(lambda e: e.activation(out=psb[:, 0:512], in_=pS[:], func=AF.Exp,
                                         bias=lst[:, 1:2], scale=0.08838834764831845,
                                         accum_out=lst[:, 2:3]), "scalar")
            astep(lambda e: e.activation(out=psb[:, 512:648], in_=pS2[:, 0:136],
                                         func=AF.Exp, bias=lst[:, 1:2], scale=0.08838834764831845,
                                         accum_out=lst[:, 3:4]), "scalar")
            astep(lambda e: e.tensor_add(lst[:, 2:3], lst[:, 2:3], lst[:, 3:4]))
            astep(lambda e: e.reciprocal(out=lst[:, 4:5], in_=lst[:, 2:3]))
            exv = ctx.v("aex")

            def tpfn(e):
                for s5 in range(5):
                    e.transpose(pTt[:, s5 * 128:(s5 + 1) * 128],
                                psb[:, s5 * 128:(s5 + 1) * 128], ident_sb[:])
                return e.transpose(pTt[0:8, 640:768], psb[:, 640:648], ident_sb[:])
            ctx.emit("tensor", tpfn, inc=("atp", 1), waits=[("aex", exv)])
            tpv = ctx.v("atp")
            ctx.emit("vector", lambda e: e.tensor_copy(out=ptb[:, 0:640], in_=pTt[:, 0:640]),
                     inc=("aex", 1), waits=[("aex", ctx.v("aex")), ("atp", tpv),
                                            ("apm", ctx.v("apm"))])
            ctx.emit("vector", lambda e: e.tensor_copy(out=ptb[0:8, 640:768],
                                                       in_=pTt[0:8, 640:768]),
                     inc=("aex", 1), waits=[("aex", ctx.v("aex"))])

            def pvfn(e, j=j, g4=g4, w0=w0):
                base = (w0 // 128)
                for s5 in range(5):
                    e.matmul(pO[:], ptb[:, s5 * 128:(s5 + 1) * 128],
                             vtr[:, (base + s5) * 512 + g4 * 128:(base + s5) * 512 + (g4 + 1) * 128],
                             start=(s5 == 0), stop=False)
                return e.matmul(pO[:], ptb[0:8, 640:768], vtr[0:8, 0 * 512 + g4 * 128:0 * 512 + (g4 + 1) * 128],
                                start=False, stop=True)
            ctx.emit("tensor", pvfn, inc=("apm", 1),
                     waits=[("aex", ctx.v("aex")), ("aoc", ctx.v("aoc"))])
            ctx.emit("vector", lambda e, h=h: e.tensor_scalar_mul(
                osb[:, h * 128:(h + 1) * 128], pO[:], lst[:, 4:5]),
                inc=("aoc", 1), waits=[("apm", ctx.v("apm")), ("ast", 16 * j)])
        ctx.dma("gpsimd", atd[j * 128:(j + 1) * 128, :], osb[:], "ast",
                waits=[("aoc", ctx.v("aoc"))])
    astv = ctx.v("ast")

    ctx._sbufs["le_y"] = ctx._sbufs["kcs"][:, 0:2048]
    ctx._sbufs["le_x"] = ctx._sbufs["ksn"][:, 0:2048]
    ctx._sbufs["le_oo"] = ctx._sbufs["rtmp"][:, 0:2048]
    ctx._sbufs["le_junk"] = ctx._sbufs["rtm2"][:, 0:2048]
    cfm = ctx._sbufs["mskb"]
    ctx._sbufs["p_b"] = ctx._sbufs["ktr"][0:1, 0:2048]
    ctx._sbufs["p_oF"] = ctx._sbufs["vtr"][:].bitcast(F32)
    ctx._psums["ct_tp"] = ctx._psums["ln_tp"]
    ctx._psums["ct_tq"] = ctx._psums["ln_tq"]
    pfin = []
    for c in range(4):
        ctr = emit_cast_transpose(ctx, "ct", atd, cfm, ident_sb, n_tt=4, nd=ND,
                                  src_row0=c * 512, fm_stride=512,
                                  wait_extra=[("ast", astv), ("pmm", ctx.v("pmm"))],
                                  src_bf=True)
        pfin = emit_proj(ctx, "p", cfm, wo, ao, n_tok=512, F=D, bias_dram=bo,
                         ones_sb=ones_sb, fm_ready=ctr, psA="pA", psB="pB",
                         wname="pwA", oname="p_oF", out_stage_dtype=F32, out_row0=c * 512)
    lfin = emit_ln_add(ctx, "le", ao, x, x4, wait_extra=pfin)
    ctx.wait("gpsimd", [lfin])
    return ctx


def build_l2c(ctx=None, io=None):
    ctx = ctx or Ctx()
    io = io or {}
    x = _io(ctx, io, "in", "xin", [TPC, D])
    wct = _io(ctx, io, "in", "wct", [D, 4096])
    tau = _io(ctx, io, "in", "tau", [1])
    genes = _io(ctx, io, "in", "genes", [4096, 256])
    wdt = _io(ctx, io, "in", "wdt", [256, D])
    wut = _io(ctx, io, "in", "wut", [256, D])
    nw = _io(ctx, io, "in", "norm_w", [D])
    sclr = _io(ctx, io, "in", "scl", [1])
    x5 = _io(ctx, io, "out", "X5", [TPC, D])
    xnfm_d = ctx.dram_tmp("XNFM", [D, TPC], BF16)
    xn32_d = ctx.dram_tmp("XN32", [TPC, D])
    lg_d = ctx.dram_tmp("LG", [TPC, 4096])
    phi_d = ctx.dram_tmp("PHI", [TPC, 4096], BF16)
    rz_d = ctx.dram_tmp("RZ", [TPC, 1])
    ex_d = ctx.dram_tmp("EX", [TPC, 256])
    ident_sb, ones_sb = setup_common(ctx)
    onesf = ctx.sbuf("onesf", [1, 128], F32)
    onesfd = ctx.const("onesf_l2c", np.ones((1, 128), dtype=np.float32))
    ctx.dma("sync", onesf[:], onesfd, "lof")
    F32EPS = float(np.finfo(np.float32).eps)

    tsb = ctx.sbuf("tsb", [1, 2], F32)
    tcol = ctx.sbuf("tcol", [128, 2], F32)
    tps = ctx.psum("tps", [128, 512])
    ctx.dma("sync", tsb[0:1, 0:1], tau.rearrange("(o f) -> o f", o=1), "lof")
    ctx.dma("sync", tsb[0:1, 1:2], sclr.rearrange("(o f) -> o f", o=1), "lof")
    ctx.emit("vector", lambda e: e.tensor_scalar_max(tsb[0:1, 0:1], tsb[0:1, 0:1], 0.1),
             inc=("tpp", 1), waits=[("lof", 48)])
    ctx.emit("vector", lambda e: e.reciprocal(out=tsb[0:1, 0:1], in_=tsb[0:1, 0:1]),
             inc=("tpp", 1), waits=[("tpp", ctx.v("tpp"))])
    ctx.emit("tensor", lambda e: e.matmul(tps[:, 0:2], onesf[:], tsb[0:1, 0:2],
                                          start=True, stop=True),
             inc=("tpp", 1), waits=[("tpp", ctx.v("tpp"))])
    ctx.emit("vector", lambda e: e.tensor_copy(out=tcol[:], in_=tps[:, 0:2]),
             inc=("tpp", 1), waits=[("tpp", ctx.v("tpp"))])
    tppv = ctx.v("tpp")

    nwb = ctx.sbuf("nwb", [128, D], F32)
    nwr = ctx.sbuf("nwr", [1, D], F32)
    ctx.dma("sync", nwr[:], nw.rearrange("(o f) -> o f", o=1), "lnw")
    for i in range(4):
        ctx.emit("tensor", lambda e, i=i: e.matmul(tps[:, 0:512], onesf[:],
                                                   nwr[0:1, i * 512:(i + 1) * 512],
                                                   start=True, stop=True),
                 inc=("nwm", 1), waits=[("lnw", 16), ("tpp", tppv), ("nwe", ctx.v("nwe"))])
        ctx.emit("vector", lambda e, i=i: e.tensor_copy(out=nwb[:, i * 512:(i + 1) * 512],
                                                        in_=tps[:, 0:512]),
                 inc=("nwe", 1), waits=[("nwm", ctx.v("nwm"))])
    nwev = ctx.v("nwe")
    nwfm = ctx.sbuf("nwfm", [128, ND], F32)
    ctx.dma("gpsimd", nwfm[:], nw.rearrange("(dc p) -> p dc", p=128), "lnw2",
            allow_slow_non_contiguous=True)
    lnwv = ctx.v("lnw2")

    xn_fm_t = ctx.sbuf("xn_fm_t", [128, ND * 128], BF16)
    for tt in range(NT):
        f1 = emit_ln_to_fm(ctx, "ln", x, xn_fm_t, ident_sb, n_tt=1, x_row0=tt * 128,
                           fm_stride=128, rms=True, eps=F32EPS, xn32_dram=xn32_d,
                           wait_extra=[("fmst", 16 * tt), ("lno3", 16 * tt)])
        ctx.dma("sync", xnfm_d[:, tt * 128:(tt + 1) * 128].rearrange("(dc p) f -> p dc f", p=128),
                xn_fm_t[:].rearrange("p (dc f) -> p dc f", dc=ND), "fmst", waits=[f1])
    fmstv = ctx.v("fmst")
    lno3v = ctx.v("lno3")

    xnc = ctx.sbuf("xnc", [128, 2 * ND * 512], BF16)
    prevs = []
    for ci in range(4):
        par = ci % 2
        ctx.dma("sync", xnc[:, par * ND * 512:(par + 1) * ND * 512]
                .rearrange("p (dc f) -> p dc f", dc=ND),
                xnfm_d[:, ci * 512:(ci + 1) * 512].rearrange("(dc p) f -> p dc f", p=128),
                "xcl%d" % par, waits=[("fmst", fmstv)] + prevs)
        fmr = ("xcl%d" % par, 16 * (ci // 2 + 1))
        lfin = emit_proj(ctx, "pl", xnc, wct, lg_d, n_tok=512, fm_stride=512,
                         fm_base=par * ND * 512, mode="tok", F=4096, fm_ready=fmr,
                         psA="pA", psB="pB", wname="pwA", oname="poA",
                         out_stage_dtype=F32, out_row0=ci * 512,
                         w_scale_fm=nwfm, w_scale_rdy=("lnw2", lnwv), wait_extra=prevs)
        prevs = lfin
    lgfin = prevs

    at_ = ctx.sbuf("at_", [128, 4096], F32)
    ap_ = ctx.sbuf("ap_", [128, 4096], F32)
    gbuf = ctx.sbuf("gbuf", [128, 32], F32)
    phib = ctx.sbuf("phib", [128, 4096], BF16)
    sl2 = ctx.sbuf("sl2", [128, 8], F32)
    for tt in range(NT):
        ctx.dma("sync", at_[:], lg_d[tt * 128:(tt + 1) * 128, :], "tld",
                waits=[("tphi", ctx.v("tphi"))] + (lgfin if tt < 2 else []))
        tldv = ctx.v("tld")

        def ts(lane, fn):
            ctx.emit(lane, fn, inc=("tsq", 1), waits=[("tsq", ctx.v("tsq")),
                                                      ("tld", tldv), ("tpp", tppv)])
        src_buf = at_
        for rd in range(4):
            ts("vector", lambda e, rd=rd, sb=src_buf: e.max(out=gbuf[:, rd * 8:(rd + 1) * 8],
                                                            in_=sb[:]))
            ts("vector", lambda e, rd=rd, sb=src_buf: e.match_replace(
                out=ap_[:], in_to_replace=gbuf[:, rd * 8:(rd + 1) * 8],
                in_values=sb[:], imm_value=-BIG))
            src_buf = ap_
        ts("vector", lambda e: e.scalar_tensor_tensor(
            out=sl2[:, 0:1], in0=gbuf[:, 0:1], scalar=-1.0, in1=tcol[:, 0:1],
            op0=ALU.mult, op1=ALU.mult))
        ts("vector", lambda e: e.tensor_tensor(out=phib[:], in0=at_[:], in1=ap_[:],
                                               op=ALU.not_equal))
        ts("scalar", lambda e: e.activation(out=at_[:], in_=at_[:], func=AF.Exp,
                                            bias=sl2[:, 0:1], scale=tcol[:, 0:1]))
        ts("vector", lambda e: e.tensor_mul(phib[:], phib[:], at_[:]))
        ts("scalar", lambda e: e.activation(out=gbuf[:], in_=gbuf[:], func=AF.Exp,
                                            bias=sl2[:, 0:1], scale=tcol[:, 0:1],
                                            accum_out=sl2[:, 1:2]))
        ts("vector", lambda e: e.reciprocal(out=sl2[:, 2:3], in_=sl2[:, 1:2]))
        tsv = ctx.v("tsq")
        ctx.dma("gpsimd", phi_d[tt * 128:(tt + 1) * 128, :], phib[:], "tphi",
                waits=[("tsq", tsv)])
        ctx.dma("gpsimd", rz_d[tt * 128:(tt + 1) * 128, :], sl2[:, 2:3], "tphi",
                waits=[("tsq", tsv)])
    tphv = ctx.v("tphi")

    cfm = ctx.sbuf("cfm", [128, 32 * 512], BF16)
    gsb = ctx.sbuf("gsb", [128, 32 * 256], BF16)
    rzc = ctx.sbuf("rzc", [128, 4], F32)
    exs = ctx.sbuf("exs", [128, 256], F32)
    gpsA = ctx.psum("pA", [128, 512])
    ctx.dma("gpsimd", gsb[:].rearrange("p (dc f) -> p dc f", dc=32),
            genes.rearrange("(dc p) f -> p dc f", p=128), "ggl")
    for c in range(4):
        ctr = emit_cast_transpose(ctx, "ct", phi_d, cfm, ident_sb, n_tt=4, nd=32,
                                  src_row0=c * 512, fm_stride=512,
                                  wait_extra=[("tphi", tphv), ("gmm", ctx.v("gmm"))],
                                  src_bf=True)
        for st4 in range(4):
            ctx.dma("sync", rzc[:, st4:st4 + 1],
                    rz_d[c * 512 + st4 * 128:c * 512 + (st4 + 1) * 128, :], "rzl",
                    waits=[("gev", ctx.v("gev"))])
        rzlv = ctx.v("rzl")
        for st4 in range(4):
            def gfn(e, st4=st4):
                ins = None
                for g in range(32):
                    ins = e.matmul(gpsA[:, 0:256],
                                   cfm[:, g * 512 + st4 * 128:g * 512 + (st4 + 1) * 128],
                                   gsb[:, g * 256:(g + 1) * 256],
                                   start=(g == 0), stop=(g == 31))
                return ins
            ctx.emit("tensor", gfn, inc=("gmm", 1),
                     waits=[ctr, ("ggl", 16), ("gev", ctx.v("gev"))])
            ctx.emit("vector", lambda e, st4=st4: e.tensor_scalar_mul(
                exs[:], gpsA[:, 0:256], rzc[:, st4:st4 + 1]),
                inc=("gev", 1), waits=[("gmm", ctx.v("gmm")), ("rzl", rzlv),
                                       ("gst", 16 * (c * 4 + st4))])
            ctx.dma("gpsimd", ex_d[c * 512 + st4 * 128:c * 512 + (st4 + 1) * 128, :],
                    exs[:], "gst", waits=[("gev", ctx.v("gev"))])
    gstv = ctx.v("gst")

    wdsb = ctx._sbufs["xnc"][:, 0:4096]
    wusb = ctx._sbufs["xnc"][:, 4096:8192]
    ctx.dma("gpsimd", wdsb.rearrange("p (dc f) -> p dc f", dc=2),
            wdt.rearrange("(dc p) f -> p dc f", p=128), "lwd", waits=lgfin)
    ctx.dma("gpsimd", wusb.rearrange("p (dc f) -> p dc f", dc=2),
            wut.rearrange("(dc p) f -> p dc f", p=128), "lwd")
    exb = ctx._sbufs["phib"][:, 0:256]
    exf = ctx._sbufs["phib"][:, 256:512]
    dsb = ctx._sbufs["at_"][:, 0:2048]
    usb = ctx._sbufs["ap_"][:, 0:2048]
    ya = ctx.sbuf("ln_x", [128, D], F32)
    yb = ctx.sbuf("ln_junk", [128, D], F32)
    yst = ctx.sbuf("yst", [128, 8], F32)
    pTb2 = ctx.psum("ln_tp", [128, 1024], BF16)
    pB2 = ctx.psum("pB", [128, 512])
    for tt in range(NT):
        sq0 = ctx.v("fsq")
        ctx.dma("gpsimd", exb, ex_d[tt * 128:(tt + 1) * 128, :], "fldg",
                waits=[("fsq", sq0), ("gst", gstv), ("lwd", 32)])
        ctx.dma("sync", ya[:], xn32_d[tt * 128:(tt + 1) * 128, :], "fld",
                waits=[("lno3", lno3v), ("fso", 16 * tt)])
        fldv = ctx.v("fld")
        fldgv = ctx.v("fldg")

        def fs(lane, fn):
            ctx.emit(lane, fn, inc=("fsq", 1), waits=[("fsq", ctx.v("fsq")),
                                                      ("fld", fldv), ("fldg", fldgv),
                                                      ("nwe", nwev)])
        fs("tensor", lambda e: (e.transpose(pTb2[:, 0:128], exb[:, 0:128], ident_sb[:]),
                                e.transpose(pTb2[:, 128:256], exb[:, 128:256], ident_sb[:]))[-1])
        fs("vector", lambda e: e.tensor_copy(out=exf, in_=pTb2[:, 0:256]))
        for fc in range(4):
            fs("tensor", lambda e, fc=fc: (
                e.matmul(pB2[:, 0:512], exf[:, 0:128],
                         wdsb[:, 0 * D + fc * 512:0 * D + (fc + 1) * 512],
                         start=True, stop=False),
                e.matmul(pB2[:, 0:512], exf[:, 128:256],
                         wdsb[:, 1 * D + fc * 512:1 * D + (fc + 1) * 512],
                         start=False, stop=True))[-1])
            fs("scalar", lambda e, fc=fc: e.activation(out=dsb[:, fc * 512:(fc + 1) * 512],
                                                       in_=pB2[:, 0:512], func=AF.Sigmoid,
                                                       bias=0.0, scale=2.0))
            fs("vector", lambda e, fc=fc: e.tensor_scalar(dsb[:, fc * 512:(fc + 1) * 512],
                                                          dsb[:, fc * 512:(fc + 1) * 512],
                                                          2.0, -1.0, op0=ALU.mult, op1=ALU.add))
            fs("tensor", lambda e, fc=fc: (
                e.matmul(pB2[:, 0:512], exf[:, 0:128],
                         wusb[:, 0 * D + fc * 512:0 * D + (fc + 1) * 512],
                         start=True, stop=False),
                e.matmul(pB2[:, 0:512], exf[:, 128:256],
                         wusb[:, 1 * D + fc * 512:1 * D + (fc + 1) * 512],
                         start=False, stop=True))[-1])
            fs("vector", lambda e, fc=fc: e.tensor_copy(out=usb[:, fc * 512:(fc + 1) * 512],
                                                        in_=pB2[:, 0:512]))
        fs("vector", lambda e: e.tensor_mul(ya[:], ya[:], nwb[:]))
        fs("vector", lambda e: e.tensor_mul(ya[:], ya[:], dsb[:]))
        fs("vector", lambda e: e.tensor_mul(ya[:], ya[:], usb[:]))
        fs("vector", lambda e: e.tensor_scalar_mul(ya[:], ya[:], tcol[:, 1:2]))
        fs("scalar", lambda e: e.activation(out=yb[:], in_=ya[:], func=AF.Square,
                                            bias=0.0, scale=1.0, accum_out=yst[:, 1:2]))
        fs("vector", lambda e: e.reduce_sum(out=yst[:, 0:1], in_=ya[:], axis=AX.X))
        fs("vector", lambda e: e.tensor_scalar_mul(yst[:, 2:3], yst[:, 0:1], 1.0 / D))
        fs("vector", lambda e: e.scalar_tensor_tensor(out=yst[:, 3:4], in0=yst[:, 2:3],
                                                      scalar=-1.0, in1=yst[:, 2:3],
                                                      op0=ALU.mult, op1=ALU.mult))
        fs("vector", lambda e: e.scalar_tensor_tensor(out=yst[:, 3:4], in0=yst[:, 1:2],
                                                      scalar=1.0 / D, in1=yst[:, 3:4],
                                                      op0=ALU.mult, op1=ALU.add))
        fs("vector", lambda e: e.tensor_scalar_add(yst[:, 3:4], yst[:, 3:4], LN_EPS))
        fs("scalar", lambda e: e.activation(out=yst[:, 4:5], in_=yst[:, 3:4], func=AF.Sqrt,
                                            bias=0.0, scale=1.0))
        fs("vector", lambda e: e.reciprocal(out=yst[:, 4:5], in_=yst[:, 4:5]))
        fs("vector", lambda e: e.scalar_tensor_tensor(out=yst[:, 5:6], in0=yst[:, 2:3],
                                                      scalar=-1.0, in1=yst[:, 4:5],
                                                      op0=ALU.mult, op1=ALU.mult))
        ctx.dma("sync", yb[:], x[tt * 128:(tt + 1) * 128, :], "fxl",
                waits=[("fsq", ctx.v("fsq"))])
        fs("vector", lambda e: e.scalar_tensor_tensor(
            out=ya[:], in0=ya[:], scalar=yst[:, 4:5],
            in1=yst[:, 5:6].to_broadcast([128, D]), op0=ALU.mult, op1=ALU.add))
        ctx.emit("vector", lambda e: e.tensor_add(ya[:], ya[:], yb[:]),
                 inc=("fsq", 1), waits=[("fsq", ctx.v("fsq")), ("fxl", ctx.v("fxl"))])
        ctx.dma("gpsimd", x5[tt * 128:(tt + 1) * 128, :], ya[:], "fso",
                waits=[("fsq", ctx.v("fsq"))])
        ctx.wait("sync", [("fso", ctx.v("fso"))]) if False else None
    ctx.wait("gpsimd", [("fso", ctx.v("fso"))])
    return ctx


def build_l3a(ctx=None, io=None):
    ctx = ctx or Ctx()
    io = io or {}
    x = _io(ctx, io, "in", "xin", [TPC, D])
    wq = _io(ctx, io, "in", "wq", [D, D])
    bq = _io(ctx, io, "in", "bq", [D])
    wk = _io(ctx, io, "in", "wk", [D, D])
    bk = _io(ctx, io, "in", "bk", [D])
    wv = _io(ctx, io, "in", "wv", [D, D])
    bv = _io(ctx, io, "in", "bv", [D])
    phiq_d = _io(ctx, io, "out", "PHIQ", [TPC, D])
    kvc_d = _io(ctx, io, "out", "KVC", [TPC, D])
    car_d = _io(ctx, io, "out", "CARRY", [1, D])
    qp_d = ctx.dram_tmp("QP", [TPC, D])
    kp_d = ctx.dram_tmp("KP", [TPC, D])
    vp_d = ctx.dram_tmp("VP", [TPC, D])
    xnfm_d = ctx.dram_tmp("XNFM", [D, TPC], BF16)
    ident_sb, ones_sb = setup_common(ctx)
    utd = ctx.const("l3_ut", np.triu(np.ones((128, 128), dtype=ml_dtypes.bfloat16)))
    uts = ctx.sbuf("uts", [128, 128], BF16)
    onesfd = ctx.const("onesf_l3a", np.ones((1, 128), dtype=np.float32))
    onesf = ctx.sbuf("onesf", [1, 128], F32)
    ctx.dma("sync", uts[:], utd, "lut")
    ctx.dma("sync", onesf[:], onesfd, "lut")

    xn_fm_t = ctx.sbuf("xn_fm_t", [128, ND * 128], BF16)
    for tt in range(NT):
        f1 = emit_ln_to_fm(ctx, "ln", x, xn_fm_t, ident_sb, n_tt=1, x_row0=tt * 128,
                           fm_stride=128, wait_extra=[("fmst", 16 * tt)])
        ctx.dma("sync", xnfm_d[:, tt * 128:(tt + 1) * 128].rearrange("(dc p) f -> p dc f", p=128),
                xn_fm_t[:].rearrange("p (dc f) -> p dc f", dc=ND), "fmst", waits=[f1])
    fmstv = ctx.v("fmst")

    xnc = ctx.sbuf("xnc", [128, 2 * ND * 512], BF16)
    prevs = []
    for ci in range(4):
        par = ci % 2
        ctx.dma("sync", xnc[:, par * ND * 512:(par + 1) * ND * 512]
                .rearrange("p (dc f) -> p dc f", dc=ND),
                xnfm_d[:, ci * 512:(ci + 1) * 512].rearrange("(dc p) f -> p dc f", p=128),
                "xcl%d" % par, waits=[("fmst", fmstv)] + prevs)
        fmr = ("xcl%d" % par, 16 * (ci // 2 + 1))
        prevs2 = []
        for pre_, wd, bd_, od in [("pq", wq, bq, qp_d), ("pk", wk, bk, kp_d),
                                  ("pv", wv, bv, vp_d)]:
            fin = emit_proj(ctx, pre_, xnc, wd, od, n_tok=512, fm_stride=512,
                            fm_base=par * ND * 512, mode="tok", F=D, fm_ready=fmr,
                            psA="pA", psB="pB", wname="pwA", oname="poA",
                            bias_dram=bd_, ones_sb=ones_sb, out_stage_dtype=F32,
                            out_row0=ci * 512,
                            wait_extra=prevs + prevs2 + [("pqev", ctx.v("pqev")),
                                                         ("pkev", ctx.v("pkev")),
                                                         ("pvev", ctx.v("pvev"))])
            prevs2 = prevs2 + fin
        prevs = prevs2
    pjf = prevs

    qa = ctx.sbuf("qa", [128, D], F32)
    ka = ctx.sbuf("ka", [128, D], F32)
    va = ctx.sbuf("va", [128, D], F32)
    tb = ctx.sbuf("tb", [128, D], F32)
    kvb = ctx.sbuf("kvb", [128, D], BF16)
    cum = ctx.sbuf("cum", [128, D], F32)
    crow = ctx.sbuf("crow", [1, D], F32)
    cps = ctx.psum("pA", [128, 512])
    ctx.emit("vector", lambda e: e.memset(crow[:], 0.0), inc=("csq", 1), waits=[])
    for tt in range(NT):
        ctx.dma("sync", qa[:], qp_d[tt * 128:(tt + 1) * 128, :], "cld",
                waits=[("csq", ctx.v("csq")), ("cst", ctx.v("cst"))] + (pjf if tt < 2 else []))
        ctx.dma("sync", ka[:], kp_d[tt * 128:(tt + 1) * 128, :], "cld")
        ctx.dma("sync", va[:], vp_d[tt * 128:(tt + 1) * 128, :], "cld")
        cldv = ctx.v("cld")

        def cs(lane, fn):
            ctx.emit(lane, fn, inc=("csq", 1), waits=[("csq", ctx.v("csq")),
                                                      ("cld", cldv), ("lut", 32)])
        for buf, outd in [(qa, phiq_d), (ka, None)]:
            cs("vector", lambda e, b=buf: e.tensor_scalar_min(tb[:], b[:], 0.0))
            cs("scalar", lambda e: e.activation(out=tb[:], in_=tb[:], func=AF.Exp,
                                                bias=0.0, scale=1.0))
            cs("vector", lambda e, b=buf: e.tensor_scalar_max(b[:], b[:], 0.0))
            cs("vector", lambda e, b=buf: e.tensor_add(b[:], b[:], tb[:]))
            if outd is not None:
                ctx.dma("gpsimd", outd[tt * 128:(tt + 1) * 128, :], buf[:], "cst",
                        waits=[("csq", ctx.v("csq"))])
        cs("vector", lambda e: e.tensor_mul(va[:], ka[:], va[:]))
        cs("vector", lambda e: e.tensor_copy(out=kvb[:], in_=va[:]))
        for fc in range(4):
            cs("tensor", lambda e, fc=fc: (
                e.matmul(cps[:, 0:512], uts[:], kvb[:, fc * 512:(fc + 1) * 512],
                         start=True, stop=False),
                e.matmul(cps[:, 0:512], onesf[:], crow[0:1, fc * 512:(fc + 1) * 512],
                         start=False, stop=True))[-1])
            cs("vector", lambda e, fc=fc: e.tensor_copy(out=cum[:, fc * 512:(fc + 1) * 512],
                                                        in_=cps[:, 0:512]))
        ctx.dma("gpsimd", kvc_d[tt * 128:(tt + 1) * 128, :], cum[:], "cst",
                waits=[("csq", ctx.v("csq"))])
        ctx.dma("gpsimd", crow[:], cum[127:128, :], "ccr",
                waits=[("csq", ctx.v("csq"))])
        ctx.wait("tensor", [("ccr", ctx.v("ccr"))])
        ctx.wait("vector", [("ccr", ctx.v("ccr"))])
    ctx.dma("gpsimd", car_d, crow[:], "cst", waits=[("ccr", ctx.v("ccr"))])
    ctx.wait("gpsimd", [("cst", ctx.v("cst"))])
    return ctx


def build_l3b(ctx=None, io=None):
    ctx = ctx or Ctx()
    io = io or {}
    x = _io(ctx, io, "in", "xin", [TPC, D])
    phiq_d = _io(ctx, io, "in", "PHIQ", [TPC, D])
    kvc_d = _io(ctx, io, "in", "KVC", [TPC, D])
    scar = _io(ctx, io, "in", "SCAR", [1, D])
    wo = _io(ctx, io, "in", "wo", [D, D])
    bo = _io(ctx, io, "in", "bo", [D])
    x6 = _io(ctx, io, "out", "X6", [TPC, D])
    ctx_d = ctx.dram_tmp("CTX", [TPC, D])
    ao = ctx.dram_tmp("AO", [TPC, D])
    ident_sb, ones_sb = setup_common(ctx)
    onesfd = ctx.const("onesf_l3b", np.ones((1, 128), dtype=np.float32))
    onesf = ctx.sbuf("onesf", [1, 128], F32)
    scr = ctx.sbuf("scr", [1, D], F32)
    scb = ctx.sbuf("scb", [128, D], F32)
    sps = ctx.psum("pA", [128, 512])
    ctx.dma("sync", onesf[:], onesfd, "lsc")
    ctx.dma("sync", scr[:], scar, "lsc")
    for i in range(4):
        ctx.emit("tensor", lambda e, i=i: e.matmul(sps[:, 0:512], onesf[:],
                                                   scr[0:1, i * 512:(i + 1) * 512],
                                                   start=True, stop=True),
                 inc=("scm", 1), waits=[("lsc", 32), ("sce", ctx.v("sce"))])
        ctx.emit("vector", lambda e, i=i: e.tensor_copy(out=scb[:, i * 512:(i + 1) * 512],
                                                        in_=sps[:, 0:512]),
                 inc=("sce", 1), waits=[("scm", ctx.v("scm"))])
    scev = ctx.v("sce")
    SCALE = float(128 ** (-0.5))
    pa = ctx.sbuf("pa", [128, D], F32)
    kb = ctx.sbuf("kb", [128, D], F32)
    for tt in range(NT):
        ctx.dma("sync", pa[:], phiq_d[tt * 128:(tt + 1) * 128, :], "bld",
                waits=[("bst", 16 * tt)])
        ctx.dma("sync", kb[:], kvc_d[tt * 128:(tt + 1) * 128, :], "bld")
        bldv = ctx.v("bld")
        ctx.emit("vector", lambda e: e.tensor_add(kb[:], kb[:], scb[:]),
                 inc=("bsq", 1), waits=[("bld", bldv), ("sce", scev),
                                        ("bsq", ctx.v("bsq"))])
        ctx.emit("vector", lambda e: e.tensor_mul(pa[:], pa[:], kb[:]),
                 inc=("bsq", 1), waits=[("bsq", ctx.v("bsq"))])
        ctx.emit("vector", lambda e: e.tensor_scalar_mul(pa[:], pa[:], SCALE),
                 inc=("bsq", 1), waits=[("bsq", ctx.v("bsq"))])
        ctx.dma("gpsimd", ctx_d[tt * 128:(tt + 1) * 128, :], pa[:], "bst",
                waits=[("bsq", ctx.v("bsq"))])
    bstv = ctx.v("bst")
    cfm = ctx.sbuf("cfm", [128, ND * 512], BF16)
    pfin = []
    for c in range(4):
        ctr = emit_cast_transpose(ctx, "ct", ctx_d, cfm, ident_sb, n_tt=4, nd=ND,
                                  src_row0=c * 512, fm_stride=512,
                                  wait_extra=[("bst", bstv), ("pmm", ctx.v("pmm"))])
        pfin = emit_proj(ctx, "p", cfm, wo, ao, n_tok=512, F=D, bias_dram=bo,
                         ones_sb=ones_sb, fm_ready=ctr, psA="pA", psB="pB",
                         out_stage_dtype=F32, out_row0=c * 512)
    lfin = emit_ln_add(ctx, "le", ao, x, x6, wait_extra=pfin)
    ctx.wait("gpsimd", [lfin])
    return ctx


def build_l3c(ctx=None, io=None):
    ctx = ctx or Ctx()
    io = io or {}
    HP = 5632
    NREAL = 5461 * 2048
    x = _io(ctx, io, "in", "xin", [TPC, D])
    w1t = _io(ctx, io, "in", "w1t", [D, HP])
    w3t = _io(ctx, io, "in", "w3t", [D, HP])
    w2tt = _io(ctx, io, "in", "w2tt", [HP, D])
    x7 = _io(ctx, io, "out", "X7", [TPC, D])
    wq_d = [ctx.dram_tmp("W1Q", [D, HP], BF16), ctx.dram_tmp("W3Q", [D, HP], BF16),
            ctx.dram_tmp("W2Q", [HP, D], BF16)]
    xnfm_d = ctx.dram_tmp("XNFM", [D, TPC], BF16)
    h1_d = ctx.dram_tmp("H1", [TPC, HP])
    h3_d = ctx.dram_tmp("H3", [TPC, HP])
    g_d = ctx.dram_tmp("G", [TPC, HP], BF16)
    ao = ctx.dram_tmp("AO", [TPC, D])
    ident_sb, ones_sb = setup_common(ctx)
    onesfd = ctx.const("onesf_l3c", np.ones((1, 128), dtype=np.float32))
    onesf = ctx.sbuf("onesf", [1, 128], F32)
    onescd = ctx.const("onesc_l3c", np.ones((128, 1), dtype=np.float32))
    onesc = ctx.sbuf("onesc", [128, 1], F32)
    ctx.dma("sync", onesf[:], onesfd, "lof")
    ctx.dma("sync", onesc[:], onescd, "lof")

    wf = ctx.sbuf("wf", [128, HP], F32)
    wb = ctx.sbuf("wb", [128, HP], BF16)
    wtmp = ctx.sbuf("wtmp", [128, HP], BF16)
    acol = ctx.sbuf("acol", [128, 4], F32)
    scol = ctx.sbuf("scol", [128, 8], F32)
    qps = ctx.psum("pA", [128, 512])

    def quant_matrix(mi, wsrc, wdst, rows, width, ready=()):
        pre = "q%d" % mi
        ntt_ = rows // 128
        ctx.emit("vector", lambda e: e.memset(acol[:, 0:1], 0.0), inc=(pre + "s", 1),
                 waits=[(pre + "s", ctx.v(pre + "s"))] + list(ready))
        for tt in range(ntt_):
            ctx.dma("sync", wf[:, 0:width], wsrc[tt * 128:(tt + 1) * 128, :], pre + "l",
                    waits=[(pre + "s", ctx.v(pre + "s"))] + list(ready))
            lv = ctx.v(pre + "l")
            ctx.emit("scalar", lambda e, width=width: e.activation(
                out=wb[:, 0:width], in_=wf[:, 0:width], func=AF.Abs, bias=0.0, scale=1.0,
                accum_out=acol[:, 1:2]),
                inc=(pre + "s", 1), waits=[(pre + "l", lv), (pre + "s", ctx.v(pre + "s"))])
            ctx.emit("vector", lambda e: e.tensor_add(acol[:, 0:1], acol[:, 0:1], acol[:, 1:2]),
                     inc=(pre + "s", 1), waits=[(pre + "s", ctx.v(pre + "s"))])
        ctx.emit("tensor", lambda e: e.matmul(qps[0:1, 0:1], acol[:, 0:1], onesc[:],
                                              start=True, stop=True),
                 inc=(pre + "s", 1), waits=[(pre + "s", ctx.v(pre + "s")), ("lof", 32)])
        ctx.emit("vector", lambda e: e.tensor_copy(out=acol[:, 2:3][0:1, :], in_=qps[0:1, 0:1]),
                 inc=(pre + "s", 1), waits=[(pre + "s", ctx.v(pre + "s"))])
        ctx.emit("tensor", lambda e: e.matmul(qps[:, 0:1], onesf[:], acol[0:1, 2:3],
                                              start=True, stop=True),
                 inc=(pre + "s", 1), waits=[(pre + "s", ctx.v(pre + "s"))])
        ctx.emit("vector", lambda e, mi=mi: e.tensor_scalar_mul(
            scol[:, 2 * mi:2 * mi + 1], qps[:, 0:1], 0.5 / NREAL),
            inc=(pre + "s", 1), waits=[(pre + "s", ctx.v(pre + "s"))])
        ctx.emit("vector", lambda e, mi=mi: e.tensor_scalar_mul(
            scol[:, 2 * mi + 1:2 * mi + 2], qps[:, 0:1], 1.0 / NREAL),
            inc=(pre + "s", 1), waits=[(pre + "s", ctx.v(pre + "s"))])
        ctx.emit("vector", lambda e, mi=mi: e.tensor_scalar_mul(
            acol[:, 3:4], scol[:, 2 * mi:2 * mi + 1], -1.0),
            inc=(pre + "s", 1), waits=[(pre + "s", ctx.v(pre + "s"))])
        for tt in range(ntt_):
            ctx.dma("sync", wf[:, 0:width], wsrc[tt * 128:(tt + 1) * 128, :], pre + "l",
                    waits=[(pre + "s", ctx.v(pre + "s")), (pre + "w", ctx.v(pre + "w"))])
            lv = ctx.v(pre + "l")
            ctx.emit("vector", lambda e, width=width, mi=mi: e.tensor_scalar(
                wb[:, 0:width], wf[:, 0:width], scol[:, 2 * mi:2 * mi + 1], None,
                op0=ALU.is_gt),
                inc=(pre + "s", 1), waits=[(pre + "l", lv), (pre + "s", ctx.v(pre + "s")),
                                           (pre + "w", ctx.v(pre + "w"))])
            ctx.emit("vector", lambda e, width=width: e.tensor_scalar(
                wtmp[:, 0:width], wf[:, 0:width], acol[:, 3:4], None, op0=ALU.is_lt),
                inc=(pre + "s", 1), waits=[(pre + "s", ctx.v(pre + "s"))])
            ctx.emit("vector", lambda e, width=width: e.tensor_sub(
                wb[:, 0:width], wb[:, 0:width], wtmp[:, 0:width]),
                inc=(pre + "s", 1), waits=[(pre + "s", ctx.v(pre + "s"))])
            ctx.dma("gpsimd", wdst[tt * 128:(tt + 1) * 128, :], wb[:, 0:width], pre + "w",
                    waits=[(pre + "s", ctx.v(pre + "s"))])
        return (pre + "w", ctx.v(pre + "w"))

    qf1 = quant_matrix(0, w1t, wq_d[0], D, HP)
    qf3 = quant_matrix(1, w3t, wq_d[1], D, HP, ready=[qf1])
    qf2 = quant_matrix(2, w2tt, wq_d[2], HP, D, ready=[qf3])

    xn_fm_t = ctx.sbuf("xn_fm_t", [128, ND * 128], BF16)
    for tt in range(NT):
        f1 = emit_ln_to_fm(ctx, "ln", x, xn_fm_t, ident_sb, n_tt=1, x_row0=tt * 128,
                           fm_stride=128, wait_extra=[("fmst", 16 * tt)])
        ctx.dma("sync", xnfm_d[:, tt * 128:(tt + 1) * 128].rearrange("(dc p) f -> p dc f", p=128),
                xn_fm_t[:].rearrange("p (dc f) -> p dc f", dc=ND), "fmst", waits=[f1])
    fmstv = ctx.v("fmst")

    xnc = ctx.sbuf("xnc", [128, 2 * ND * 512], BF16)
    prevs = [qf1, qf3, qf2]
    for ci in range(4):
        par = ci % 2
        ctx.dma("sync", xnc[:, par * ND * 512:(par + 1) * ND * 512]
                .rearrange("p (dc f) -> p dc f", dc=ND),
                xnfm_d[:, ci * 512:(ci + 1) * 512].rearrange("(dc p) f -> p dc f", p=128),
                "xcl%d" % par, waits=[("fmst", fmstv)] + prevs)
        fmr = ("xcl%d" % par, 16 * (ci // 2 + 1))
        p1 = emit_proj(ctx, "p1", xnc, wq_d[0], h1_d, n_tok=512, fm_stride=512,
                       fm_base=par * ND * 512, mode="tok", F=HP, fm_ready=fmr,
                       psA="pA", psB="pB", wname="pwA", oname="poA",
                       out_stage_dtype=F32, out_row0=ci * 512,
                       scale_ap=scol[:, 1:2], scale_ap_rdy=("q0s", ctx.v("q0s")),
                       wait_extra=prevs + [("p1ev", ctx.v("p1ev")), ("p3ev", ctx.v("p3ev"))])
        p3 = emit_proj(ctx, "p3", xnc, wq_d[1], h3_d, n_tok=512, fm_stride=512,
                       fm_base=par * ND * 512, mode="tok", F=HP, fm_ready=fmr,
                       psA="pA", psB="pB", wname="pwA", oname="poA",
                       out_stage_dtype=F32, out_row0=ci * 512,
                       scale_ap=scol[:, 3:4], scale_ap_rdy=("q1s", ctx.v("q1s")),
                       wait_extra=p1 + [("p3ev", ctx.v("p3ev"))])
        prevs = p1 + p3
    hfin = prevs

    ga = ctx.sbuf("wf", None, None) if False else None
    ha = ctx._sbufs["wf"][:, 0:HP]
    hb = ctx._sbufs["wb"][:, 0:HP]
    hg = ctx._sbufs["wtmp"][:, 0:HP]
    for tt in range(NT):
        ctx.dma("sync", ha, h1_d[tt * 128:(tt + 1) * 128, :], "gld",
                waits=[("gsq", ctx.v("gsq"))] + (hfin if tt < 2 else []))
        ctx.dma("gpsimd", hb, h3_d[tt * 128:(tt + 1) * 128, :], "gldg",
                waits=[("gsq", ctx.v("gsq"))] + (hfin if tt < 2 else []))
        glv, glgv = ctx.v("gld"), ctx.v("gldg")
        ctx.emit("scalar", lambda e: e.activation(out=hg, in_=ha, func=AF.Silu,
                                                  bias=0.0, scale=1.0),
                 inc=("gsq", 1), waits=[("gld", glv), ("gsq", ctx.v("gsq")),
                                        ("gst", 16 * tt)])
        ctx.emit("vector", lambda e: e.tensor_mul(hg, hg, hb),
                 inc=("gsq", 1), waits=[("gldg", glgv), ("gsq", ctx.v("gsq"))])
        ctx.dma("gpsimd", g_d[tt * 128:(tt + 1) * 128, :], hg, "gst",
                waits=[("gsq", ctx.v("gsq"))])
    gstv = ctx.v("gst")

    cfm = ctx.sbuf("cfm", [128, 44 * 512], BF16)
    pw2 = ctx._sbufs["xnc"][:, 0:22 * 512]
    ctx._psums["ct_tp"] = ctx._psums["ln_tp"]
    ctx._psums["ct_tq"] = ctx._psums["ln_tq"]
    po2 = ctx._sbufs["xnc"][:].bitcast(F32)[:, 5632:5632 + 4 * 512]
    ps4 = [ctx.psum("w2p%d" % i, [128, 512]) for i in range(4)]
    q2sv = ctx.v("q2s")
    for c in range(4):
        ctr = emit_cast_transpose(ctx, "ct", g_d, cfm, ident_sb, n_tt=4, nd=44,
                                  src_row0=c * 512, fm_stride=512,
                                  wait_extra=[("gst", gstv), ("p2m", ctx.v("p2m"))],
                                  src_bf=True)
        for fc in range(4):
            for half in range(2):
                nds = 22
                ctx.dma("gpsimd", pw2.rearrange("p (dc f) -> p dc f", f=512),
                        wq_d[2][half * 22 * 128:(half + 1) * 22 * 128,
                                fc * 512:(fc + 1) * 512]
                        .rearrange("(dc p) f -> p dc f", p=128),
                        "p2l", waits=[("p2m", ctx.v("p2m"))])
                p2lv = ctx.v("p2l")

                def w2fn(e, c=c, fc=fc, half=half):
                    ins = None
                    for tt in range(4):
                        for d_ in range(22):
                            gcol = (half * 22 + d_) * 512 + tt * 128
                            ins = e.matmul(ps4[tt][:],
                                           cfm[:, gcol:gcol + 128],
                                           pw2[:, d_ * 512:(d_ + 1) * 512],
                                           start=(half == 0 and d_ == 0),
                                           stop=(half == 1 and d_ == 21))
                    return ins
                ctx.emit("tensor", w2fn, inc=("p2m", 1),
                         waits=[("p2l", p2lv), ctr, ("p2e", ctx.v("p2e"))])
            for tt in range(4):
                ctx.emit("scalar", lambda e, tt=tt: e.activation(
                    out=po2[:, tt * 512:(tt + 1) * 512], in_=ps4[tt][:],
                    func=AF.Identity, bias=0.0, scale=scol[:, 5:6]),
                    inc=("p2e", 1), waits=[("p2m", ctx.v("p2m")), ("q2s", q2sv),
                                           ("p2st", 16 * (c * 4 * 4 + fc * 4))])
                ctx.dma("sync", ao[c * 512 + tt * 128:c * 512 + (tt + 1) * 128,
                                  fc * 512:(fc + 1) * 512],
                        po2[:, tt * 512:(tt + 1) * 512], "p2st",
                        waits=[("p2e", ctx.v("p2e"))])
    pfin = [("p2st", ctx.v("p2st"))]
    ctx._sbufs["le_y"] = ctx._sbufs["wf"][:, 0:2048]
    ctx._sbufs["le_x"] = ctx._sbufs["wf"][:, 2048:4096]
    ctx._sbufs["le_oo"] = ctx._sbufs["wb"][:].bitcast(F32)[:, 0:2048]
    ctx._sbufs["le_junk"] = ctx._sbufs["cfm"][:].bitcast(F32)[:, 0:2048]
    lfin = emit_ln_add(ctx, "le", ao, x, x7, wait_extra=pfin)
    ctx.wait("gpsimd", [lfin])
    return ctx



_BUILD_CACHE = {}


def ceil_div(a, b):
    return -(-a // b)


def emit_cast_transpose(ctx, pre, src_dram, fm_sb, ident_sb, n_tt=NT, nd=ND,
                        src_row0=0, fm_col0=0, fm_stride=None, wait_extra=(),
                        src_bf=False):
    if fm_stride is None:
        fm_stride = n_tt * 128
    Dw = nd * 128
    xnb = ctx.sbuf(pre + "_nb", [128, Dw], BF16)
    tp = ctx.psum(pre + "_tp", [128, 1024], BF16)
    tp2 = ctx.psum(pre + "_tq", [128, 1024], BF16)
    banks = [tp, tp2]
    for tt in range(n_tt):
        ctx.dma("gpsimd" if not src_bf else "sync", xnb[:],
                src_dram[src_row0 + tt * 128:src_row0 + (tt + 1) * 128, :],
                pre + "ld", waits=[(pre + "tp", ctx.v(pre + "tp"))] + list(wait_extra))
        ldv = ctx.v(pre + "ld")
        for g in range(ceil_div(nd, 8)):
            gsz = min(8, nd - g * 8)
            gidx = ctx.v(pre + "tp")
            bank = banks[gidx % 2]
            gw = [(pre + "ld", ldv)]
            if gidx >= 2:
                gw.append((pre + "ge", gidx - 1))
            for i in range(gsz):
                d = g * 8 + i
                ctx.emit("tensor",
                         lambda e, d=d, i=i, bank=bank: e.transpose(
                             bank[:, i * 128:(i + 1) * 128],
                             xnb[:, d * 128:(d + 1) * 128], ident_sb[:]),
                         inc=(pre + "tp", 1) if i == gsz - 1 else None,
                         waits=(gw if i == 0 else []))
            tpv = ctx.v(pre + "tp")
            for i in range(gsz):
                d = g * 8 + i
                col = d * fm_stride + fm_col0 + tt * 128
                ctx.emit("vector",
                         lambda e, i=i, col=col, bank=bank: e.tensor_copy(
                             out=fm_sb[:, col:col + 128], in_=bank[:, i * 128:(i + 1) * 128]),
                         inc=(pre + "ge", 1) if i == gsz - 1 else None,
                         waits=([(pre + "tp", tpv)] if i == 0 else []))
    return (pre + "ge", ctx.v(pre + "ge"))


def build_l0c(ctx=None, io=None):
    ctx = ctx or Ctx()
    io = io or {}
    x = _io(ctx, io, "in", "xin", [TPC, D])
    rw = _io(ctx, io, "in", "router_w", [D, 8])
    rb = _io(ctx, io, "in", "router_b", [8])
    ew = _io(ctx, io, "in", "expert_w", [8, D, D])
    eb = _io(ctx, io, "in", "expert_b", [8, D])
    pw = _io(ctx, io, "in", "proj_w", [D, D])
    pb = _io(ctx, io, "in", "proj_b", [D])
    nw = _io(ctx, io, "in", "norm_w", [D])
    x2 = _io(ctx, io, "out", "X2", [TPC, D])
    xnfm_d = ctx.dram_tmp("XNFM", [D, TPC], BF16)
    xn32_d = ctx.dram_tmp("XN32", [TPC, D])
    ewb = ctx.dram_tmp("EWB", [8, D, D], BF16)
    cmb_d = ctx.dram_tmp("CMB", [TPC, D])
    mo_d = ctx.dram_tmp("MO", [TPC, D])
    ident_sb, ones_sb = setup_common(ctx)

    stg = ctx.sbuf("stg", [128, 2048], BF16)
    for i in range(128):
        e8, hh = divmod(i, 16)
        ctx.dma("gpsimd", stg[:], ew[e8, hh * 128:(hh + 1) * 128, :],
                "pc", waits=[("pcs", 16 * i)])
        ctx.dma("sync", ewb[e8, hh * 128:(hh + 1) * 128, :], stg[:],
                "pcs", waits=[("pc", 16 * (i + 1))])
    pcsv = ctx.v("pcs")

    xn_fm_t = ctx.sbuf("xn_fm_t", [128, ND * 128], BF16)
    for tt in range(NT):
        fmr1 = emit_ln_to_fm(ctx, "ln", x, xn_fm_t, ident_sb, n_tt=1, x_row0=tt * 128,
                             fm_stride=128, xn32_dram=xn32_d,
                             wait_extra=[("fmst", 16 * tt), ("lno3", 16 * tt)])
        ctx.dma("sync", xnfm_d[:, tt * 128:(tt + 1) * 128].rearrange("(dc p) f -> p dc f", p=128),
                xn_fm_t[:].rearrange("p (dc f) -> p dc f", dc=ND), "fmst", waits=[fmr1])
    fmstv = ctx.v("fmst")
    lno3v = ctx.v("lno3")

    rwsb = ctx.sbuf("rwsb", [128, ND * 8], BF16)
    rbsb = ctx.sbuf("rbsb", [1, 8], BF16)
    w8 = ctx.sbuf("w8", [128, NT * 8], F32)
    rlg = ctx.sbuf("rlg", [128, 8], F32)
    rl2 = ctx.sbuf("rl2", [128, 8], F32)
    req = ctx.sbuf("req", [128, 8], F32)
    rtmp = ctx.sbuf("rtmp", [128, 8], F32)
    rst = ctx.sbuf("rst", [128, 8], F32)
    rps = ctx.psum("rps", [128, 512])
    ctx.dma("gpsimd", rwsb[:].rearrange("p (dc f) -> p dc f", dc=ND),
            rw.rearrange("(dc p) f -> p dc f", p=128), "lrw")
    ctx.dma("gpsimd", rbsb[:], rb.rearrange("(o f) -> o f", o=1), "lrw")
    xfm_r = ctx.sbuf("xn_fm_t", [128, ND * 128], BF16)
    for tt in range(NT):
        ctx.dma("sync", xfm_r[:].rearrange("p (dc f) -> p dc f", dc=ND),
                xnfm_d[:, tt * 128:(tt + 1) * 128].rearrange("(dc p) f -> p dc f", p=128),
                "rld", waits=[("fmst", fmstv), ("rdn", ctx.v("rdn"))])

        def rfn(e, tt=tt):
            for d in range(ND):
                e.matmul(rps[:, 0:8], xfm_r[:, d * 128:(d + 1) * 128],
                         rwsb[:, d * 8:(d + 1) * 8], start=(d == 0), stop=False)
            return e.matmul(rps[:, 0:8], ones_sb[0:1, 0:128], rbsb[0:1, :],
                            start=False, stop=True)
        ctx.emit("tensor", rfn, inc=("rmm", 1),
                 waits=[("lrw", 32), ("rld", ctx.v("rld")), ("rcp", ctx.v("rcp"))])
        ctx.emit("vector", lambda e: e.tensor_copy(out=rlg[:], in_=rps[:, 0:8]),
                 inc=("rcp", 1), waits=[("rmm", ctx.v("rmm")), ("rdn", ctx.v("rdn"))])

        def step(fn, lane="vector"):
            ctx.emit(lane, fn, inc=("rdn", 1),
                     waits=[("rcp", ctx.v("rcp")), ("rdn", ctx.v("rdn"))])
        step(lambda e: e.reduce_max(out=rst[:, 0:1], in_=rlg[:], axis=AX.X))
        step(lambda e: e.tensor_scalar(req[:], rlg[:], rst[:, 0:1], None, op0=ALU.is_equal))
        step(lambda e: e.scalar_tensor_tensor(out=rl2[:], in0=req[:], scalar=-BIG, in1=rlg[:],
                                              op0=ALU.mult, op1=ALU.add))
        step(lambda e: e.reduce_max(out=rst[:, 1:2], in_=rl2[:], axis=AX.X))
        step(lambda e: e.tensor_sub(rst[:, 2:3], rst[:, 0:1], rst[:, 1:2]))
        step(lambda e: e.activation(out=rst[:, 3:4], in_=rst[:, 2:3],
                                    func=AF.Sigmoid, bias=0.0, scale=1.0), lane="scalar")
        step(lambda e: e.tensor_scalar(rst[:, 4:5], rst[:, 3:4], -1.0, 1.0,
                                       op0=ALU.mult, op1=ALU.add))
        step(lambda e: e.tensor_scalar(rtmp[:], rlg[:], rst[:, 1:2], None, op0=ALU.is_equal))
        step(lambda e: e.tensor_scalar_mul(req[:], req[:], rst[:, 3:4]))
        step(lambda e, tt=tt: e.scalar_tensor_tensor(out=w8[:, tt * 8:(tt + 1) * 8],
                                                     in0=rtmp[:], scalar=rst[:, 4:5],
                                                     in1=req[:], op0=ALU.mult, op1=ALU.add))
    w8v = ctx.v("rdn")

    CH = 512
    nch = TPC // CH
    ntt_c = CH // 128
    acc = ctx.sbuf("acc", [128, ntt_c * D], F32)
    xfm_c = ctx.sbuf("xfm_c", [128, 2 * ND * CH], BF16)
    ewsb = ctx.sbuf("ewsb", [128, 2 * ND * 512], BF16)
    ebsb = ctx.sbuf("ebsb", [1, 2 * D], BF16)
    ssb = ctx.sbuf("ssb", [128, 2 * 512], BF16)
    eps_ = [ctx.psum("ep0", [128, 512]), ctx.psum("ep1", [128, 512])]
    for c in range(nch):
        for e8 in range(8):
            li = c * 8 + e8
            par = li % 2
            ctx.dma("sync", xfm_c[:, par * ND * CH:(par + 1) * ND * CH]
                    .rearrange("p (dc f) -> p dc f", dc=ND),
                    xnfm_d[:, c * CH:(c + 1) * CH].rearrange("(dc p) f -> p dc f", p=128),
                    "xcl%d" % par,
                    waits=[("fmst", fmstv), ("emm", max(0, (li - 1) * 16))])
            ctx.dma("gpsimd", ebsb[0:1, par * D:(par + 1) * D],
                    eb[e8:e8 + 1, :], "ebl%d" % par,
                    waits=[("emm", max(0, (li - 1) * 16))])
            for fc in range(4):
                wi = li * 4 + fc
                wpar = wi % 2
                ctx.dma("gpsimd",
                        ewsb[:, wpar * ND * 512:(wpar + 1) * ND * 512]
                        .rearrange("p (dc f) -> p dc f", dc=ND),
                        ewb[e8, :, fc * 512:(fc + 1) * 512].rearrange("(dc p) f -> p dc f", p=128),
                        "ewl%d" % wpar,
                        waits=[("pcs", pcsv), ("emm", max(0, (wi - 1) * 4))])
                for tt in range(ntt_c):
                    job = wi * 4 + tt
                    bank = job % 2

                    def efn(e, par=par, wpar=wpar, fc=fc, tt=tt, bank=bank):
                        for d in range(ND):
                            e.matmul(eps_[bank][:],
                                     xfm_c[:, par * ND * CH + d * CH + tt * 128:
                                           par * ND * CH + d * CH + (tt + 1) * 128],
                                     ewsb[:, wpar * ND * 512 + d * 512:
                                          wpar * ND * 512 + (d + 1) * 512],
                                     start=(d == 0), stop=False)
                        return e.matmul(eps_[bank][:], ones_sb[0:1, 0:128],
                                        ebsb[0:1, par * D + fc * 512:par * D + (fc + 1) * 512],
                                        start=False, stop=True)
                    ctx.emit("tensor", efn, inc=("emm", 1),
                             waits=[("xcl%d" % par, 16 * (li // 2 + 1)),
                                    ("ebl%d" % par, 16 * (li // 2 + 1)),
                                    ("ewl%d" % wpar, 16 * (wi // 2 + 1)),
                                    ("esl", max(0, job - 1)), ("rdn", w8v)])
                    ctx.emit("scalar", lambda e, bank=bank: e.activation(
                        out=ssb[:, bank * 512:(bank + 1) * 512], in_=eps_[bank][:],
                        func=AF.Silu, bias=0.0, scale=1.0),
                        inc=("esl", 1), waits=[("emm", ctx.v("emm")),
                                               ("eac", max(0, job - 1))])
                    accsl = acc[:, tt * D + fc * 512: tt * D + (fc + 1) * 512]
                    wcol = w8[:, (c * ntt_c + tt) * 8 + e8:(c * ntt_c + tt) * 8 + e8 + 1]
                    if e8 == 0:
                        ctx.emit("vector", lambda e, bank=bank, accsl=accsl, wcol=wcol:
                                 e.tensor_scalar_mul(accsl, ssb[:, bank * 512:(bank + 1) * 512],
                                                     wcol),
                                 inc=("eac", 1), waits=[("esl", ctx.v("esl")),
                                                        ("cst", ctx.v("cst"))])
                    else:
                        ctx.emit("vector", lambda e, bank=bank, accsl=accsl, wcol=wcol:
                                 e.scalar_tensor_tensor(
                                     out=accsl, in0=ssb[:, bank * 512:(bank + 1) * 512],
                                     scalar=wcol, in1=accsl, op0=ALU.mult, op1=ALU.add),
                                 inc=("eac", 1), waits=[("esl", ctx.v("esl"))])
        eacv = ctx.v("eac")
        for tt in range(ntt_c):
            ctx.dma("gpsimd", cmb_d[c * CH + tt * 128:c * CH + (tt + 1) * 128, :],
                    acc[:, tt * D:(tt + 1) * D], "cst", waits=[("eac", eacv)])
    cstv = ctx.v("cst")

    cfm = ctx.sbuf("xfm_c", [128, 2 * ND * CH], BF16)
    pfin = []
    for c in range(nch):
        ctr = emit_cast_transpose(ctx, "ct", cmb_d, cfm, ident_sb, n_tt=ntt_c,
                                  src_row0=c * CH, fm_stride=CH,
                                  wait_extra=[("cst", cstv), ("pmm", ctx.v("pmm"))])
        pfin = emit_proj(ctx, "p", cfm, pw, mo_d, n_tok=CH, F=D, bias_dram=pb,
                         ones_sb=ones_sb, fm_ready=ctr, psA="ep0", psB="ep1",
                         out_stage_dtype=F32, out_row0=c * CH)

    nwb = ctx.sbuf("nwb", [128, D], F32)
    nwsb = ctx.sbuf("nwsb", [1, D], F32)
    onesfh = ctx.sbuf("onesfh", [1, 128], F32)
    onesfd = ctx.const("onesf_l0c", np.ones((1, 128), dtype=np.float32))
    ctx.dma("sync", nwsb[:], nw.rearrange("(o f) -> o f", o=1), "nwl")
    ctx.dma("sync", onesfh[:], onesfd, "nwl")
    for i in range(4):
        ctx.emit("tensor", lambda e, i=i: e.matmul(rps[:, 0:512], onesfh[:],
                                                   nwsb[0:1, i * 512:(i + 1) * 512],
                                                   start=True, stop=True),
                 inc=("nwm", 1), waits=[("nwl", 32), ("rcp", ctx.v("rcp")),
                                        ("nwe", ctx.v("nwe"))])
        ctx.emit("vector", lambda e, i=i: e.tensor_copy(out=nwb[:, i * 512:(i + 1) * 512],
                                                        in_=rps[:, 0:512]),
                 inc=("nwe", 1), waits=[("nwm", ctx.v("nwm"))])
    nwev = ctx.v("nwe")

    ya = ctx.sbuf("ln_x", [128, D], F32)
    yb = ctx.sbuf("ln_junk", [128, D], F32)
    yst = ctx.sbuf("yst", [128, 8], F32)
    yjunk = yb
    for tt in range(NT):
        w0 = [("yso", 16 * tt), ("lno3", lno3v)]
        if tt == 0:
            w0 += pfin
        ctx.dma("sync", ya[:], xn32_d[tt * 128:(tt + 1) * 128, :], "yld", waits=w0)
        ctx.dma("sync", yb[:], mo_d[tt * 128:(tt + 1) * 128, :], "yld")
        ylv = ctx.v("yld")

        def ystep(fn, lane="vector"):
            ctx.emit(lane, fn, inc=("yad", 1), waits=[("yad", ctx.v("yad"))])
        ctx.emit("vector", lambda e: e.tensor_add(ya[:], ya[:], yb[:]),
                 inc=("yad", 1), waits=[("yld", ylv), ("yad", ctx.v("yad"))])
        ystep(lambda e: e.activation(out=yjunk[:], in_=ya[:], func=AF.Square,
                                     bias=0.0, scale=1.0, accum_out=yst[:, 1:2]), "scalar")
        ystep(lambda e: e.tensor_scalar(yst[:, 3:4], yst[:, 1:2], 1.0 / D, 1e-6,
                                        op0=ALU.mult, op1=ALU.add))
        ystep(lambda e: e.activation(out=yst[:, 4:5], in_=yst[:, 3:4], func=AF.Sqrt,
                                     bias=0.0, scale=1.0), "scalar")
        ystep(lambda e: e.reciprocal(out=yst[:, 4:5], in_=yst[:, 4:5]))
        ystep(lambda e: e.tensor_scalar_mul(ya[:], ya[:], yst[:, 4:5]))
        ctx.emit("vector", lambda e: e.tensor_mul(ya[:], ya[:], nwb[:]),
                 inc=("yad", 1), waits=[("yad", ctx.v("yad")), ("nwe", nwev)])
        ystep(lambda e: e.activation(out=yjunk[:], in_=ya[:], func=AF.Square,
                                     bias=0.0, scale=1.0, accum_out=yst[:, 1:2]), "scalar")
        ystep(lambda e: e.reduce_sum(out=yst[:, 0:1], in_=ya[:], axis=AX.X))
        ystep(lambda e: e.tensor_scalar_mul(yst[:, 2:3], yst[:, 0:1], 1.0 / D))
        ystep(lambda e: e.scalar_tensor_tensor(out=yst[:, 3:4], in0=yst[:, 2:3],
                                               scalar=-1.0, in1=yst[:, 2:3],
                                               op0=ALU.mult, op1=ALU.mult))
        ystep(lambda e: e.scalar_tensor_tensor(out=yst[:, 3:4], in0=yst[:, 1:2],
                                               scalar=1.0 / D, in1=yst[:, 3:4],
                                               op0=ALU.mult, op1=ALU.add))
        ystep(lambda e: e.tensor_scalar_add(yst[:, 3:4], yst[:, 3:4], LN_EPS))
        ystep(lambda e: e.activation(out=yst[:, 4:5], in_=yst[:, 3:4], func=AF.Sqrt,
                                     bias=0.0, scale=1.0), "scalar")
        ystep(lambda e: e.reciprocal(out=yst[:, 4:5], in_=yst[:, 4:5]))
        ystep(lambda e: e.scalar_tensor_tensor(out=yst[:, 5:6], in0=yst[:, 2:3],
                                               scalar=-1.0, in1=yst[:, 4:5],
                                               op0=ALU.mult, op1=ALU.mult))
        ctx.dma("sync", yb[:], x[tt * 128:(tt + 1) * 128, :], "yxl",
                waits=[("yad", ctx.v("yad"))])
        ctx.emit("vector", lambda e: e.scalar_tensor_tensor(
            out=ya[:], in0=ya[:], scalar=yst[:, 4:5],
            in1=yst[:, 5:6].to_broadcast([128, D]), op0=ALU.mult, op1=ALU.add),
            inc=("yad", 1), waits=[("yad", ctx.v("yad"))])
        ctx.emit("vector", lambda e: e.tensor_add(ya[:], ya[:], yb[:]),
                 inc=("yad", 1), waits=[("yad", ctx.v("yad")), ("yxl", ctx.v("yxl"))])
        ctx.dma("gpsimd", x2[tt * 128:(tt + 1) * 128, :], ya[:], "yso",
                waits=[("yad", ctx.v("yad"))])
    ctx.wait("gpsimd", [("yso", ctx.v("yso"))])
    return ctx


def l0_masks():
    m = np.zeros((4, 128, 512), dtype=np.float32)
    k = np.arange(128)[:, None]
    q = np.arange(512)[None, :]
    for j in range(4):
        m[j] = np.where(k <= q - 128 * j, 0.0, -BIG)
    return m


def build_l0b(ctx=None, io=None):
    ctx = ctx or Ctx()
    io = io or {}
    qt = _io(ctx, io, "in", "QT", [D, TPC], BF16)
    kto = _io(ctx, io, "in", "KTo", [D, TPC], BF16)
    vo = _io(ctx, io, "in", "Vo", [TPC, D], BF16)
    kts = _io(ctx, io, "in", "KTs", [D, TPC], BF16)
    vs = _io(ctx, io, "in", "Vs", [TPC, D], BF16)
    wo = _io(ctx, io, "in", "wo", [D, D])
    x = _io(ctx, io, "in", "xin", [TPC, D])
    sibmask = _io(ctx, io, "in", "sibmask", [128, 1])
    masksd = ctx.const("l0masks", l0_masks())
    x1 = _io(ctx, io, "out", "X1", [TPC, D])
    ot = ctx.dram_tmp("OT", [D, TPC], BF16)
    ao = ctx.dram_tmp("AO", [TPC, D])
    ident_sb, ones_sb = setup_common(ctx)

    fl_fin = emit_l0_flash(ctx, "fl", qt, kto, kts, vo, vs, ot, sibmask, masksd)
    ot_fm = ctx.sbuf("ot_fm", [128, ND * TPC], BF16)
    for dd in range(ND):
        ctx.dma("sync", ot_fm[:, dd * TPC:(dd + 1) * TPC], ot[dd * 128:(dd + 1) * 128, :],
                "otld", waits=[fl_fin])
    fmr = ("otld", ctx.v("otld"))
    pfin = emit_proj(ctx, "wo", ot_fm, wo, ao, mode="tok", F=D, fm_ready=fmr,
                     psA="fl_psO", psB="fl_psL", out_stage_dtype=F32)
    lfin = emit_ln_add(ctx, "l2", ao, x, x1, wait_extra=pfin)
    ctx.wait("gpsimd", [lfin])
    return ctx


AG_PAIRS = [[0, 1], [2, 3], [4, 5], [6, 7]]


def build_mega():
    nc = bass.Bass("TRN2", target_bir_lowering=False, debug=False)
    mega_consts = {}

    def ein(name, shape, dtype=F32):
        return nc.dram_tensor(name, list(shape), dtype, kind="ExternalInput").ap()

    def internal(name, shape, dtype=F32):
        return nc.dram_tensor(name, list(shape), dtype, kind="Internal").ap()

    xin = ein("xin", [TPC, D])
    sibmask = ein("sibmask", [128, 1])
    grol = ein("grol", [128, 1])
    s0z = ein("S0z", [6, 256, 512])
    msk = ein("l2masks", [16, 128, 648], BF16)
    qcos = ein("qcos", [128, TPC])
    qsin = ein("qsin", [128, TPC])
    kcos = ein("kcos", [128, 2688])
    ksin = ein("ksin", [128, 2688])
    wn = {}
    for name, shape in [
            ("l0_wq", [D, D]), ("l0_wk", [D, D]), ("l0_wv", [D, D]), ("l0_wo", [D, D]),
            ("l0f_rw", [D, 8]), ("l0f_rb", [8]), ("l0f_ew", [8, D, D]),
            ("l0f_eb", [8, D]), ("l0f_pw", [D, D]), ("l0f_pb", [D]), ("l0f_nw", [D]),
            ("l1_wq", [D, 1536]), ("l1_wk0", [D, 1536]), ("l1_wk1", [D, 1536]),
            ("l1_wv0", [D, 3072]), ("l1_wv1", [D, 3072]), ("l1_wbab", [D, 18]),
            ("l1_dtb18", [18]), ("l1_wg", [D, 3072]), ("l1_cqw", [1536, 4]),
            ("l1_ckw0", [1536, 4]), ("l1_ckw1", [1536, 4]), ("l1_cvw0", [3072, 4]),
            ("l1_cvw1", [3072, 4]), ("l1_alog", [6]), ("l1_onw", [512]),
            ("l1_wo", [3072, D]), ("l1f_wt", [D, D]), ("l1f_sw", [D]), ("l1f_bias", [D]),
            ("l2_wq", [D, D]), ("l2_bq", [D]), ("l2_wk", [D, 512]), ("l2_bk", [512]),
            ("l2_wv", [D, 512]), ("l2_bv", [512]), ("l2_wo", [D, D]), ("l2_bo", [D]),
            ("l2f_wct", [D, 4096]), ("l2f_tau", [1]), ("l2f_genes", [4096, 256]),
            ("l2f_wdt", [256, D]), ("l2f_wut", [256, D]), ("l2f_nw", [D]),
            ("l2f_scl", [1]), ("l3_wq", [D, D]), ("l3_bq", [D]), ("l3_wk", [D, D]),
            ("l3_bk", [D]), ("l3_wv", [D, D]), ("l3_bv", [D]), ("l3_wo", [D, D]),
            ("l3_bo", [D]), ("l3f_w1t", [D, 5632]), ("l3f_w3t", [D, 5632]),
            ("l3f_w2tt", [5632, D])]:
        wn[name] = ein(name, shape)

    QT = internal("QT", [D, TPC], BF16)
    KT = internal("KT", [D, TPC], BF16)
    V = internal("V", [TPC, D], BF16)
    KTg = internal("KTg", [2 * D, TPC], BF16)
    Vg = internal("Vg", [2 * TPC, D], BF16)
    X1 = internal("X1", [TPC, D])
    X2 = internal("X2", [TPC, D])
    XBg = internal("XBg", [6, D])
    xbnd = internal("xbnd", [128, D])
    QFN = internal("QFN", [1536, TPC], BF16)
    KFN0 = internal("KFN0", [1536, TPC], BF16)
    KFN1 = internal("KFN1", [1536, TPC], BF16)
    V0T = internal("V0T", [TPC, 3072], BF16)
    V1T = internal("V1T", [TPC, 3072], BF16)
    B0 = internal("B0", [TPC, 6])
    B1 = internal("B1", [TPC, 6])
    G = internal("G", [TPC, 6])
    GG = internal("GG", [TPC, 3072])
    OD1 = internal("OD1", [TPC, 3072])
    SF1 = internal("SF1", [6, 256, 512])
    SFg = internal("SFg", [12, 256, 512])
    S0b = internal("S0b", [6, 256, 512])
    OD2 = internal("OD2", [TPC, 3072])
    SF2 = internal("SF2", [6, 256, 512])
    X3 = internal("X3", [TPC, D])
    X3b = internal("X3b", [TPC, D])
    XSa = internal("XSa", [8, D])
    XSb = internal("XSb", [1024, D])
    xsib = internal("xsib", [640, D])
    X4 = internal("X4", [TPC, D])
    X5 = internal("X5", [TPC, D])
    PHIQ = internal("PHIQ", [TPC, D])
    KVC = internal("KVC", [TPC, D])
    CARRY = internal("CARRY", [1, D])
    Cg = internal("Cg", [2, D])
    SCAR = internal("SCAR", [1, D])
    X6 = internal("X6", [TPC, D])
    X7 = nc.dram_tensor("X7", [TPC, D], F32, kind="ExternalOutput").ap()

    _ninstr = [0]

    def section(tag, fn):
        with nc.cleanup_on_exit():
            ctx = Ctx(nc=nc, tag=tag + "_")
            fn(ctx)
            ctx.wait("gpsimd", [(s, v) for s, v in sorted(ctx.semv.items()) if v > 0])
            ctx.finalize(close=False)
            nc.all_engine_barrier()
            mega_consts.update(ctx.consts)
            _ninstr[0] += sum(len(v) for v in ctx.lanes.values())

    def ag(tag, pairs):
        def fn(ctx):
            n = 0
            for (src, dst) in pairs:
                ctx.emit("gpsimd", lambda e, src=src, dst=dst: e.collective_compute(
                    "AllGather", ALU.bypass, replica_groups=AG_PAIRS,
                    ins=[src], outs=[dst]),
                    inc=("cd", 1), waits=[])
                n += 1
            ctx.wait("gpsimd", [("cd", n)])
        section(tag, fn)

    section("s00", lambda ctx: build_l0a(ctx=ctx, io={
        "xin": xin, "wq": wn["l0_wq"], "wk": wn["l0_wk"], "wv": wn["l0_wv"],
        "QT": QT, "KT": KT, "V": V}))
    ag("s01", [(KT, KTg), (V, Vg)])
    section("s02", lambda ctx: build_l0b(ctx=ctx, io={
        "QT": QT, "KTo": KT, "Vo": V, "KTs": KTg[0:D, :], "Vs": Vg[0:TPC, :],
        "wo": wn["l0_wo"], "xin": xin, "sibmask": sibmask, "X1": X1}))
    section("s03", lambda ctx: build_l0c(ctx=ctx, io={
        "xin": X1, "router_w": wn["l0f_rw"], "router_b": wn["l0f_rb"],
        "expert_w": wn["l0f_ew"], "expert_b": wn["l0f_eb"], "proj_w": wn["l0f_pw"],
        "proj_b": wn["l0f_pb"], "norm_w": wn["l0f_nw"], "X2": X2}))
    ag("s04", [(X2[TPC - 3:TPC, :], XBg)])

    def glue_xbnd(ctx):
        xb = ctx.sbuf("xb", [128, D], F32)
        ctx.emit("vector", lambda e: e.memset(xb[:], 0.0), inc=("ms", 1), waits=[])
        ctx.dma("sync", xb[125:128, :], XBg[0:3, :], "ld", waits=[("ms", 1)])
        gb = ctx.sbuf("gb", [128, 1], F32)
        ctx.dma("sync", gb[:], grol, "lg")
        ctx.emit("vector", lambda e: e.tensor_scalar_mul(xb[:], xb[:], gb[:, 0:1]),
                 inc=("mu", 1), waits=[("ld", 16), ("lg", 16)])
        ctx.dma("gpsimd", xbnd, xb[:], "st", waits=[("mu", 1)])
        ctx.wait("gpsimd", [("st", 16)])
    section("s05", glue_xbnd)
    section("s06", lambda ctx: build_l1a(ctx=ctx, io={
        "xin": X2, "xbnd": xbnd, "wq": wn["l1_wq"], "wk0": wn["l1_wk0"],
        "wk1": wn["l1_wk1"], "wv0": wn["l1_wv0"], "wv1": wn["l1_wv1"],
        "wbab": wn["l1_wbab"], "dtb18": wn["l1_dtb18"], "wg": wn["l1_wg"],
        "cqw": wn["l1_cqw"], "ckw0": wn["l1_ckw0"], "ckw1": wn["l1_ckw1"],
        "cvw0": wn["l1_cvw0"], "cvw1": wn["l1_cvw1"], "a_log": wn["l1_alog"],
        "QFN": QFN, "KFN0": KFN0, "KFN1": KFN1, "V0T": V0T, "V1T": V1T,
        "B0": B0, "B1": B1, "G": G, "GG": GG}))
    l1b_io = {"QFN": QFN, "KFN0": KFN0, "KFN1": KFN1, "V0T": V0T, "V1T": V1T,
              "B0": B0, "B1": B1, "G": G}
    section("s07", lambda ctx: build_l1b(ctx=ctx, io=dict(
        l1b_io, S0=s0z, OD=OD1, SF=SF1)))
    ag("s08", [(SF1, SFg)])

    def glue_s0(ctx):
        buf = ctx.sbuf("sb", [128, 512], F32)
        gb = ctx.sbuf("gb", [128, 1], F32)
        ctx.dma("sync", gb[:], grol, "lg")
        src = SFg[0:6].rearrange("h k v -> (h k) v")
        dst = S0b.rearrange("h k v -> (h k) v")
        for i in range(12):
            ctx.dma("sync", buf[:], src[i * 128:(i + 1) * 128, :], "ld",
                    waits=[("mu", ctx.v("mu")), ("st", 16 * i)])
            ctx.emit("vector", lambda e: e.tensor_scalar_mul(buf[:], buf[:], gb[:, 0:1]),
                     inc=("mu", 1), waits=[("ld", ctx.v("ld")), ("lg", 16)])
            ctx.dma("gpsimd", dst[i * 128:(i + 1) * 128, :], buf[:], "st",
                    waits=[("mu", ctx.v("mu"))])
        ctx.wait("gpsimd", [("st", 16 * 12)])
    section("s09", glue_s0)
    section("s10", lambda ctx: build_l1b(ctx=ctx, io=dict(
        l1b_io, S0=S0b, OD=OD2, SF=SF2)))
    section("s11", lambda ctx: build_l1c(ctx=ctx, io={
        "OD": OD2, "GG": GG, "onw": wn["l1_onw"], "wo": wn["l1_wo"],
        "xin": X2, "X3": X3}))
    section("s12", lambda ctx: build_l1d(ctx=ctx, io={
        "xin": X3, "wt": wn["l1f_wt"], "sw": wn["l1f_sw"], "bias": wn["l1f_bias"],
        "XO": X3b}))
    ag("s13", [(X3b[0:4, :], XSa), (X3b[TPC - 512:TPC, :], XSb)])

    def glue_xsib(ctx):
        zb = ctx.sbuf("zb", [128, D], F32)
        ctx.emit("vector", lambda e: e.memset(zb[:], 0.0), inc=("ms", 1), waits=[])
        ctx.dma("gpsimd", xsib[4:128, :], zb[0:124, :], "st", waits=[("ms", 1)])
        ctx.dma("sync", xsib[0:4, :], XSa[0:4, :], "st2")
        ctx.dma("sync", xsib[128:640, :], XSb[0:512, :], "st2")
        ctx.wait("gpsimd", [("st", 16), ("st2", 32)])
    section("s14", glue_xsib)
    section("s15", lambda ctx: build_l2a(ctx=ctx, io={
        "xin": X3b, "xsib": xsib, "wq": wn["l2_wq"], "bq": wn["l2_bq"],
        "wk": wn["l2_wk"], "bk": wn["l2_bk"], "wv": wn["l2_wv"], "bv": wn["l2_bv"],
        "wo": wn["l2_wo"], "bo": wn["l2_bo"], "l2masks": msk, "qcos": qcos,
        "qsin": qsin, "kcos": kcos, "ksin": ksin, "X4": X4}))
    section("s16", lambda ctx: build_l2c(ctx=ctx, io={
        "xin": X4, "wct": wn["l2f_wct"], "tau": wn["l2f_tau"],
        "genes": wn["l2f_genes"], "wdt": wn["l2f_wdt"], "wut": wn["l2f_wut"],
        "norm_w": wn["l2f_nw"], "scl": wn["l2f_scl"], "X5": X5}))
    section("s17", lambda ctx: build_l3a(ctx=ctx, io={
        "xin": X5, "wq": wn["l3_wq"], "bq": wn["l3_bq"], "wk": wn["l3_wk"],
        "bk": wn["l3_bk"], "wv": wn["l3_wv"], "bv": wn["l3_bv"],
        "PHIQ": PHIQ, "KVC": KVC, "CARRY": CARRY}))
    ag("s18", [(CARRY, Cg)])

    def glue_scar(ctx):
        row = ctx.sbuf("row", [1, D], F32)
        gb = ctx.sbuf("gb", [128, 1], F32)
        ctx.dma("sync", gb[:], grol, "lg")
        ctx.dma("sync", row[:], Cg[0:1, :], "lg")
        ctx.emit("vector", lambda e: e.tensor_scalar_mul(row[:], row[:], gb[0:1, 0:1]),
                 inc=("mu", 1), waits=[("lg", 32)])
        ctx.dma("gpsimd", SCAR, row[:], "st", waits=[("mu", 1)])
        ctx.wait("gpsimd", [("st", 16)])
    section("s19", glue_scar)
    section("s20", lambda ctx: build_l3b(ctx=ctx, io={
        "xin": X5, "PHIQ": PHIQ, "KVC": KVC, "SCAR": SCAR, "wo": wn["l3_wo"],
        "bo": wn["l3_bo"], "X6": X6}))
    section("s21", lambda ctx: build_l3c(ctx=ctx, io={
        "xin": X6, "w1t": wn["l3f_w1t"], "w3t": wn["l3f_w3t"],
        "w2tt": wn["l3f_w2tt"], "X7": X7}))
    print("[mega] total emitted thunks:", _ninstr[0])
    return nc, mega_consts


def _get(name, builder):
    if name not in _BUILD_CACHE:
        ctx = builder()
        nc = ctx.finalize()
        _BUILD_CACHE[name] = (nc, ctx.consts)
    return _BUILD_CACHE[name]


def _run(name, builder, per_core_inputs):
    nc, consts = _get(name, builder)
    in_maps = []
    for c in range(NCORES):
        m = dict(per_core_inputs[c])
        m.update(consts)
        in_maps.append(m)
    res = run_bass_kernel_spmd(nc, in_maps, core_ids=list(range(NCORES)))
    return res.results


_STAGE_DUMP = None


def _dump(name, cores, X):
    if _STAGE_DUMP is None:
        return
    full = np.empty((4, S, D), np.float32)
    for c, (b, r) in enumerate(cores):
        full[b, r * TPC:(r + 1) * TPC] = np.asarray(X[c], np.float32)
    _STAGE_DUMP[name] = full


def _run_mega(per_core_inputs):
    if "mega" not in _BUILD_CACHE:
        _BUILD_CACHE["mega"] = build_mega()
    nc, consts = _BUILD_CACHE["mega"]
    in_maps = []
    for c in range(NCORES):
        m = dict(per_core_inputs[c])
        m.update(consts)
        in_maps.append(m)
    return run_bass_kernel_spmd(nc, in_maps, core_ids=list(range(NCORES))).results


_FUSED = True


def kernel(**inputs):
    if not _FUSED:
        return _kernel_unfused(**inputs)
    pc, cores = _mega_inmaps(inputs)
    rr = _run_mega(pc)
    out = np.empty((4, S, D), np.float32)
    for c, (b, r) in enumerate(cores):
        out[b, r * TPC:(r + 1) * TPC] = rr[c]["X7"]
    return out


def _mega_inmaps(inputs):
    f32 = np.float32
    ct = np.ascontiguousarray
    x = np.asarray(inputs["L0_attn_x"], f32)
    cores = [(c // 2, c % 2) for c in range(NCORES)]
    xs = [ct(x[b, r * TPC:(r + 1) * TPC]) for b, r in cores]
    Wb = inputs["L1_attn_Wb"]
    HP = 5632
    w1 = np.asarray(inputs["L3_ffn_w1"], f32)
    w3 = np.asarray(inputs["L3_ffn_w3"], f32)
    w2 = np.asarray(inputs["L3_ffn_w2"], f32)
    wmap = {
        "l0_wq": inputs["L0_attn_w_q"], "l0_wk": inputs["L0_attn_w_k"],
        "l0_wv": inputs["L0_attn_w_v"], "l0_wo": inputs["L0_attn_w_o"],
        "l0f_rw": inputs["L0_ffn_router_w"], "l0f_rb": inputs["L0_ffn_router_b"],
        "l0f_ew": inputs["L0_ffn_expert_w"], "l0f_eb": inputs["L0_ffn_expert_b"],
        "l0f_pw": inputs["L0_ffn_proj_w"], "l0f_pb": inputs["L0_ffn_proj_b"],
        "l0f_nw": inputs["L0_ffn_norm_w"],
        "l1_wq": inputs["L1_attn_Wq"],
        "l1_wk0": inputs["L1_attn_Wk"][0], "l1_wk1": inputs["L1_attn_Wk"][1],
        "l1_wv0": inputs["L1_attn_Wv"][0], "l1_wv1": inputs["L1_attn_Wv"][1],
        "l1_wbab": ct(np.concatenate([Wb[0], Wb[1], inputs["L1_attn_Wa"]], axis=1)),
        "l1_dtb18": ct(np.concatenate([np.zeros(12, f32),
                                       np.asarray(inputs["L1_attn_dt_bias"], f32)])),
        "l1_wg": inputs["L1_attn_Wg"], "l1_cqw": inputs["L1_attn_conv_q_w"],
        "l1_ckw0": inputs["L1_attn_conv_k_w"][0],
        "l1_ckw1": inputs["L1_attn_conv_k_w"][1],
        "l1_cvw0": inputs["L1_attn_conv_v_w"][0],
        "l1_cvw1": inputs["L1_attn_conv_v_w"][1],
        "l1_alog": inputs["L1_attn_A_log"], "l1_onw": inputs["L1_attn_o_norm_w"],
        "l1_wo": inputs["L1_attn_Wo"],
        "l1f_wt": ct(np.asarray(inputs["L1_ffn_weight"], f32).T),
        "l1f_sw": inputs["L1_ffn_scale_weight"], "l1f_bias": inputs["L1_ffn_bias"],
        "l2_wq": inputs["L2_attn_Wq"], "l2_bq": inputs["L2_attn_bq"],
        "l2_wk": inputs["L2_attn_Wk"], "l2_bk": inputs["L2_attn_bk"],
        "l2_wv": inputs["L2_attn_Wv"], "l2_bv": inputs["L2_attn_bv"],
        "l2_wo": inputs["L2_attn_Wo"], "l2_bo": inputs["L2_attn_bo"],
        "l2f_wct": ct(np.asarray(inputs["L2_ffn_Wc"], f32).T),
        "l2f_tau": np.asarray(inputs["L2_ffn_temperature"], f32).reshape(1),
        "l2f_genes": inputs["L2_ffn_genes"],
        "l2f_wdt": ct(np.asarray(inputs["L2_ffn_Wd"], f32).T),
        "l2f_wut": ct(np.asarray(inputs["L2_ffn_Wu"], f32).T),
        "l2f_nw": inputs["L2_ffn_norm_w"],
        "l2f_scl": np.asarray(inputs["L2_ffn_scale"], f32).reshape(1),
        "l3_wq": ct(np.asarray(inputs["L3_attn_Wq"], f32).T),
        "l3_bq": inputs["L3_attn_bq"],
        "l3_wk": ct(np.asarray(inputs["L3_attn_Wk"], f32).T),
        "l3_bk": inputs["L3_attn_bk"],
        "l3_wv": ct(np.asarray(inputs["L3_attn_Wv"], f32).T),
        "l3_bv": inputs["L3_attn_bv"],
        "l3_wo": ct(np.asarray(inputs["L3_attn_Wo"], f32).T),
        "l3_bo": inputs["L3_attn_bo"],
        "l3f_w1t": ct(np.pad(w1.T, ((0, 0), (0, HP - w1.shape[0])))),
        "l3f_w3t": ct(np.pad(w3.T, ((0, 0), (0, HP - w3.shape[0])))),
        "l3f_w2tt": ct(np.pad(w2.T, ((0, HP - w2.shape[1]), (0, 0)))),
    }
    rope_q = [l2_rope_tables(r * TPC + np.arange(TPC)) for r in range(2)]
    rope_k = [l2_rope_tables(l2_klocal_pos(r)) for r in range(2)]
    msks = [l2_masks(r) for r in range(2)]
    sm0 = np.full((128, 1), -BIG, f32)
    sm1 = np.zeros((128, 1), f32)
    ZS = np.zeros((6, 256, 512), f32)
    pc = []
    for c, (b, r) in enumerate(cores):
        m = dict(wmap)
        m.update({"xin": xs[c], "sibmask": sm1 if r else sm0,
                  "grol": np.full((128, 1), float(r), f32), "S0z": ZS,
                  "l2masks": msks[r], "qcos": rope_q[r][0], "qsin": rope_q[r][1],
                  "kcos": rope_k[r][0], "ksin": rope_k[r][1]})
        pc.append(m)
    return pc, cores


def _kernel_unfused(**inputs):
    f32 = np.float32
    bfd = ml_dtypes.bfloat16
    ct = np.ascontiguousarray
    x = np.asarray(inputs["L0_attn_x"], f32)
    cores = [(c // 2, c % 2) for c in range(NCORES)]
    xs = [ct(x[b, r * TPC:(r + 1) * TPC]) for b, r in cores]

    pc = [{"xin": xs[c], "wq": inputs["L0_attn_w_q"], "wk": inputs["L0_attn_w_k"],
           "wv": inputs["L0_attn_w_v"]} for c in range(NCORES)]
    r0 = _run("l0a", build_l0a, pc)
    Z2 = np.zeros((D, TPC), bfd)
    Z3 = np.zeros((TPC, D), bfd)
    sm0 = np.full((128, 1), -BIG, f32)
    sm1 = np.zeros((128, 1), f32)
    pc = []
    for c, (b, r) in enumerate(cores):
        sib = 2 * b
        pc.append({"QT": r0[c]["QT"], "KTo": r0[c]["KT"], "Vo": r0[c]["V"],
                   "KTs": r0[sib]["KT"] if r else Z2,
                   "Vs": r0[sib]["V"] if r else Z3,
                   "wo": inputs["L0_attn_w_o"], "xin": xs[c],
                   "sibmask": sm1 if r else sm0})
    rr = _run("l0b", build_l0b, pc)
    X = [rr[c]["X1"] for c in range(NCORES)]
    _dump("x1", cores, X)

    pc = [{"xin": X[c], "router_w": inputs["L0_ffn_router_w"],
           "router_b": inputs["L0_ffn_router_b"],
           "expert_w": inputs["L0_ffn_expert_w"],
           "expert_b": inputs["L0_ffn_expert_b"],
           "proj_w": inputs["L0_ffn_proj_w"], "proj_b": inputs["L0_ffn_proj_b"],
           "norm_w": inputs["L0_ffn_norm_w"]} for c in range(NCORES)]
    rr = _run("l0c", build_l0c, pc)
    X = [rr[c]["X2"] for c in range(NCORES)]
    _dump("x2", cores, X)

    Wb = inputs["L1_attn_Wb"]
    wbab = ct(np.concatenate([Wb[0], Wb[1], inputs["L1_attn_Wa"]], axis=1))
    dtb18 = ct(np.concatenate([np.zeros(12, f32),
                               np.asarray(inputs["L1_attn_dt_bias"], f32)]))
    pc = []
    for c, (b, r) in enumerate(cores):
        xbnd = np.zeros((128, D), f32)
        if r:
            xbnd[125:128] = X[2 * b][TPC - 3:TPC]
        pc.append({"xin": X[c], "xbnd": xbnd, "wq": inputs["L1_attn_Wq"],
                   "wk0": inputs["L1_attn_Wk"][0], "wk1": inputs["L1_attn_Wk"][1],
                   "wv0": inputs["L1_attn_Wv"][0], "wv1": inputs["L1_attn_Wv"][1],
                   "wbab": wbab, "dtb18": dtb18, "wg": inputs["L1_attn_Wg"],
                   "cqw": inputs["L1_attn_conv_q_w"],
                   "ckw0": inputs["L1_attn_conv_k_w"][0],
                   "ckw1": inputs["L1_attn_conv_k_w"][1],
                   "cvw0": inputs["L1_attn_conv_v_w"][0],
                   "cvw1": inputs["L1_attn_conv_v_w"][1],
                   "a_log": inputs["L1_attn_A_log"]})
    r1a = _run("l1a", build_l1a, pc)
    ZS = np.zeros((6, 256, 512), f32)
    pc = [{"QFN": r1a[c]["QFN"], "KFN0": r1a[c]["KFN0"], "KFN1": r1a[c]["KFN1"],
           "V0T": r1a[c]["V0T"], "V1T": r1a[c]["V1T"], "B0": r1a[c]["B0"],
           "B1": r1a[c]["B1"], "G": r1a[c]["G"], "S0": ZS} for c in range(NCORES)]
    r1b = _run("l1b", build_l1b, pc)
    pc2 = []
    for c, (b, r) in enumerate(cores):
        m = dict(pc[c])
        m["S0"] = r1b[2 * b]["SF"] if r else ZS
        pc2.append(m)
    r1b = _run("l1b", build_l1b, pc2)
    pc = [{"OD": r1b[c]["OD"], "GG": r1a[c]["GG"],
           "onw": inputs["L1_attn_o_norm_w"], "wo": inputs["L1_attn_Wo"],
           "xin": X[c]} for c in range(NCORES)]
    rr = _run("l1c", build_l1c, pc)
    X = [rr[c]["X3"] for c in range(NCORES)]
    _dump("x3", cores, X)

    w1t = ct(np.asarray(inputs["L1_ffn_weight"], f32).T)
    pc = [{"xin": X[c], "wt": w1t, "sw": inputs["L1_ffn_scale_weight"],
           "bias": inputs["L1_ffn_bias"]} for c in range(NCORES)]
    rr = _run("l1d", build_l1d, pc)
    X = [rr[c]["XO"] for c in range(NCORES)]
    _dump("x3b", cores, X)

    rope_q = [l2_rope_tables(r * TPC + np.arange(TPC)) for r in range(2)]
    rope_k = [l2_rope_tables(l2_klocal_pos(r)) for r in range(2)]
    msks = [l2_masks(r) for r in range(2)]
    pc = []
    for c, (b, r) in enumerate(cores):
        xsib = np.zeros((640, D), f32)
        xsib[0:4] = X[2 * b][0:4]
        if r:
            xsib[128:640] = X[2 * b][1536:2048]
        pc.append({"xin": X[c], "xsib": xsib, "wq": inputs["L2_attn_Wq"],
                   "bq": inputs["L2_attn_bq"], "wk": inputs["L2_attn_Wk"],
                   "bk": inputs["L2_attn_bk"], "wv": inputs["L2_attn_Wv"],
                   "bv": inputs["L2_attn_bv"], "wo": inputs["L2_attn_Wo"],
                   "bo": inputs["L2_attn_bo"], "l2masks": msks[r],
                   "qcos": rope_q[r][0], "qsin": rope_q[r][1],
                   "kcos": rope_k[r][0], "ksin": rope_k[r][1]})
    rr = _run("l2a", build_l2a, pc)
    X = [rr[c]["X4"] for c in range(NCORES)]
    _dump("x4", cores, X)

    tau = np.asarray(inputs["L2_ffn_temperature"], f32).reshape(1)
    scl = np.asarray(inputs["L2_ffn_scale"], f32).reshape(1)
    wct = ct(np.asarray(inputs["L2_ffn_Wc"], f32).T)
    wdt = ct(np.asarray(inputs["L2_ffn_Wd"], f32).T)
    wut = ct(np.asarray(inputs["L2_ffn_Wu"], f32).T)
    pc = [{"xin": X[c], "wct": wct, "tau": tau, "genes": inputs["L2_ffn_genes"],
           "wdt": wdt, "wut": wut, "norm_w": inputs["L2_ffn_norm_w"],
           "scl": scl} for c in range(NCORES)]
    rr = _run("l2c", build_l2c, pc)
    X = [rr[c]["X5"] for c in range(NCORES)]
    _dump("x5", cores, X)

    wqT = ct(np.asarray(inputs["L3_attn_Wq"], f32).T)
    wkT = ct(np.asarray(inputs["L3_attn_Wk"], f32).T)
    wvT = ct(np.asarray(inputs["L3_attn_Wv"], f32).T)
    woT = ct(np.asarray(inputs["L3_attn_Wo"], f32).T)
    pc = [{"xin": X[c], "wq": wqT, "bq": inputs["L3_attn_bq"],
           "wk": wkT, "bk": inputs["L3_attn_bk"],
           "wv": wvT, "bv": inputs["L3_attn_bv"]}
          for c in range(NCORES)]
    r3a = _run("l3a", build_l3a, pc)
    Z1 = np.zeros((1, D), f32)
    pc = []
    for c, (b, r) in enumerate(cores):
        pc.append({"xin": X[c], "PHIQ": r3a[c]["PHIQ"], "KVC": r3a[c]["KVC"],
                   "SCAR": r3a[2 * b]["CARRY"] if r else Z1,
                   "wo": woT, "bo": inputs["L3_attn_bo"]})
    rr = _run("l3b", build_l3b, pc)
    X = [rr[c]["X6"] for c in range(NCORES)]
    _dump("x6", cores, X)

    HP = 5632
    w1 = np.asarray(inputs["L3_ffn_w1"], f32)
    w3 = np.asarray(inputs["L3_ffn_w3"], f32)
    w2 = np.asarray(inputs["L3_ffn_w2"], f32)
    w1t = ct(np.pad(w1.T, ((0, 0), (0, HP - w1.shape[0]))))
    w3t = ct(np.pad(w3.T, ((0, 0), (0, HP - w3.shape[0]))))
    w2tt = ct(np.pad(w2.T, ((0, HP - w2.shape[1]), (0, 0))))
    pc = [{"xin": X[c], "w1t": w1t, "w3t": w3t, "w2tt": w2tt}
          for c in range(NCORES)]
    rr = _run("l3c", build_l3c, pc)

    out = np.empty((4, S, D), f32)
    for c, (b, r) in enumerate(cores):
        out[b, r * TPC:(r + 1) * TPC] = rr[c]["X7"]
    return out


# version 04
# speedup vs baseline: 1.0306x; 1.0306x over previous
import math
from contextlib import ExitStack

import numpy as np
import ml_dtypes
import concourse.bass as bass
import concourse.mybir as mybir
from concourse.bass_utils import run_bass_kernel_spmd

F32 = mybir.dt.float32
BF16 = mybir.dt.bfloat16
AF = mybir.ActivationFunctionType
ALU = mybir.AluOpType
AX = mybir.AxisListType

NCORES = 8
B, S, D = 4, 4096, 2048
TPC = 2048
NT = TPC // 128
ND = D // 128
LN_EPS = 1e-5
BIG = 30000.0

ENGINES = ("sync", "tensor", "vector", "scalar", "gpsimd")


class Ctx:

    def __init__(self, nc=None, tag=""):
        self.nc = nc or bass.Bass("TRN2", target_bir_lowering=False, debug=False)
        self.tag = tag
        self.stack = ExitStack()
        self.lanes = {e: [] for e in ENGINES}
        self.semv = {}
        self._sems = {}
        self._sbufs = {}
        self._psums = {}
        self.consts = {}

    def sem(self, name):
        if name not in self._sems:
            self._sems[name] = self.stack.enter_context(self.nc.semaphore(self.tag + name))
            self.semv[name] = 0
        return self._sems[name]

    def sbuf(self, name, shape, dtype):
        if name not in self._sbufs:
            self._sbufs[name] = self.stack.enter_context(
                self.nc.sbuf_tensor(self.tag + name, list(shape), dtype))
        return self._sbufs[name]

    def psum(self, name, shape, dtype=F32):
        if name not in self._psums:
            self._psums[name] = self.stack.enter_context(
                self.nc.psum_tensor(self.tag + name, list(shape), dtype))
        return self._psums[name]

    def dram_in(self, name, shape, dtype=F32):
        return self.nc.dram_tensor(name, list(shape), dtype, kind="ExternalInput").ap()

    def dram_out(self, name, shape, dtype=F32):
        return self.nc.dram_tensor(name, list(shape), dtype, kind="ExternalOutput").ap()

    def dram_tmp(self, name, shape, dtype=F32):
        return self.nc.dram_tensor(self.tag + name, list(shape), dtype, kind="Internal").ap()

    def const(self, name, arr):
        arr = np.ascontiguousarray(arr)
        reg = getattr(self.nc, "_const_reg", None)
        if reg is None:
            reg = self.nc._const_reg = {}
        self.consts[name] = arr
        if name in reg:
            ap, old = reg[name]
            assert old.shape == arr.shape and old.dtype == arr.dtype \
                and np.array_equal(old, arr), name
            return ap
        dt = {np.dtype(np.float32): F32, np.dtype(np.int32): mybir.dt.int32,
              np.dtype(ml_dtypes.bfloat16): BF16}[arr.dtype]
        ap = self.nc.dram_tensor(name, list(arr.shape), dt, kind="ExternalInput").ap()
        reg[name] = (ap, arr)
        return ap

    def emit(self, lane, fn, inc=None, waits=()):
        waits = tuple(waits)
        for sname, _ in waits:
            self.sem(sname)
        if inc is not None:
            self.sem(inc[0])
            self.semv[inc[0]] += inc[1]

        def thunk(eng, _fn=fn, _waits=waits, _inc=inc):
            for sname, v in _waits:
                if v > 0:
                    eng.wait_ge(self._sems[sname], v)
            ins = _fn(eng)
            if _inc is not None:
                assert ins is not None
                ins.then_inc(self._sems[_inc[0]], _inc[1])

        self.lanes[lane].append(thunk)

    def v(self, sname):
        return self.semv.get(sname, 0)

    def dma(self, lane, out, in_, inc, waits=(), **kw):
        self.emit(lane, lambda e: e.dma_start(out=out, in_=in_, **kw),
                  inc=(inc, 16), waits=waits)

    def wait(self, lane, waits):
        self.emit(lane, lambda e: None, waits=waits)

    def barrier(self, name):
        for ln in ENGINES:
            if self.lanes[ln]:
                pass
        raise NotImplementedError

    def finalize(self, close=True):
        with self.nc.Block() as blk:
            for lane in ENGINES:
                thunks = self.lanes[lane]
                getattr(blk, lane)(lambda eng, _t=tuple(thunks): [t(eng) for t in _t])
        if close:
            self.stack.close()
        return self.nc



def _io(ctx, io, kind, n, s, d=F32):
    if io and n in io:
        return io[n]
    f = {"in": ctx.dram_in, "out": ctx.dram_out}[kind]
    return f(n, s, d)


def emit_ln_to_fm(ctx, pre, x_dram, xn_fm_sb, ident_sb, n_tt=NT, nd=ND,
                  eps=LN_EPS, rms=False, extra_scale=None,
                  xn_bf_dram=None, xn32_dram=None, fm_col0=0, fm_stride=None,
                  x_row0=0, wait_extra=()):
    if fm_stride is None:
        fm_stride = n_tt * 128
    Dw = nd * 128
    xsb = ctx.sbuf(pre + "_x", [128, Dw], F32)
    xnb = ctx.sbuf(pre + "_nb", [128, Dw], BF16)
    junk = ctx.sbuf(pre + "_junk", [128, Dw], F32)
    st = ctx.sbuf(pre + "_st", [128, 8], F32)
    tp = ctx.psum(pre + "_tp", [128, 1024], BF16)
    ob0, o30 = ctx.v(pre + "ob"), ctx.v(pre + "o3")

    for tt in range(n_tt):
        ctx.dma("sync", xsb[:], x_dram[x_row0 + tt * 128:x_row0 + (tt + 1) * 128, :],
                pre + "ld", waits=[(pre + "ev", ctx.v(pre + "ev"))] + list(wait_extra))
        ldv = ctx.v(pre + "ld")
        sqw = [(pre + "ld", ldv)]
        if xn32_dram is not None and tt > 0:
            sqw.append((pre + "o3", o30 + 16 * tt))
        ctx.emit("scalar", lambda e: e.activation(out=junk[:], in_=xsb[:], func=AF.Square,
                                                  bias=0.0, scale=1.0, accum_out=st[:, 1:2]),
                 inc=(pre + "sq", 1), waits=sqw)
        if not rms:
            ctx.emit("vector", lambda e: e.reduce_sum(out=st[:, 0:1], in_=xsb[:], axis=AX.X),
                     inc=(pre + "sq", 1), waits=[(pre + "ld", ldv)])
        sqv = ctx.v(pre + "sq")
        if not rms:
            ctx.emit("vector", lambda e: e.tensor_scalar_mul(st[:, 2:3], st[:, 0:1], 1.0 / Dw),
                     inc=(pre + "s2", 1), waits=[(pre + "sq", sqv)])
            ctx.emit("vector", lambda e: e.scalar_tensor_tensor(out=st[:, 3:4], in0=st[:, 2:3],
                                                                scalar=-1.0, in1=st[:, 2:3],
                                                                op0=ALU.mult, op1=ALU.mult),
                     inc=(pre + "s2", 1), waits=[(pre + "s2", ctx.v(pre + "s2"))])
            ctx.emit("vector", lambda e: e.scalar_tensor_tensor(out=st[:, 3:4], in0=st[:, 1:2],
                                                                scalar=1.0 / Dw, in1=st[:, 3:4],
                                                                op0=ALU.mult, op1=ALU.add),
                     inc=(pre + "s2", 1), waits=[(pre + "s2", ctx.v(pre + "s2"))])
            ctx.emit("vector", lambda e: e.tensor_scalar_add(st[:, 3:4], st[:, 3:4], float(eps)),
                     inc=(pre + "s2", 1), waits=[(pre + "s2", ctx.v(pre + "s2"))])
        else:
            ctx.emit("vector", lambda e: e.tensor_scalar(st[:, 3:4], st[:, 1:2], 1.0 / Dw,
                                                         float(eps), op0=ALU.mult, op1=ALU.add),
                     inc=(pre + "s2", 1), waits=[(pre + "sq", sqv)])
        ctx.emit("scalar", lambda e: e.activation(out=st[:, 4:5], in_=st[:, 3:4], func=AF.Sqrt,
                                                  bias=0.0, scale=1.0),
                 inc=(pre + "s2", 1), waits=[(pre + "s2", ctx.v(pre + "s2"))])
        ctx.emit("vector", lambda e: e.reciprocal(out=st[:, 4:5], in_=st[:, 4:5]),
                 inc=(pre + "s2", 1), waits=[(pre + "s2", ctx.v(pre + "s2"))])
        if extra_scale is not None:
            ctx.emit("vector", lambda e: e.tensor_scalar_mul(st[:, 4:5], st[:, 4:5], float(extra_scale)),
                     inc=(pre + "s2", 1), waits=[(pre + "s2", ctx.v(pre + "s2"))])
        if not rms:
            ctx.emit("vector", lambda e: e.scalar_tensor_tensor(out=st[:, 5:6], in0=st[:, 2:3],
                                                                scalar=-1.0, in1=st[:, 4:5],
                                                                op0=ALU.mult, op1=ALU.mult),
                     inc=(pre + "s2", 1), waits=[(pre + "s2", ctx.v(pre + "s2"))])
        stv = ctx.v(pre + "s2")
        bias_ap = st[:, 5:6] if not rms else 0.0
        nmw = [(pre + "s2", stv)]
        if xn_bf_dram is not None and tt > 0:
            nmw.append((pre + "ob", ob0 + 16 * tt))
        ctx.emit("scalar", lambda e, b=bias_ap: e.activation(out=xnb[:], in_=xsb[:], func=AF.Identity,
                                                             bias=b, scale=st[:, 4:5]),
                 inc=(pre + "nm", 1), waits=nmw)
        nmv = ctx.v(pre + "nm")
        if xn_bf_dram is not None:
            ctx.dma("gpsimd", xn_bf_dram[x_row0 + tt * 128:x_row0 + (tt + 1) * 128, :],
                    xnb[:], pre + "ob", waits=[(pre + "nm", nmv)])
        if xn32_dram is not None:
            if not rms:
                ctx.emit("vector", lambda e: e.scalar_tensor_tensor(
                    out=junk[:], in0=xsb[:], scalar=st[:, 4:5],
                    in1=st[:, 5:6].to_broadcast([128, Dw]),
                    op0=ALU.mult, op1=ALU.add),
                    inc=(pre + "n32", 1), waits=[(pre + "s2", stv)])
            else:
                ctx.emit("vector", lambda e: e.tensor_scalar_mul(junk[:], xsb[:], st[:, 4:5]),
                         inc=(pre + "n32", 1), waits=[(pre + "s2", stv)])
            ctx.dma("gpsimd", xn32_dram[x_row0 + tt * 128:x_row0 + (tt + 1) * 128, :],
                    junk[:], pre + "o3", waits=[(pre + "n32", ctx.v(pre + "n32"))])
        tp2 = ctx.psum(pre + "_tq", [128, 1024], BF16)
        banks = [tp, tp2]
        for g in range(nd // 8):
            gidx = ctx.v(pre + "tp") // 8
            bank = banks[gidx % 2]
            gw = [(pre + "nm", nmv)]
            if gidx >= 2:
                gw.append((pre + "ev", 8 * (gidx - 1)))
            for i in range(8):
                d = g * 8 + i
                ctx.emit("tensor",
                         lambda e, d=d, i=i, bank=bank: e.transpose(
                             bank[:, i * 128:(i + 1) * 128],
                             xnb[:, d * 128:(d + 1) * 128], ident_sb[:]),
                         inc=(pre + "tp", 1), waits=(gw if i == 0 else []))
            tpv = ctx.v(pre + "tp")
            for i in range(8):
                d = g * 8 + i
                col = d * fm_stride + fm_col0 + tt * 128
                ctx.emit("vector",
                         lambda e, i=i, col=col, bank=bank: e.tensor_copy(
                             out=xn_fm_sb[:, col:col + 128], in_=bank[:, i * 128:(i + 1) * 128]),
                         inc=(pre + "ev", 1), waits=([(pre + "tp", tpv)] if i == 0 else []))
    return (pre + "ev", ctx.v(pre + "ev"))


def emit_proj(ctx, pre, xn_fm_sb, w_dram, out_dram, n_tok=TPC, nd=ND,
              mode="tok", F=None, scale=None, bias_dram=None, ones_sb=None,
              fm_ready=None, wait_extra=(), fm_base=0, fm_stride=None,
              w_col0=0, out_sb=None, out_col0=0, psA="pjA", psB="pjB",
              out_stage_dtype=BF16, out_row0=0, wname=None, oname=None,
              bias_fm_dram=None, w_scale_fm=None, w_scale_rdy=None,
              scale_ap=None, scale_ap_rdy=None):
    if fm_stride is None:
        fm_stride = n_tok
    nfc_w = (F // 512) if mode == "tok" else (F // 128)
    if mode == "tok":
        tchunks = [(i * 128, 128) for i in range(n_tok // 128)]
    else:
        tchunks = [(i * 512, 512) for i in range(n_tok // 512)]
        if n_tok % 512:
            assert n_tok % 128 == 0
            tchunks += [(512 * (n_tok // 512) + i * 128, 128)
                        for i in range((n_tok % 512) // 128)]
    ntc = len(tchunks)
    wW = 512 if mode == "tok" else 128
    wsb = ctx.sbuf((wname or pre + "_w"), [128, 2 * nd * wW], BF16)
    osb = ctx.sbuf((oname or pre + "_o"), [128, 2 * 512], out_stage_dtype) if out_sb is None else None
    bfm = None
    if bias_fm_dram is not None:
        bfm = ctx.sbuf(pre + "_bf", [128, F // 128], F32)
        if ctx.v(pre + "lbf") == 0:
            ctx.dma("gpsimd", bfm[:],
                    bias_fm_dram.rearrange("(fo p) -> p fo", p=128), pre + "lbf",
                    allow_slow_non_contiguous=True)
    bsb = None
    if bias_dram is not None:
        bsb = ctx.sbuf(pre + "_b", [1, F], BF16)
        if ctx.v(pre + "lb") == 0:
            ctx.dma("gpsimd", bsb[:],
                    bias_dram.rearrange("(o f) -> o f", o=1) if len(bias_dram.shape) == 1 else bias_dram,
                    pre + "lb")
    ps = [ctx.psum(psA, [128, 512]), ctx.psum(psB, [128, 512])]
    job = 0
    mm0, ev0 = ctx.v(pre + "mm"), ctx.v(pre + "ev")
    lw0 = [ctx.v(pre + "lw0"), ctx.v(pre + "lw1")]
    st0 = [ctx.v(pre + "st0"), ctx.v(pre + "st1")]
    for fc in range(nfc_w):
        par = fc % 2
        wlw = [(pre + "mm", mm0 + max(0, (fc - 1) * ntc))]
        if fc < 2:
            wlw += list(wait_extra)
        ctx.dma("gpsimd",
                wsb[:, par * nd * wW:(par + 1) * nd * wW].rearrange("p (dc f) -> p dc f", dc=nd),
                w_dram[:, w_col0 + fc * wW:w_col0 + (fc + 1) * wW]
                .rearrange("(dc p) f -> p dc f", p=128),
                pre + "lw%d" % par, waits=wlw)
        if w_scale_fm is not None:
            for d_ in range(nd):
                ws_w = [(pre + "lw%d" % par, lw0[par] + 16 * (fc // 2 + 1)),
                        (pre + "mm", mm0 + max(0, (fc - 1) * ntc))]
                if w_scale_rdy is not None:
                    ws_w.append(w_scale_rdy)
                ctx.emit("vector", lambda e, par=par, d_=d_: e.tensor_scalar_mul(
                    wsb[:, par * nd * wW + d_ * wW:par * nd * wW + (d_ + 1) * wW],
                    wsb[:, par * nd * wW + d_ * wW:par * nd * wW + (d_ + 1) * wW],
                    w_scale_fm[:, d_:d_ + 1]),
                    inc=(pre + "lws%d" % par, 1), waits=ws_w)
        for tc, (toff, twid) in enumerate(tchunks):
            bank = job % 2
            waits = [(pre + "lw%d" % par, lw0[par] + 16 * (fc // 2 + 1)),
                     (pre + "ev", ev0 + max(0, job - 1))]
            if w_scale_fm is not None:
                waits.append((pre + "lws%d" % par, ctx.v(pre + "lws%d" % par)))
            if bsb is not None:
                waits.append((pre + "lb", 16))
            if fm_ready is not None:
                waits.append(fm_ready)
            waits.extend(wait_extra)

            def mmfn(e, fc=fc, toff=toff, twid=twid, par=par, bank=bank):
                ins = None
                for d in range(nd):
                    if mode == "tok":
                        lhs = xn_fm_sb[:, d * fm_stride + fm_base + toff:
                                       d * fm_stride + fm_base + toff + twid]
                        rhs = wsb[:, par * nd * 512 + d * 512:par * nd * 512 + (d + 1) * 512]
                    else:
                        lhs = wsb[:, par * nd * 128 + d * 128:par * nd * 128 + (d + 1) * 128]
                        rhs = xn_fm_sb[:, d * fm_stride + fm_base + toff:
                                       d * fm_stride + fm_base + toff + twid]
                    ins = e.matmul(ps[bank][:, 0:(512 if mode == "tok" else twid)],
                                   lhs, rhs, start=(d == 0),
                                   stop=(d == nd - 1 and bsb is None))
                if bsb is not None:
                    ins = e.matmul(ps[bank][:, 0:512], ones_sb[0:1, 0:128],
                                   bsb[0:1, fc * 512:(fc + 1) * 512], start=False, stop=True)
                return ins

            ctx.emit("tensor", mmfn, inc=(pre + "mm", 1), waits=waits)
            ow = 512 if mode == "tok" else twid
            if out_sb is not None:
                ob = out_sb[:, (toff // 128) * F + out_col0 + fc * 512:
                            (toff // 128) * F + out_col0 + (fc + 1) * 512]
            else:
                ob = osb[:, bank * 512:bank * 512 + ow]
            evw = [(pre + "mm", ctx.v(pre + "mm"))]
            if out_sb is None and out_dram is not None:
                if job >= 2:
                    evw.append((pre + "st%d" % bank, st0[bank] + 16 * (job // 2)))
                elif st0[bank] > 0:
                    evw.append((pre + "st%d" % bank, st0[bank]))
            if job < 2:
                evw += list(wait_extra)
            if scale_ap is not None:
                if scale_ap_rdy is not None:
                    evw.append(scale_ap_rdy)
                ctx.emit("scalar", lambda e, ob=ob, bank=bank, ow=ow: e.activation(
                    out=ob, in_=ps[bank][:, 0:ow], func=AF.Identity,
                    bias=0.0, scale=scale_ap),
                    inc=(pre + "ev", 1), waits=evw)
            elif bfm is not None:
                evw.append((pre + "lbf", 16))
                ctx.emit("scalar", lambda e, ob=ob, bank=bank, ow=ow, fc=fc: e.activation(
                    out=ob, in_=ps[bank][:, 0:ow], func=AF.Identity,
                    bias=bfm[:, fc:fc + 1], scale=float(scale if scale is not None else 1.0)),
                    inc=(pre + "ev", 1), waits=evw)
            elif scale is not None:
                ctx.emit("scalar", lambda e, ob=ob, bank=bank, ow=ow: e.activation(
                    out=ob, in_=ps[bank][:, 0:ow], func=AF.Identity, bias=0.0, scale=float(scale)),
                    inc=(pre + "ev", 1), waits=evw)
            else:
                ctx.emit("vector", lambda e, ob=ob, bank=bank, ow=ow: e.tensor_copy(
                    out=ob, in_=ps[bank][:, 0:ow]),
                    inc=(pre + "ev", 1), waits=evw)
            if out_dram is not None:
                if mode == "tok":
                    dst = out_dram[out_row0 + toff:out_row0 + toff + twid,
                                   fc * 512:(fc + 1) * 512]
                else:
                    dst = out_dram[fc * 128:(fc + 1) * 128, toff:toff + twid]
                ctx.dma("sync", dst, ob, pre + "st%d" % bank, waits=[(pre + "ev", ctx.v(pre + "ev"))])
            job += 1
    nst = ctx.v(pre + "st0") + ctx.v(pre + "st1")
    if out_dram is not None:
        return [(pre + "st0", ctx.v(pre + "st0")), (pre + "st1", ctx.v(pre + "st1"))]
    return [(pre + "ev", ctx.v(pre + "ev"))]


def emit_ln_add(ctx, pre, y_dram, x_dram, out_dram, n_tt=NT, nd=ND, eps=LN_EPS,
                wait_extra=()):
    Dw = nd * 128
    ysb = ctx.sbuf(pre + "_y", [128, Dw], F32)
    xsb = ctx.sbuf(pre + "_x", [128, Dw], F32)
    osb = ctx.sbuf(pre + "_oo", [128, Dw], F32)
    st = ctx.sbuf(pre + "_st", [128, 8], F32)
    junk = ctx.sbuf(pre + "_junk", [128, Dw], F32)
    so0 = ctx.v(pre + "so")
    for tt in range(n_tt):
        ctx.dma("sync", ysb[:], y_dram[tt * 128:(tt + 1) * 128, :], pre + "ld",
                waits=[(pre + "ad", ctx.v(pre + "ad"))] + list(wait_extra))
        ctx.dma("sync", xsb[:], x_dram[tt * 128:(tt + 1) * 128, :], pre + "ld",
                waits=[(pre + "ad", ctx.v(pre + "ad"))])
        ldv = ctx.v(pre + "ld")
        ctx.emit("scalar", lambda e: e.activation(out=junk[:], in_=ysb[:], func=AF.Square,
                                                  bias=0.0, scale=1.0, accum_out=st[:, 1:2]),
                 inc=(pre + "sq", 1), waits=[(pre + "ld", ldv)])
        ctx.emit("vector", lambda e: e.reduce_sum(out=st[:, 0:1], in_=ysb[:], axis=AX.X),
                 inc=(pre + "sq", 1), waits=[(pre + "ld", ldv)])
        sqv = ctx.v(pre + "sq")
        ctx.emit("vector", lambda e: e.tensor_scalar_mul(st[:, 2:3], st[:, 0:1], 1.0 / Dw),
                 inc=(pre + "s2", 1), waits=[(pre + "sq", sqv)])
        ctx.emit("vector", lambda e: e.scalar_tensor_tensor(out=st[:, 3:4], in0=st[:, 2:3],
                                                            scalar=-1.0, in1=st[:, 2:3],
                                                            op0=ALU.mult, op1=ALU.mult),
                 inc=(pre + "s2", 1), waits=[(pre + "s2", ctx.v(pre + "s2"))])
        ctx.emit("vector", lambda e: e.scalar_tensor_tensor(out=st[:, 3:4], in0=st[:, 1:2],
                                                            scalar=1.0 / Dw, in1=st[:, 3:4],
                                                            op0=ALU.mult, op1=ALU.add),
                 inc=(pre + "s2", 1), waits=[(pre + "s2", ctx.v(pre + "s2"))])
        ctx.emit("vector", lambda e: e.tensor_scalar_add(st[:, 3:4], st[:, 3:4], float(eps)),
                 inc=(pre + "s2", 1), waits=[(pre + "s2", ctx.v(pre + "s2"))])
        ctx.emit("scalar", lambda e: e.activation(out=st[:, 4:5], in_=st[:, 3:4], func=AF.Sqrt,
                                                  bias=0.0, scale=1.0),
                 inc=(pre + "s2", 1), waits=[(pre + "s2", ctx.v(pre + "s2"))])
        ctx.emit("vector", lambda e: e.reciprocal(out=st[:, 4:5], in_=st[:, 4:5]),
                 inc=(pre + "s2", 1), waits=[(pre + "s2", ctx.v(pre + "s2"))])
        ctx.emit("vector", lambda e: e.scalar_tensor_tensor(out=st[:, 5:6], in0=st[:, 2:3],
                                                            scalar=-1.0, in1=st[:, 4:5],
                                                            op0=ALU.mult, op1=ALU.mult),
                 inc=(pre + "s2", 1), waits=[(pre + "s2", ctx.v(pre + "s2"))])
        stv = ctx.v(pre + "s2")
        ctx.emit("vector", lambda e: e.scalar_tensor_tensor(
            out=osb[:], in0=ysb[:], scalar=st[:, 4:5],
            in1=st[:, 5:6].to_broadcast([128, Dw]), op0=ALU.mult, op1=ALU.add),
            inc=(pre + "nm", 1), waits=[(pre + "s2", stv), (pre + "so", so0 + 16 * tt)])
        ctx.emit("vector", lambda e: e.tensor_add(osb[:], osb[:], xsb[:]),
                 inc=(pre + "ad", 1), waits=[(pre + "nm", ctx.v(pre + "nm"))])
        ctx.dma("gpsimd", out_dram[tt * 128:(tt + 1) * 128, :], osb[:], pre + "so",
                waits=[(pre + "ad", ctx.v(pre + "ad"))])
    return (pre + "so", ctx.v(pre + "so"))


def emit_l0_flash(ctx, pre, qt_dram, kt_own, kt_sib, v_own, v_sib, ot_dram,
                  sibmask_dram, masks_dram, ready=()):
    ksb = ctx.sbuf(pre + "_k", [128, 2 * 4096], BF16)
    vsb = ctx.sbuf(pre + "_v", [128, 2 * 4096], BF16)
    qsb = ctx.sbuf(pre + "_q", [128, 2 * 2048], BF16)
    ptb = ctx.sbuf(pre + "_pt", [128, 2 * 1024], BF16)
    mstat = ctx.sbuf(pre + "_ms", [128, 16], F32)
    mtr = ctx.sbuf(pre + "_mt", [128, 128], F32)
    mrow = ctx.sbuf(pre + "_mr", [1, 128], F32)
    rl = ctx.sbuf(pre + "_rl", [1, 512], F32)
    rlb = ctx.sbuf(pre + "_rlb", [128, 512], F32)
    biasv = ctx.sbuf(pre + "_bv", [128, 4], F32)
    sibm = ctx.sbuf(pre + "_sm", [128, 1], F32)
    masks = ctx.sbuf(pre + "_mk", [128, 4 * 512], F32)
    otb = ctx.sbuf(pre + "_otb", [128, 512], BF16)
    onesf = ctx.sbuf(pre + "_of", [1, 128], F32)
    onesk = ctx.sbuf(pre + "_ok", [128, 1], BF16)
    identf = ctx.sbuf(pre + "_idf", [128, 128], F32)

    psT = [ctx.psum(pre + "_psT0", [128, 1024]), ctx.psum(pre + "_psT1", [128, 1024])]
    psO = ctx.psum(pre + "_psO", [128, 512])
    psL = ctx.psum(pre + "_psL", [128, 512])
    psB = ctx.psum(pre + "_psB", [128, 512])

    onesd = ctx.const(pre + "_ones128f", np.ones((128, 128), dtype=np.float32))
    onesb = ctx.const(pre + "_ones128b", np.ones((128, 1), dtype=ml_dtypes.bfloat16))
    identd = ctx.const(pre + "_identf32", np.eye(128, dtype=np.float32))
    ctx.dma("sync", sibm[:], sibmask_dram, pre + "cl", waits=ready)
    for m in range(4):
        ctx.dma("sync", masks[:, m * 512:(m + 1) * 512], masks_dram[m], pre + "cl")
    ctx.dma("sync", onesf[:], onesd[0:1, :], pre + "cl")
    ctx.dma("sync", onesk[:], onesb, pre + "cl")
    ctx.dma("sync", identf[:], identd, pre + "cl")
    clv = ctx.v(pre + "cl")
    for lane in ("tensor", "scalar", "vector"):
        ctx.wait(lane, [(pre + "cl", clv)])

    def head_load(h, par):
        w = [(pre + "hd", 16 * 4 * (h - 1))] if h >= 2 else []
        ctx.dma("gpsimd", ksb[:, par * 4096:par * 4096 + 2048],
                kt_sib[h * 128:(h + 1) * 128, :], pre + "lk%d" % par, waits=w)
        ctx.dma("gpsimd", ksb[:, par * 4096 + 2048:par * 4096 + 4096],
                kt_own[h * 128:(h + 1) * 128, :], pre + "lk%d" % par)
        ctx.dma("gpsimd", vsb[:, par * 4096:par * 4096 + 2048].rearrange("p (kt f) -> p kt f", kt=16),
                v_sib[:, h * 128:(h + 1) * 128].rearrange("(kt p) f -> p kt f", p=128),
                pre + "lv%d" % par)
        ctx.dma("gpsimd", vsb[:, par * 4096 + 2048:par * 4096 + 4096].rearrange("p (kt f) -> p kt f", kt=16),
                v_own[:, h * 128:(h + 1) * 128].rearrange("(kt p) f -> p kt f", p=128),
                pre + "lv%d" % par)
        ctx.dma("gpsimd", qsb[:, par * 2048:(par + 1) * 2048],
                qt_dram[h * 128:(h + 1) * 128, :], pre + "lq%d" % par)

    head_load(0, 0)
    head_load(1, 1)
    for h in range(16):
        par = h % 2
        kof, qof = par * 4096, par * 2048
        nload = 16 * (h // 2 + 1)
        ctx.wait("tensor", [(pre + "lk%d" % par, 2 * nload), (pre + "lv%d" % par, 2 * nload),
                            (pre + "lq%d" % par, nload)])
        for qsc in range(4):
            n_own = 4 * qsc + 4
            ktiles = 16 + n_own
            qslice = qsb[:, qof + qsc * 512: qof + (qsc + 1) * 512]
            nkb = (ktiles * 128) // 512
            jobs = [(qc, kb) for kb in range(nkb) for qc in range(4)]
            for jp in range(0, len(jobs), 2):
                pj = jp // 2
                bank = psT[pj % 2]
                w = [(pre + "am", ctx.v(pre + "am") - 1)] if ctx.v(pre + "am") >= 2 else []
                if pj < 2:
                    w.append((pre + "ex", ctx.v(pre + "ex")))

                def afn(e, jobs=jobs, jp=jp, bank=bank, kof=kof, qof=qof, qsc=qsc):
                    ins = None
                    for half, (qc, kb) in enumerate(jobs[jp:jp + 2]):
                        ins = e.matmul(bank[:, half * 512:(half + 1) * 512],
                                       qsb[:, qof + (qsc * 4 + qc) * 128: qof + (qsc * 4 + qc + 1) * 128],
                                       ksb[:, kof + kb * 512: kof + (kb + 1) * 512],
                                       start=True, stop=True)
                    return ins
                ctx.emit("tensor", afn, inc=(pre + "as", 1), waits=w)
                mw = [(pre + "as", ctx.v(pre + "as"))]
                if pj == 0:
                    mw.append((pre + "mx", ctx.v(pre + "mx")))
                ctx.emit("vector", lambda e, pj=pj, bank=bank: e.reduce_max(
                    out=mstat[:, pj % 16:pj % 16 + 1], in_=bank[:, 0:1024], axis=AX.X),
                    inc=(pre + "am", 1), waits=mw)
            npairs_a = len(jobs) // 2
            amv = ctx.v(pre + "am")
            ctx.emit("vector", lambda e, n=npairs_a: e.reduce_max(
                out=mtr[:, 0:1], in_=mstat[:, 0:n], axis=AX.X),
                inc=(pre + "mx", 1), waits=[(pre + "am", amv)])
            ctx.emit("vector", lambda e: e.memset(mtr[:, 1:128], -BIG),
                     inc=(pre + "mx", 1), waits=[])
            ctx.emit("tensor", lambda e: e.transpose(psB[:, 0:128], mtr[:], identf[:]),
                     inc=(pre + "mx", 1), waits=[(pre + "mx", ctx.v(pre + "mx")),
                                                 (pre + "fin", ctx.v(pre + "fin"))])
            ctx.emit("vector", lambda e: e.reduce_max(out=mrow[0:1, 0:1], in_=psB[0:1, 0:128], axis=AX.X),
                     inc=(pre + "mx", 1), waits=[(pre + "mx", ctx.v(pre + "mx"))])
            ctx.emit("tensor", lambda e: e.matmul(psB[:, 128:129], onesf[:], mrow[0:1, 0:1],
                                                  start=True, stop=True),
                     inc=(pre + "mx", 1), waits=[(pre + "mx", ctx.v(pre + "mx"))])
            ctx.emit("vector", lambda e: e.tensor_scalar_mul(biasv[:, 0:1], psB[:, 128:129], -1.0),
                     inc=(pre + "mx", 1), waits=[(pre + "mx", ctx.v(pre + "mx"))])
            ctx.emit("vector", lambda e: e.tensor_add(biasv[:, 1:2], biasv[:, 0:1], sibm[:]),
                     inc=(pre + "mx", 1), waits=[(pre + "mx", ctx.v(pre + "mx"))])
            mxv = ctx.v(pre + "mx")
            npair = ktiles // 2
            for j in range(npair):
                bank = psT[j % 2]
                pt = ptb[:, (j % 2) * 1024:(j % 2 + 1) * 1024]
                w = [(pre + "mx", mxv)]
                if j >= 2:
                    w.append((pre + "ex", ctx.v(pre + "ex") - 1))

                def tfn(e, j=j, bank=bank, kof=kof, qslice=qslice):
                    ins = None
                    for half in range(2):
                        kt = 2 * j + half
                        ins = e.matmul(bank[:, half * 512:(half + 1) * 512],
                                       ksb[:, kof + kt * 128: kof + (kt + 1) * 128],
                                       qslice, start=True, stop=True)
                    return ins
                ctx.emit("tensor", tfn, inc=(pre + "ts", 1), waits=w)
                for half in range(2):
                    kt = 2 * j + half
                    m = kt - 16 - 4 * qsc
                    if 0 <= m <= 3:
                        ctx.emit("vector", lambda e, half=half, m=m, bank=bank: e.tensor_add(
                            bank[:, half * 512:(half + 1) * 512],
                            bank[:, half * 512:(half + 1) * 512],
                            masks[:, m * 512:(m + 1) * 512]),
                            inc=(pre + "ts", 1), waits=[(pre + "ts", ctx.v(pre + "ts"))])
                bias = biasv[:, 1:2] if (2 * j + 1) < 16 else biasv[:, 0:1]
                w = [(pre + "ts", ctx.v(pre + "ts"))]
                if j >= 2:
                    w.append((pre + "pv", ctx.v(pre + "pv") - 4))
                ctx.emit("scalar", lambda e, bank=bank, pt=pt, bias=bias: e.activation(
                    out=pt, in_=bank[:, 0:1024], func=AF.Exp, bias=bias, scale=1.0),
                    inc=(pre + "ex", 1), waits=w)
                exv = ctx.v(pre + "ex")

                def pvfn(e, j=j, pt=pt, kof=kof, npair=npair):
                    ins = None
                    for half in range(2):
                        kt = 2 * j + half
                        e.matmul(psO[:], vsb[:, kof + kt * 128: kof + (kt + 1) * 128],
                                 pt[:, half * 512:(half + 1) * 512],
                                 start=(j == 0 and half == 0), stop=(j == npair - 1 and half == 1))
                        ins = e.matmul(psL[0:1, 0:512], onesk[:],
                                       pt[:, half * 512:(half + 1) * 512],
                                       start=(j == 0 and half == 0), stop=(j == npair - 1 and half == 1))
                    return ins
                w = [(pre + "ex", exv)]
                if j == 0:
                    w.append((pre + "fin", ctx.v(pre + "fin")))
                ctx.emit("tensor", pvfn, inc=(pre + "pv", 4), waits=w)
            pvv = ctx.v(pre + "pv")
            ctx.emit("vector", lambda e: e.reciprocal(out=rl[:], in_=psL[0:1, 0:512]),
                     inc=(pre + "fz", 1), waits=[(pre + "pv", pvv)])
            ctx.emit("tensor", lambda e: e.matmul(psB[:, 0:512], onesf[:], rl[:],
                                                  start=True, stop=True),
                     inc=(pre + "fz", 1), waits=[(pre + "fz", ctx.v(pre + "fz"))])
            ctx.emit("vector", lambda e: e.tensor_copy(out=rlb[:], in_=psB[:, 0:512]),
                     inc=(pre + "fz", 1), waits=[(pre + "fz", ctx.v(pre + "fz"))])
            ctx.emit("vector", lambda e: e.scalar_tensor_tensor(
                out=otb[:], in0=psO[:], scalar=1.0, in1=rlb[:], op0=ALU.mult, op1=ALU.mult),
                inc=(pre + "fz", 1), waits=[(pre + "fz", ctx.v(pre + "fz")),
                                            (pre + "hd", 16 * (4 * h + qsc))])
            ctx.dma("gpsimd", ot_dram[h * 128:(h + 1) * 128, qsc * 512:(qsc + 1) * 512], otb[:],
                    pre + "hd", waits=[(pre + "fz", ctx.v(pre + "fz"))])
            ctx.emit("vector", lambda e: e.tensor_copy(out=mtr[:, 1:2], in_=rlb[:, 0:1]),
                     inc=(pre + "fin", 1), waits=[(pre + "fz", ctx.v(pre + "fz"))])
        if h + 2 < 16:
            head_load(h + 2, par)
    return (pre + "hd", ctx.v(pre + "hd"))



def setup_common(ctx):
    ident_sb = ctx.sbuf("ident_sb", [128, 128], BF16)
    identb = ctx.const("identb16", np.eye(128, dtype=ml_dtypes.bfloat16))
    ctx.dma("sync", ident_sb[:], identb, "ldi")
    ones_sb = ctx.sbuf("ones_sb", [1, 512], BF16)
    onesc = ctx.const("ones512", np.ones((1, 512), dtype=ml_dtypes.bfloat16))
    ctx.dma("sync", ones_sb[:], onesc, "ldo")
    for lane in ("tensor", "vector", "scalar", "gpsimd"):
        ctx.wait(lane, [("ldi", 16), ("ldo", 16)])
    return ident_sb, ones_sb


def build_l0a(parts="lnqkv", ctx=None, io=None):
    ctx = ctx or Ctx()
    io = io or {}
    x = _io(ctx, io, "in", "xin", [TPC, D])
    wq = _io(ctx, io, "in", "wq", [D, D])
    wk = _io(ctx, io, "in", "wk", [D, D])
    wv = _io(ctx, io, "in", "wv", [D, D])
    qt = _io(ctx, io, "out", "QT", [D, TPC], BF16)
    kt = _io(ctx, io, "out", "KT", [D, TPC], BF16)
    vo = _io(ctx, io, "out", "V", [TPC, D], BF16)
    ident_sb, ones_sb = setup_common(ctx)
    xn_fm = ctx.sbuf("xn_fm", [128, ND * TPC], BF16)
    fmr = emit_ln_to_fm(ctx, "ln", x, xn_fm, ident_sb)
    fin = [fmr]
    if "q" in parts:
        fin += emit_proj(ctx, "q", xn_fm, wq, qt, mode="fm", F=D, scale=1.0 / math.sqrt(128.0),
                         fm_ready=fmr, psA="p0", psB="p1")
    if "k" in parts:
        fin += emit_proj(ctx, "k", xn_fm, wk, kt, mode="fm", F=D, fm_ready=fmr, psA="p2", psB="p3")
    if "v" in parts:
        fin += emit_proj(ctx, "v", xn_fm, wv, vo, mode="tok", F=D, fm_ready=fmr, psA="p4", psB="p5")
    ctx.wait("gpsimd", fin)
    return ctx



def emit_conv_silu(ctx, pre, raw_dram, convw_dram, out_dram, F, ready=(), bufpre=None):
    nfc = F // 128
    bp = bufpre or pre
    raw = ctx.sbuf(bp + "_raw", [128, 2 * 2176], BF16)
    cw = ctx.sbuf(bp + "_cw", [128, 2 * 4], F32)
    cv = ctx.sbuf(bp + "_cv", [128, 2048], F32)
    sil = ctx.sbuf(bp + "_sil", [128, 2048], BF16)
    cd0, st0_ = ctx.v(pre + "cd"), ctx.v(pre + "st")
    for fc in range(nfc):
        par = fc % 2
        w = [(pre + "cd", cd0 + max(0, 4 * (fc - 1)))] + list(ready)
        ctx.dma("sync", raw[:, par * 2176:(par + 1) * 2176],
                raw_dram[fc * 128:(fc + 1) * 128, :], pre + "ld%d" % par, waits=w)
        ctx.dma("sync", cw[:, par * 4:(par + 1) * 4],
                convw_dram[fc * 128:(fc + 1) * 128, :], pre + "ld%d" % par)
        ldv = ctx.v(pre + "ld%d" % par)
        ctx.emit("vector", lambda e, par=par: e.tensor_scalar_mul(
            cv[:], raw[:, par * 2176 + 125: par * 2176 + 125 + 2048],
            cw[:, par * 4: par * 4 + 1]),
            inc=(pre + "cd", 1), waits=[(pre + "ld%d" % par, ldv),
                                        (pre + "sl", ctx.v(pre + "sl"))])
        for j in range(1, 4):
            ctx.emit("vector", lambda e, par=par, j=j: e.scalar_tensor_tensor(
                out=cv[:], in0=raw[:, par * 2176 + 125 + j: par * 2176 + 125 + j + 2048],
                scalar=cw[:, par * 4 + j: par * 4 + j + 1], in1=cv[:],
                op0=ALU.mult, op1=ALU.add),
                inc=(pre + "cd", 1), waits=[(pre + "cd", ctx.v(pre + "cd"))])
        ctx.emit("scalar", lambda e: e.activation(out=sil[:], in_=cv[:], func=AF.Silu,
                                                  bias=0.0, scale=1.0),
                 inc=(pre + "sl", 1), waits=[(pre + "cd", ctx.v(pre + "cd")),
                                             (pre + "st", st0_ + 16 * fc)])
        ctx.dma("gpsimd", out_dram[fc * 128:(fc + 1) * 128, :], sil[:], pre + "st",
                waits=[(pre + "sl", ctx.v(pre + "sl"))])
    return (pre + "st", ctx.v(pre + "st"))


def emit_l2norm_fm(ctx, pre, src_dram, out_dram, onesf_sb, F=1536, dh=256, scale=None,
                   eps=1e-6, ready=(), bufpre=None):
    nh = F // dh
    nfc_h = dh // 128
    bp = bufpre or pre
    sb = ctx.sbuf(bp + "_s", [128, nfc_h * 2048], BF16)
    sq = ctx.sbuf(bp + "_q", [128, 2048], F32)
    nrow = ctx.sbuf(bp + "_nr", [1, 2048], F32)
    rb = ctx.sbuf(bp + "_rb", [128, 2048], F32)
    ob = ctx.sbuf(bp + "_ob", [128, 2048], BF16)
    nps = ctx.psum(bp + "_np", [128, 512])
    so0 = ctx.v(pre + "so")
    for h in range(nh):
        for i in range(nfc_h):
            ctx.dma("sync", sb[:, i * 2048:(i + 1) * 2048],
                    src_dram[(h * nfc_h + i) * 128:(h * nfc_h + i + 1) * 128, :],
                    pre + "ld", waits=[(pre + "so", so0 + 16 * nfc_h * h)] + list(ready))
        ldv = ctx.v(pre + "ld")
        for cc in range(4):
            for i in range(nfc_h):
                ctx.emit("scalar", lambda e, cc=cc, i=i: e.activation(
                    out=sq[:, cc * 512:(cc + 1) * 512],
                    in_=sb[:, i * 2048 + cc * 512: i * 2048 + (cc + 1) * 512],
                    func=AF.Square, bias=0.0, scale=1.0),
                    inc=(pre + "sc", 1), waits=[(pre + "ld", ldv),
                                                (pre + "nm", ctx.v(pre + "nm"))])
                mw = [(pre + "sc", ctx.v(pre + "sc"))]
                if i == 0:
                    mw.append((pre + "nm", ctx.v(pre + "nm")))
                ctx.emit("tensor", lambda e, cc=cc, i=i: e.matmul(
                    nps[0:1, 0:512], onesf_sb[0:128, 0:1], sq[:, cc * 512:(cc + 1) * 512],
                    start=(i == 0), stop=(i == nfc_h - 1)),
                    inc=(pre + "nm", 1), waits=mw)
            ctx.emit("vector", lambda e, cc=cc: e.tensor_scalar_add(
                nrow[0:1, cc * 512:(cc + 1) * 512], nps[0:1, 0:512], float(eps)),
                inc=(pre + "nm", 1), waits=[(pre + "nm", ctx.v(pre + "nm"))])
        ctx.emit("scalar", lambda e: e.activation(out=nrow[:], in_=nrow[:], func=AF.Sqrt,
                                                  bias=0.0, scale=1.0),
                 inc=(pre + "nm", 1), waits=[(pre + "nm", ctx.v(pre + "nm"))])
        ctx.emit("vector", lambda e: e.reciprocal(out=nrow[:], in_=nrow[:]),
                 inc=(pre + "nm", 1), waits=[(pre + "nm", ctx.v(pre + "nm"))])
        if scale is not None:
            ctx.emit("vector", lambda e: e.tensor_scalar_mul(nrow[:], nrow[:], float(scale)),
                     inc=(pre + "nm", 1), waits=[(pre + "nm", ctx.v(pre + "nm"))])
        nmv = ctx.v(pre + "nm")
        for cc in range(4):
            ctx.emit("tensor", lambda e, cc=cc: e.matmul(
                nps[:, 0:512], onesf_sb[0:1, 0:128], nrow[0:1, cc * 512:(cc + 1) * 512],
                start=True, stop=True),
                inc=(pre + "bc", 1), waits=[(pre + "nm", nmv), (pre + "be", ctx.v(pre + "be"))])
            ctx.emit("vector", lambda e, cc=cc: e.tensor_copy(
                out=rb[:, cc * 512:(cc + 1) * 512], in_=nps[:, 0:512]),
                inc=(pre + "be", 1), waits=[(pre + "bc", ctx.v(pre + "bc"))])
        bev = ctx.v(pre + "be")
        for i in range(nfc_h):
            ctx.emit("vector", lambda e, i=i: e.tensor_mul(ob[:], sb[:, i * 2048:(i + 1) * 2048],
                                                           rb[:]),
                     inc=(pre + "ml", 1), waits=[(pre + "be", bev),
                                                 (pre + "so", so0 + 16 * (nfc_h * h + i))])
            ctx.dma("gpsimd", out_dram[(h * nfc_h + i) * 128:(h * nfc_h + i + 1) * 128, :],
                    ob[:], pre + "so", waits=[(pre + "ml", ctx.v(pre + "ml"))])
    return (pre + "so", ctx.v(pre + "so"))


def emit_fm_to_il(ctx, pre, src_dram, il_dram, parity, ident_sb, F, ready=(), bufpre=None):
    nfc = F // 128
    bp = bufpre or pre
    sb = ctx.sbuf(bp + "_s", [128, 2048], BF16)
    tk = ctx.sbuf(bp + "_t", [128, 2048], BF16)
    tpA = ctx.psum(bp + "_tA", [128, 1024], BF16)
    tpB = ctx.psum(bp + "_tB", [128, 1024], BF16)
    dst3 = il_dram
    ist0 = ctx.v(pre + "st")
    for fc in range(nfc):
        ctx.dma("sync", sb[:], src_dram[fc * 128:(fc + 1) * 128, :], pre + "ld",
                waits=[(pre + "te", ctx.v(pre + "te"))] + list(ready))
        ldv = ctx.v(pre + "ld")
        for g in range(2):
            gidx = ctx.v(pre + "tp") // 8
            bank = [tpA, tpB][gidx % 2]
            gw = [(pre + "ld", ldv)]
            if gidx >= 2:
                gw.append((pre + "te", 8 * (gidx - 1)))
            for i in range(8):
                t8 = g * 8 + i
                ctx.emit("tensor", lambda e, t8=t8, i=i, bank=bank: e.transpose(
                    bank[:, i * 128:(i + 1) * 128], sb[:, t8 * 128:(t8 + 1) * 128], ident_sb[:]),
                    inc=(pre + "tp", 1), waits=(gw if i == 0 else []))
            tpv = ctx.v(pre + "tp")
            for i in range(8):
                t8 = g * 8 + i
                ctx.emit("vector", lambda e, t8=t8, i=i, bank=bank: e.tensor_copy(
                    out=tk[:, t8 * 128:(t8 + 1) * 128], in_=bank[:, i * 128:(i + 1) * 128]),
                    inc=(pre + "te", 1),
                    waits=([(pre + "tp", tpv), (pre + "st", ist0 + 16 * 16 * fc)] if i == 0 else []))
        tev = ctx.v(pre + "te")
        for tt in range(16):
            ctx.dma("gpsimd", dst3[tt * 128:(tt + 1) * 128, fc * 128:(fc + 1) * 128],
                    tk[:, tt * 128:(tt + 1) * 128], pre + "st", waits=[(pre + "te", tev)])
    return (pre + "st", ctx.v(pre + "st"))


def build_l1a(ctx=None, io=None):
    ctx = ctx or Ctx()
    io = io or {}
    x = _io(ctx, io, "in", "xin", [TPC, D])
    xbnd = _io(ctx, io, "in", "xbnd", [128, D])
    wq = _io(ctx, io, "in", "wq", [D, 1536])
    wk0 = _io(ctx, io, "in", "wk0", [D, 1536])
    wk1 = _io(ctx, io, "in", "wk1", [D, 1536])
    wv0 = _io(ctx, io, "in", "wv0", [D, 3072])
    wv1 = _io(ctx, io, "in", "wv1", [D, 3072])
    wbab = _io(ctx, io, "in", "wbab", [D, 18])
    dtb18 = _io(ctx, io, "in", "dtb18", [18])
    wg = _io(ctx, io, "in", "wg", [D, 3072])
    cqw = _io(ctx, io, "in", "cqw", [1536, 4])
    ckw0 = _io(ctx, io, "in", "ckw0", [1536, 4])
    ckw1 = _io(ctx, io, "in", "ckw1", [1536, 4])
    cvw0 = _io(ctx, io, "in", "cvw0", [3072, 4])
    cvw1 = _io(ctx, io, "in", "cvw1", [3072, 4])
    a_log = _io(ctx, io, "in", "a_log", [6])
    qfn = _io(ctx, io, "out", "QFN", [1536, TPC], BF16)
    kfn0 = _io(ctx, io, "out", "KFN0", [1536, TPC], BF16)
    kfn1 = _io(ctx, io, "out", "KFN1", [1536, TPC], BF16)
    v0t = _io(ctx, io, "out", "V0T", [TPC, 3072], BF16)
    v1t = _io(ctx, io, "out", "V1T", [TPC, 3072], BF16)
    b0d = _io(ctx, io, "out", "B0", [TPC, 6])
    b1d = _io(ctx, io, "out", "B1", [TPC, 6])
    gd = _io(ctx, io, "out", "G", [TPC, 6])
    gg = _io(ctx, io, "out", "GG", [TPC, 3072])
    qraw = ctx.dram_tmp("QRAW", [1536, 2176], BF16)
    kraw0 = ctx.dram_tmp("KRAW0", [1536, 2176], BF16)
    kraw1 = ctx.dram_tmp("KRAW1", [1536, 2176], BF16)
    vraw0 = ctx.dram_tmp("VRAW0", [3072, 2176], BF16)
    vraw1 = ctx.dram_tmp("VRAW1", [3072, 2176], BF16)
    qcv = ctx.dram_tmp("QCV", [1536, TPC], BF16)
    kcv0 = ctx.dram_tmp("KCV0", [1536, TPC], BF16)
    kcv1 = ctx.dram_tmp("KCV1", [1536, TPC], BF16)
    vcv0 = ctx.dram_tmp("VCV0", [3072, TPC], BF16)
    vcv1 = ctx.dram_tmp("VCV1", [3072, TPC], BF16)
    ident_sb, ones_sb = setup_common(ctx)
    onesf = ctx.sbuf("onesf", [128, 128], F32)
    onesfd = ctx.const("onesf128", np.ones((128, 128), dtype=np.float32))
    ctx.dma("sync", onesf[:], onesfd, "lof")
    ctx.wait("tensor", [("lof", 16)])
    ctx.wait("vector", [("lof", 16)])
    ctx.wait("scalar", [("lof", 16)])

    xn_fm = ctx.sbuf("xn_fm", [128, ND * 2176], BF16)
    emit_ln_to_fm(ctx, "ln", xbnd, xn_fm, ident_sb, n_tt=1, fm_stride=2176, fm_col0=0)
    fmr = emit_ln_to_fm(ctx, "ln", x, xn_fm, ident_sb, n_tt=NT, fm_stride=2176, fm_col0=128)

    prev = []
    fins = {}
    for pre_, wd, od, FF in [("pq", wq, qraw, 1536), ("pk0", wk0, kraw0, 1536),
                             ("pk1", wk1, kraw1, 1536), ("pv0", wv0, vraw0, 3072),
                             ("pv1", wv1, vraw1, 3072)]:
        fin = emit_proj(ctx, "pp", xn_fm, wd, od, n_tok=2176, fm_stride=2176, mode="fm",
                        F=FF, fm_ready=fmr, wait_extra=prev, psA="pA", psB="pB",
                        wname="pwfm", oname="pofm")
        prev = [("ppev", ctx.v("ppev"))] + fin
        fins[pre_] = list(fin)
    ggfin = emit_proj(ctx, "pt", xn_fm, wg, gg, n_tok=TPC, fm_stride=2176, fm_base=128,
                      mode="tok", F=3072, fm_ready=fmr, wait_extra=prev, psA="pA", psB="pB",
                      out_stage_dtype=F32)

    bab = ctx.sbuf("bab", [128, ND * 18], BF16)
    dtb = ctx.sbuf("dtb", [1, 18], BF16)
    bps = ctx.psum("bps", [128, 512])
    bsg = ctx.sbuf("bsg", [128, 18], F32)
    gtmp = ctx.sbuf("gtmp", [128, 6], F32)
    gtm2 = ctx.sbuf("gtm2", [128, 6], F32)
    onecol = ctx.sbuf("onecol", [128, 1], F32)
    ealog = ctx.sbuf("ealog", [128, 8], F32)
    ctx.dma("gpsimd", bab[:].rearrange("p (dc f) -> p dc f", dc=ND),
            wbab.rearrange("(dc p) f -> p dc f", p=128), "lba")
    ctx.dma("gpsimd", dtb[:], dtb18.rearrange("(o f) -> o f", o=1), "lba")
    ctx.dma("gpsimd", ealog[0:1, 0:6], a_log.rearrange("(o f) -> o f", o=1), "lba")
    ctx.emit("vector", lambda e: e.memset(onecol[:], 1.0), inc=("bgp", 1), waits=[])
    ctx.emit("scalar", lambda e: e.activation(out=ealog[0:1, 0:6], in_=ealog[0:1, 0:6],
                                              func=AF.Exp, bias=0.0, scale=1.0),
             inc=("bgp", 1), waits=[("lba", 48), ("bgp", ctx.v("bgp"))])
    ctx.emit("tensor", lambda e: e.matmul(bps[:, 0:6], onesf[0:1, 0:128], ealog[0:1, 0:6],
                                          start=True, stop=True),
             inc=("bgp", 1), waits=[("bgp", ctx.v("bgp"))])
    ctx.emit("vector", lambda e: e.tensor_scalar_mul(ealog[:, 0:6], bps[:, 0:6], -1.0),
             inc=("bgp", 1), waits=[("bgp", ctx.v("bgp"))])
    bgpv = ctx.v("bgp")
    for tt in range(NT):
        def bfn(e, tt=tt):
            for d in range(ND):
                e.matmul(bps[:, 0:18], xn_fm[:, d * 2176 + 128 + tt * 128:
                                             d * 2176 + 128 + (tt + 1) * 128],
                         bab[:, d * 18:(d + 1) * 18], start=(d == 0), stop=False)
            return e.matmul(bps[:, 0:18], ones_sb[0:1, 0:128], dtb[0:1, :],
                            start=False, stop=True)
        ctx.emit("tensor", bfn, inc=("bmm", 1),
                 waits=[("lba", 48), fmr, ("bdn", 3 * tt), ("bsi", ctx.v("bsi")),
                        ("bga", ctx.v("bga")), ("bgp", bgpv)])
        bmv = ctx.v("bmm")
        ctx.emit("scalar", lambda e: e.activation(out=bsg[:, 0:12], in_=bps[:, 0:12],
                                                  func=AF.Sigmoid, bias=0.0, scale=1.0),
                 inc=("bsi", 1), waits=[("bmm", bmv), ("bst", 16 * 3 * tt)])
        ctx.emit("scalar", lambda e: e.activation(out=gtmp[:], in_=bps[:, 12:18],
                                                  func=AF.Abs, bias=0.0, scale=1.0),
                 inc=("bga", 1), waits=[("bmm", bmv), ("bdn", 3 * tt),
                                        ("bst", 16 * 3 * tt)])
        ctx.emit("scalar", lambda e: e.activation(out=gtmp[:], in_=gtmp[:],
                                                  func=AF.Exp, bias=0.0, scale=-1.0),
                 inc=("bga", 1), waits=[("bga", ctx.v("bga"))])
        ctx.emit("scalar", lambda e: e.activation(out=gtmp[:], in_=gtmp[:],
                                                  func=AF.Ln, bias=onecol[:, 0:1], scale=1.0),
                 inc=("bga", 1), waits=[("bga", ctx.v("bga")), ("bgp", bgpv)])
        ctx.emit("vector", lambda e: e.tensor_scalar_max(gtm2[:], bps[:, 12:18], 0.0),
                 inc=("bdn", 1), waits=[("bmm", bmv), ("bst", 16 * 3 * tt)])
        ctx.emit("vector", lambda e: e.tensor_add(gtmp[:], gtmp[:], gtm2[:]),
                 inc=("bdn", 1), waits=[("bdn", ctx.v("bdn")), ("bga", ctx.v("bga"))])
        ctx.emit("vector", lambda e: e.tensor_mul(gtmp[:], gtmp[:], ealog[:, 0:6]),
                 inc=("bdn", 1), waits=[("bdn", ctx.v("bdn"))])
        bdv = ctx.v("bdn")
        ctx.dma("gpsimd", b0d[tt * 128:(tt + 1) * 128, :], bsg[:, 0:6], "bst",
                waits=[("bsi", ctx.v("bsi"))])
        ctx.dma("gpsimd", b1d[tt * 128:(tt + 1) * 128, :], bsg[:, 6:12], "bst",
                waits=[("bsi", ctx.v("bsi"))])
        ctx.dma("gpsimd", gd[tt * 128:(tt + 1) * 128, :], gtmp[:], "bst",
                waits=[("bdn", bdv)])

    allproj = fins["pq"]
    cq = emit_conv_silu(ctx, "cs", qraw, cqw, qcv, 1536, bufpre="cb", ready=fins["pq"])
    ck0 = emit_conv_silu(ctx, "cs", kraw0, ckw0, kcv0, 1536, bufpre="cb", ready=fins["pk0"])
    ck1 = emit_conv_silu(ctx, "cs", kraw1, ckw1, kcv1, 1536, bufpre="cb", ready=fins["pk1"])
    cv0 = emit_conv_silu(ctx, "cs", vraw0, cvw0, vcv0, 3072, bufpre="cb", ready=fins["pv0"])
    cv1 = emit_conv_silu(ctx, "cs", vraw1, cvw1, vcv1, 3072, bufpre="cb", ready=fins["pv1"])

    nq = emit_l2norm_fm(ctx, "nr", qcv, qfn, onesf, scale=1.0 / 16.0, bufpre="nb", ready=[cq])
    nk0 = emit_l2norm_fm(ctx, "nr", kcv0, kfn0, onesf, bufpre="nb", ready=[ck0])
    nk1 = emit_l2norm_fm(ctx, "nr", kcv1, kfn1, onesf, bufpre="nb", ready=[ck1])

    v0 = emit_fm_to_il(ctx, "il", vcv0, v0t, 0, ident_sb, 3072, bufpre="ib", ready=[cv0])
    v1 = emit_fm_to_il(ctx, "il", vcv1, v1t, 1, ident_sb, 3072, bufpre="ib", ready=[cv1])

    ctx.wait("gpsimd", [nk1, v1, ggfin[0], ggfin[1], ("bst", ctx.v("bst"))])
    return ctx



def build_l1b(dbg=False, ctx=None, io=None, emit_out=True):
    ctx = ctx or Ctx()
    io = io or {}
    qfn = _io(ctx, io, "in", "QFN", [1536, TPC], BF16)
    kfn = [_io(ctx, io, "in", "KFN0", [1536, TPC], BF16), _io(ctx, io, "in", "KFN1", [1536, TPC], BF16)]
    vt = [_io(ctx, io, "in", "V0T", [TPC, 3072], BF16), _io(ctx, io, "in", "V1T", [TPC, 3072], BF16)]
    bd = [_io(ctx, io, "in", "B0", [TPC, 6]), _io(ctx, io, "in", "B1", [TPC, 6])]
    gdr = _io(ctx, io, "in", "G", [TPC, 6])
    s0in = _io(ctx, io, "in", "S0", [6, 256, 512])
    od = _io(ctx, io, "out", "OD", [TPC, 3072])
    sf = _io(ctx, io, "out", "SF", [6, 256, 512])
    if dbg:
        dbd = {nm: _io(ctx, io, "out", "DBG_" + nm, shp)
               for nm, shp in [("cgs6", [128, 6]), ("g0", [128, 128]), ("nmat", [128, 128]),
                               ("zmat", [128, 128]), ("rr", [128, 512]), ("gq", [128, 64]),
                               ("cgrow", [1, 768])]}
        dbb = {nm: _io(ctx, io, "out", "DBB_" + nm, shp, BF16)
               for nm, shp in [("ub", [128, 512]), ("afin", [128, 64]), ("kfm", [128, 256]),
                               ("qfm", [128, 128])]}

    def dump(names_bufs):
        for nm, buf in names_bufs:
            tgt = dbd.get(nm)
            if tgt is None:
                tgt = dbb[nm]
            ctx.dma("gpsimd", tgt.rearrange("a b -> a b"), buf, "dbg",
                    waits=[("sq", ctx.v("sq"))])

    H, NCH = 6, 32
    step = np.where(np.arange(128) < 64, 2 * np.arange(128), 2 * (np.arange(128) - 64) + 1)
    slmB = np.where(step[None, :] < step[:, None], 0.0, BIG).astype(np.float32)
    tt_ = np.arange(64)[None, :]
    qmB = np.where(step[:, None] <= 2 * tt_ + 1, 0.0, BIG).astype(np.float32)
    uts_np = np.zeros((128, 128), dtype=np.float32)
    m = np.arange(64)[:, None]
    n = np.arange(128)[None, :]
    uts_np[:64] = (m <= np.where(n < 64, n, n - 64)).astype(np.float32)
    consts = [("uts", uts_np), ("slmB", slmB), ("qmB", qmB),
              ("id2s", 2.0 * np.eye(128, dtype=np.float32)),
              ("idfs", np.eye(128, dtype=np.float32)),
              ("idbs", np.eye(128, dtype=ml_dtypes.bfloat16)),
              ("onesf1", np.ones((1, 128), dtype=np.float32))]
    sbs = {}
    for nm, arr in consts:
        dr = ctx.const("l1b_" + nm, np.ascontiguousarray(arr))
        sbs[nm] = ctx.sbuf(nm, list(arr.shape), BF16 if arr.dtype == ml_dtypes.bfloat16 else F32)
        ctx.dma("sync", sbs[nm][:], dr, "cl")
    uts, slmBs, qmBs = sbs["uts"], sbs["slmB"], sbs["qmB"]
    id2s, idfs, idbs, onesf = sbs["id2s"], sbs["idfs"], sbs["idbs"], sbs["onesf1"]
    clv = ctx.v("cl")

    sst = ctx.sbuf("sst", [128, H * 2 * 512], F32)
    sstb = ctx.sbuf("sstb", [128, H * 2 * 512], BF16)
    for h in range(H):
        for dk in range(2):
            ctx.dma("sync", sst[:, (h * 2 + dk) * 512:(h * 2 + dk + 1) * 512],
                    s0in[h, dk * 128:(dk + 1) * 128, :], "cls")
    clv2 = ctx.v("cls")
    for h in range(H):
        for dk in range(2):
            ctx.emit("vector", lambda e, h=h, dk=dk: e.tensor_copy(
                out=sstb[:, (h * 2 + dk) * 512:(h * 2 + dk + 1) * 512],
                in_=sst[:, (h * 2 + dk) * 512:(h * 2 + dk + 1) * 512]),
                inc=("sq", 1), waits=[("cls", clv2), ("sq", ctx.v("sq"))])

    gc = ctx.sbuf("gc", [128, 6], F32)
    bcol = ctx.sbuf("bcol", [128, 1], F32)
    cgs6 = ctx.sbuf("cgs6", [128, 6], F32)
    cgrow = ctx.sbuf("cgrow", [1, H * 128], F32)
    kfm = ctx.sbuf("kfm", [128, 256], BF16)
    qfm = ctx.sbuf("qfm", [128, 128], BF16)
    vc = ctx.sbuf("vc", [128, 512], BF16)
    g0 = ctx.sbuf("g0", [128, 128], F32)
    nmat = ctx.sbuf("nmat", [128, 128], F32)
    zmat = ctx.sbuf("zmat", [128, 128], F32)
    zt = ctx.sbuf("zt", [128, 128], F32)
    qq = ctx.sbuf("qq", [128, 128], F32)
    rr = ctx.sbuf("rr", [128, 512], F32)
    tks = ctx.sbuf("tks", [128, 512], F32)
    ub = ctx.sbuf("ub", [128, 512], BF16)
    gq = ctx.sbuf("gq", [128, 64], F32)
    afin = ctx.sbuf("afin", [128, 64], BF16)
    gob = ctx.sbuf("gob", [128, 64], F32)
    grow = ctx.sbuf("grow", [1, 64], F32)
    qtl = ctx.sbuf("qtl", [128, 128], BF16)
    ktok = ctx.sbuf("ktok", [128, 256], BF16)
    obuf = ctx.sbuf("obuf", [128, 512], F32)
    scl = ctx.sbuf("scl", [128, 16], F32)
    pA = ctx.psum("pA", [128, 512])
    pC = ctx.psum("pC", [128, 512])
    pT = ctx.psum("pT", [128, 128])
    pTb = ctx.psum("pTb", [128, 1024], BF16)

    ctx.emit("vector", lambda e: e.memset(gc[:], 0.0), inc=("sq", 1),
             waits=[("sq", ctx.v("sq"))])

    def seq(lane, fn):
        ctx.emit(lane, fn, inc=("sq", 1), waits=[("sq", ctx.v("sq")), ("cl", clv)])

    for c in range(NCH):
        sq_c = ctx.v("sq")
        ctx.dma("sync", gc[0:64, :], gdr[c * 64:(c + 1) * 64, :], "gld",
                waits=[("sq", sq_c)])
        gldv = ctx.v("gld")
        ctx.wait("tensor", [("gld", gldv)])
        seq("tensor", lambda e: e.matmul(pT[:, 0:6], uts[:], gc[:], start=True, stop=True))
        seq("vector", lambda e: e.tensor_copy(out=cgs6[:], in_=pT[:, 0:6]))
        for h in range(H):
            seq("tensor", lambda e, h=h: e.transpose(pT[0:1, 0:128], cgs6[:, h:h + 1], idfs[:]))
            seq("vector", lambda e, h=h: e.tensor_copy(
                out=cgrow[0:1, h * 128:(h + 1) * 128], in_=pT[0:1, 0:128]))

        for h in range(H):
            u = c * H + h
            sq_u = ctx.v("sq")
            for dk in range(2):
                ctx.dma("gpsimd", kfm[:, dk * 128:dk * 128 + 64],
                        kfn[0][h * 256 + dk * 128:h * 256 + (dk + 1) * 128,
                               c * 64:(c + 1) * 64], "kld",
                        waits=([("sq", sq_u)] + ([("ost", 16 * u)] if emit_out else [])
                               if dk == 0 else []))
                ctx.dma("gpsimd", kfm[:, dk * 128 + 64:dk * 128 + 128],
                        kfn[1][h * 256 + dk * 128:h * 256 + (dk + 1) * 128,
                               c * 64:(c + 1) * 64], "kld")
                ctx.dma("gpsimd", qfm[:, dk * 64:(dk + 1) * 64],
                        qfn[h * 256 + dk * 128:h * 256 + (dk + 1) * 128,
                            c * 64:(c + 1) * 64], "kld")
            ctx.dma("gpsimd", vc[0:64, :], vt[0][c * 64:(c + 1) * 64, h * 512:(h + 1) * 512],
                    "kld")
            ctx.dma("gpsimd", vc[64:128, :], vt[1][c * 64:(c + 1) * 64, h * 512:(h + 1) * 512],
                    "kld")
            ctx.dma("gpsimd", bcol[0:64, :], bd[0][c * 64:(c + 1) * 64, h:h + 1], "kld",
                    allow_slow_non_contiguous=True)
            ctx.dma("gpsimd", bcol[64:128, :], bd[1][c * 64:(c + 1) * 64, h:h + 1], "kld",
                    allow_slow_non_contiguous=True)
            kldv = ctx.v("kld")
            for lane in ("tensor", "vector", "scalar"):
                ctx.wait(lane, [("kld", kldv)])

            cgc = cgs6[:, h:h + 1]
            cgr = cgrow[0:1, h * 128:(h + 1) * 128]
            cgodd = cgr[0:1, 64:128]
            sA = sst[:, h * 1024:h * 1024 + 512]
            sB = sst[:, h * 1024 + 512:h * 1024 + 1024]
            sbA = sstb[:, h * 1024:h * 1024 + 512]
            sbB = sstb[:, h * 1024 + 512:h * 1024 + 1024]

            seq("tensor", lambda e, cgr=cgr: (
                e.matmul(pA[:, 0:128], kfm[:, 0:128], kfm[:, 0:128], start=True, stop=False),
                e.matmul(pA[:, 0:128], kfm[:, 128:256], kfm[:, 128:256], start=False, stop=True),
                e.matmul(pT[:], onesf[:], cgr, start=True, stop=True))[-1])
            seq("vector", lambda e: e.tensor_add(pT[:], pT[:], slmBs[:]))
            seq("scalar", lambda e, cgc=cgc: e.activation(out=g0[:], in_=pT[:], func=AF.Exp,
                                                 bias=cgc, scale=-1.0))
            seq("vector", lambda e: e.scalar_tensor_tensor(
                out=nmat[:], in0=pA[:, 0:128], scalar=bcol[:], in1=g0[:],
                op0=ALU.mult, op1=ALU.mult))
            seq("tensor", lambda e: e.transpose(pT[:], nmat[:], idfs[:]))
            seq("vector", lambda e: e.scalar_tensor_tensor(
                out=zmat[:], in0=pT[:], scalar=-1.0, in1=idfs[:], op0=ALU.mult, op1=ALU.add))
            for it in range(6):
                seq("tensor", lambda e: (
                    e.matmul(pA[:, 0:128], nmat[:], zmat[:], start=True, stop=False),
                    e.matmul(pA[:, 0:128], idfs[:], zmat[:], start=False, stop=True))[-1])
                seq("vector", lambda e: e.scalar_tensor_tensor(
                    out=qq[:], in0=pA[:, 0:128], scalar=-1.0, in1=id2s[:],
                    op0=ALU.mult, op1=ALU.add))
                seq("tensor", lambda e: e.transpose(pT[:], zmat[:], idfs[:]))
                seq("vector", lambda e: e.tensor_copy(out=zt[:], in_=pT[:]))
                seq("tensor", lambda e: e.matmul(pA[:, 0:128], zt[:], qq[:],
                                                 start=True, stop=True))
                seq("vector", lambda e: e.tensor_copy(out=zmat[:], in_=pA[:, 0:128]))
            seq("tensor", lambda e, sbA=sbA, sbB=sbB: (
                e.matmul(pC[:, 0:512], kfm[:, 0:128], sbA, start=True, stop=False),
                e.matmul(pC[:, 0:512], kfm[:, 128:256], sbB, start=False, stop=True))[-1])
            seq("scalar", lambda e, cgc=cgc: e.activation(out=scl[:, 0:1], in_=cgc, func=AF.Exp,
                                                 bias=0.0, scale=1.0))
            seq("vector", lambda e: e.tensor_mul(scl[:, 1:2], scl[:, 0:1], bcol[:]))
            seq("vector", lambda e: e.tensor_scalar_mul(tks[:], pC[:, 0:512], scl[:, 1:2]))
            seq("vector", lambda e: e.scalar_tensor_tensor(
                out=rr[:], in0=vc[:], scalar=bcol[:], in1=tks[:],
                op0=ALU.mult, op1=ALU.subtract))
            seq("tensor", lambda e: e.matmul(pC[:, 0:512], zmat[:], rr[:],
                                             start=True, stop=True))
            seq("vector", lambda e: e.tensor_copy(out=ub[:], in_=pC[:, 0:512]))
            if emit_out:
                seq("tensor", lambda e, cgodd=cgodd: (
                    e.matmul(pA[:, 0:64], kfm[:, 0:128], qfm[:, 0:64], start=True, stop=False),
                    e.matmul(pA[:, 0:64], kfm[:, 128:256], qfm[:, 64:128], start=False, stop=True),
                    e.matmul(pT[:, 0:64], onesf[:], cgodd, start=True, stop=True))[-1])
                seq("vector", lambda e: e.tensor_sub(pT[:, 0:64], pT[:, 0:64], qmBs[:]))
                seq("vector", lambda e, cgc=cgc: e.tensor_scalar_mul(scl[:, 2:3], cgc, -1.0))
                seq("scalar", lambda e: e.activation(out=gq[:], in_=pT[:, 0:64], func=AF.Exp,
                                                     bias=scl[:, 2:3], scale=1.0))
                seq("vector", lambda e: e.tensor_mul(afin[:], pA[:, 0:64], gq[:]))
                seq("scalar", lambda e, cgodd=cgodd: e.activation(out=grow[:], in_=cgodd,
                                                     func=AF.Exp, bias=0.0, scale=1.0))
                seq("tensor", lambda e: e.matmul(pT[:, 0:64], onesf[:], grow[:],
                                                 start=True, stop=True))
                seq("vector", lambda e: e.tensor_copy(out=gob[:], in_=pT[:, 0:64]))
                seq("vector", lambda e: e.tensor_mul(qtl[:, 0:64], qfm[:, 0:64], gob[:]))
                seq("vector", lambda e: e.tensor_mul(qtl[:, 64:128], qfm[:, 64:128], gob[:]))
                seq("tensor", lambda e, sbA=sbA, sbB=sbB: (
                    e.matmul(pC[0:64, 0:512], qtl[:, 0:64], sbA, start=True, stop=False),
                    e.matmul(pC[0:64, 0:512], qtl[:, 64:128], sbB, start=False, stop=False),
                    e.matmul(pC[0:64, 0:512], afin[:], ub[:], start=False, stop=True))[-1])
                seq("vector", lambda e: e.tensor_copy(out=obuf[0:64, :], in_=pC[0:64, 0:512]))
                ctx.dma("gpsimd", od[c * 64:(c + 1) * 64, h * 512:(h + 1) * 512],
                        obuf[0:64, :], "ost", waits=[("sq", ctx.v("sq"))])
            seq("tensor", lambda e: (
                e.transpose(pTb[:, 0:128], kfm[:, 0:128], idbs[:]),
                e.transpose(pTb[:, 128:256], kfm[:, 128:256], idbs[:]))[-1])
            seq("vector", lambda e: e.tensor_copy(out=ktok[:], in_=pTb[:, 0:256]))
            seq("tensor", lambda e, cgr=cgr: e.matmul(pT[:, 0:1], onesf[:], cgr[0:1, 127:128],
                                             start=True, stop=True))
            seq("vector", lambda e: e.tensor_copy(out=scl[:, 3:4], in_=pT[:, 0:1]))
            seq("vector", lambda e, cgc=cgc: e.tensor_sub(scl[:, 4:5], scl[:, 3:4], cgc))
            seq("scalar", lambda e: e.activation(out=scl[:, 5:6], in_=scl[:, 4:5],
                                                 func=AF.Exp, bias=0.0, scale=1.0))
            seq("scalar", lambda e: e.activation(out=scl[:, 6:7], in_=scl[:, 3:4],
                                                 func=AF.Exp, bias=0.0, scale=1.0))
            seq("vector", lambda e: e.tensor_scalar_mul(ktok[:], ktok[:], scl[:, 5:6]))
            if dbg and c == 0 and h == 0:
                dump([("cgs6", cgs6[:]), ("g0", g0[:]), ("nmat", nmat[:]),
                      ("zmat", zmat[:]), ("rr", rr[:]), ("gq", gq[:]),
                      ("cgrow", cgrow[0:1, :]), ("ub", ub[:]), ("afin", afin[:]),
                      ("kfm", kfm[:]), ("qfm", qfm[:])])
                ctx.wait("gpsimd", [("dbg", ctx.v("dbg"))])
            for dk in range(2):
                seq("tensor", lambda e, dk=dk: e.matmul(
                    pC[:, 0:512], ktok[:, dk * 128:(dk + 1) * 128], ub[:],
                    start=True, stop=True))
                seq("vector", lambda e, dk=dk, sX=(sA, sB)[dk]: e.scalar_tensor_tensor(
                    out=sX, in0=sX, scalar=scl[:, 6:7], in1=pC[:, 0:512],
                    op0=ALU.mult, op1=ALU.add))
                seq("vector", lambda e, dk=dk, sX=(sA, sB)[dk], sbX=(sbA, sbB)[dk]:
                    e.tensor_copy(out=sbX, in_=sX))
    sqf = ctx.v("sq")
    for h in range(H):
        for dk in range(2):
            ctx.dma("gpsimd", sf[h, dk * 128:(dk + 1) * 128, :],
                    sst[:, (h * 2 + dk) * 512:(h * 2 + dk + 1) * 512], "ost",
                    waits=[("sq", sqf)])
    ctx.wait("gpsimd", [("ost", ctx.v("ost"))])
    return ctx


def build_l1c(ctx=None, io=None):
    ctx = ctx or Ctx()
    io = io or {}
    odd = _io(ctx, io, "in", "OD", [TPC, 3072])
    ggd = _io(ctx, io, "in", "GG", [TPC, 3072])
    onw = _io(ctx, io, "in", "onw", [512])
    wo = _io(ctx, io, "in", "wo", [3072, D])
    x = _io(ctx, io, "in", "xin", [TPC, D])
    x3 = _io(ctx, io, "out", "X3", [TPC, D])
    o2d = ctx.dram_tmp("O2", [TPC, 3072])
    ao = ctx.dram_tmp("AO", [TPC, D])
    ident_sb, ones_sb = setup_common(ctx)
    onesf = ctx.sbuf("onesf", [1, 128], F32)
    onesfd = ctx.const("onesf_l1c", np.ones((1, 128), dtype=np.float32))
    ctx.dma("sync", onesf[:], onesfd, "lof")

    nwb = ctx.sbuf("nwb", [128, 512], F32)
    nwsb = ctx.sbuf("nwsb", [1, 512], F32)
    nps = ctx.psum("nps", [128, 512])
    ctx.dma("sync", nwsb[:], onw.rearrange("(o f) -> o f", o=1), "lof")
    ctx.emit("tensor", lambda e: e.matmul(nps[:], onesf[:], nwsb[:], start=True, stop=True),
             inc=("nwm", 1), waits=[("lof", 32)])
    ctx.emit("vector", lambda e: e.tensor_copy(out=nwb[:], in_=nps[:]),
             inc=("nwe", 1), waits=[("nwm", 1)])

    ot = ctx.sbuf("ot", [128, 3072], F32)
    gt = ctx.sbuf("gt", [128, 3072], F32)
    sg = ctx.sbuf("sg", [128, 3072], F32)
    st6 = ctx.sbuf("st6", [128, 16], F32)
    for tt in range(NT):
        ctx.dma("sync", ot[:], odd[tt * 128:(tt + 1) * 128, :], "old",
                waits=[("ost", 16 * tt)])
        ctx.dma("sync", gt[:], ggd[tt * 128:(tt + 1) * 128, :], "old")
        olv = ctx.v("old")

        def step(lane, fn):
            ctx.emit(lane, fn, inc=("oc", 1), waits=[("oc", ctx.v("oc")), ("old", olv),
                                                     ("nwe", 1)])
        step("scalar", lambda e: e.activation(out=sg[:], in_=ot[:], func=AF.Square,
                                              bias=0.0, scale=1.0))
        step("vector", lambda e: e.tensor_reduce(
            out=st6[:, 0:6], in_=sg[:].rearrange("p (h v) -> p h v", h=6),
            axis=AX.X, op=ALU.add))
        step("vector", lambda e: e.tensor_scalar(st6[:, 0:6], st6[:, 0:6], 1.0 / 512.0, 1e-5,
                                                 op0=ALU.mult, op1=ALU.add))
        step("scalar", lambda e: e.activation(out=st6[:, 0:6], in_=st6[:, 0:6], func=AF.Sqrt,
                                              bias=0.0, scale=1.0))
        step("vector", lambda e: e.reciprocal(out=st6[:, 0:6], in_=st6[:, 0:6]))
        for h in range(6):
            step("vector", lambda e, h=h: e.tensor_scalar_mul(
                ot[:, h * 512:(h + 1) * 512], ot[:, h * 512:(h + 1) * 512],
                st6[:, h:h + 1]))
            step("vector", lambda e, h=h: e.tensor_mul(
                ot[:, h * 512:(h + 1) * 512], ot[:, h * 512:(h + 1) * 512], nwb[:]))
        step("scalar", lambda e: e.activation(out=sg[:], in_=gt[:], func=AF.Sigmoid,
                                              bias=0.0, scale=1.0))
        step("vector", lambda e: e.tensor_mul(ot[:], ot[:], gt[:]))
        step("vector", lambda e: e.tensor_mul(ot[:], ot[:], sg[:]))
        ctx.dma("gpsimd", o2d[tt * 128:(tt + 1) * 128, :], ot[:], "ost",
                waits=[("oc", ctx.v("oc"))])
    ostv = ctx.v("ost")

    CH = 512
    cfm = ctx.sbuf("cfm", [128, 24 * CH], BF16)
    pfin = []
    for c in range(TPC // CH):
        ctr = emit_cast_transpose(ctx, "ct", o2d, cfm, ident_sb, n_tt=CH // 128, nd=24,
                                  src_row0=c * CH, fm_stride=CH,
                                  wait_extra=[("ost", ostv), ("pmm", ctx.v("pmm"))])
        pfin = emit_proj(ctx, "p", cfm, wo, ao, n_tok=CH, nd=24, F=D,
                         fm_ready=ctr, psA="nps", psB="pB2",
                         out_stage_dtype=F32, out_row0=c * CH)
    lfin = emit_ln_add(ctx, "le", ao, x, x3, wait_extra=pfin)
    ctx.wait("gpsimd", [lfin])
    return ctx



def l2_rope_tables(pos):
    half = 64
    inv = 1.0 / (10000.0 ** (np.arange(half, dtype=np.float64) * 2.0 / 128.0))
    ang = pos[None, :].astype(np.float64) * inv[:, None]
    cos = np.cos(ang).astype(np.float32)
    sin = np.sin(ang).astype(np.float32)
    return np.concatenate([cos, cos], 0), np.concatenate([sin, sin], 0)


def l2_klocal_pos(s):
    p = np.zeros(2688, np.int64)
    p[0:4] = np.arange(4)
    p[128:640] = (s * 2048 - 512) + np.arange(512) if s == 1 else np.arange(512)
    p[640:] = s * 2048 + np.arange(2048)
    return p


def l2_masks(s):
    out = np.zeros((16, 128, 648), np.float32)
    ql = np.arange(128)[:, None]
    for j in range(16):
        m = out[j]
        kl = np.arange(128)[None, :]
        m[:, 0:128] = np.where(kl >= ql, 0.0, -BIG)
        m[:, 512:640] = np.where(kl <= ql, 0.0, -BIG)
        w0 = 640 + (j - 4) * 128
        for sb in range(5):
            r0 = w0 + sb * 128
            if s == 0:
                valid = r0 >= 640
            else:
                valid = r0 >= 128
            if not valid:
                m[:, sb * 128:(sb + 1) * 128] = -BIG
        m[:, 644:648] = -BIG
        if s == 0:
            if j <= 3:
                m[:, 640:644] = -BIG
            elif j == 4:
                ks = np.arange(4)[None, :]
                m[:, 640:644] = np.where(ks < ql, 0.0, -BIG)
    return out.astype(ml_dtypes.bfloat16)


def build_l1d(ctx=None, io=None):
    ctx = ctx or Ctx()
    io = io or {}
    x = _io(ctx, io, "in", "xin", [TPC, D])
    wt = _io(ctx, io, "in", "wt", [D, D])
    sw = _io(ctx, io, "in", "sw", [D])
    bias = _io(ctx, io, "in", "bias", [D])
    xo = _io(ctx, io, "out", "XO", [TPC, D])
    wsc = ctx.dram_tmp("WSC", [D, D], BF16)
    ao = ctx.dram_tmp("AO", [TPC, D])
    ident_sb, ones_sb = setup_common(ctx)
    onesfd = ctx.const("onesf_l1d", np.ones((1, 128), dtype=np.float32))
    onesf = ctx.sbuf("onesf", [1, 128], F32)
    swr = ctx.sbuf("swr", [1, D], F32)
    swb = ctx.sbuf("swb", [128, D], F32)
    sps = ctx.psum("pS", [128, 512])
    ctx.dma("sync", onesf[:], onesfd, "lsc")
    ctx.dma("sync", swr[:], sw.rearrange("(o f) -> o f", o=1), "lsc")
    for i in range(4):
        ctx.emit("tensor", lambda e, i=i: e.matmul(sps[:, 0:512], onesf[:],
                                                   swr[0:1, i * 512:(i + 1) * 512],
                                                   start=True, stop=True),
                 inc=("scm", 1), waits=[("lsc", 32), ("sce", ctx.v("sce"))])
        ctx.emit("vector", lambda e, i=i: e.tensor_copy(out=swb[:, i * 512:(i + 1) * 512],
                                                        in_=sps[:, 0:512]),
                 inc=("sce", 1), waits=[("scm", ctx.v("scm"))])
    scev = ctx.v("sce")
    wf = ctx.sbuf("wf", [128, D], F32)
    wb = ctx.sbuf("wb", [128, D], BF16)
    for i in range(NT):
        ctx.dma("sync", wf[:], wt[i * 128:(i + 1) * 128, :], "wld",
                waits=[("wmu", ctx.v("wmu"))])
        ctx.emit("vector", lambda e: e.tensor_mul(wb[:], wf[:], swb[:]),
                 inc=("wmu", 1), waits=[("wld", ctx.v("wld")), ("sce", scev),
                                        ("wst", 16 * i)])
        ctx.dma("gpsimd", wsc[i * 128:(i + 1) * 128, :], wb[:], "wst",
                waits=[("wmu", ctx.v("wmu"))])
    wstv = ctx.v("wst")
    cfm = ctx.sbuf("cfm", [128, ND * 512], BF16)
    pfin = []
    for c in range(4):
        fmr = emit_ln_to_fm(ctx, "ln", x, cfm, ident_sb, n_tt=4, x_row0=c * 512,
                            fm_stride=512, wait_extra=[("pmm", ctx.v("pmm"))])
        pfin = emit_proj(ctx, "p", cfm, wsc, ao, n_tok=512, F=D, bias_dram=bias,
                         ones_sb=ones_sb, fm_ready=fmr, psA="pA", psB="pB",
                         out_stage_dtype=F32, out_row0=c * 512,
                         wait_extra=[("wst", wstv)])
    lfin = emit_ln_add(ctx, "le", ao, x, xo, wait_extra=pfin)
    ctx.wait("gpsimd", [lfin])
    return ctx


def build_l2a(ctx=None, io=None):
    ctx = ctx or Ctx()
    io = io or {}
    x = _io(ctx, io, "in", "xin", [TPC, D])
    xsib = _io(ctx, io, "in", "xsib", [640, D])
    wq = _io(ctx, io, "in", "wq", [D, D])
    bq = _io(ctx, io, "in", "bq", [D])
    wk = _io(ctx, io, "in", "wk", [D, 512])
    bk = _io(ctx, io, "in", "bk", [512])
    wv = _io(ctx, io, "in", "wv", [D, 512])
    bv = _io(ctx, io, "in", "bv", [512])
    wo = _io(ctx, io, "in", "wo", [D, D])
    bo = _io(ctx, io, "in", "bo", [D])
    msk = _io(ctx, io, "in", "l2masks", [16, 128, 648], BF16)
    qcosd = _io(ctx, io, "in", "qcos", [128, TPC])
    qsind = _io(ctx, io, "in", "qsin", [128, TPC])
    kcosd = _io(ctx, io, "in", "kcos", [128, 2688])
    ksind = _io(ctx, io, "in", "ksin", [128, 2688])
    x4 = _io(ctx, io, "out", "X4", [TPC, D])
    xnfm_d = ctx.dram_tmp("XNFM", [D, 2688], BF16)
    ktd = ctx.dram_tmp("KT", [512, 2688], BF16)
    qtd = ctx.dram_tmp("QT", [D, TPC], BF16)
    vtd = ctx.dram_tmp("VT", [2688, 512], BF16)
    atd = ctx.dram_tmp("AT", [TPC, D], BF16)
    ao = ctx.dram_tmp("AO", [TPC, D])
    ident_sb, ones_sb = setup_common(ctx)
    pwA = ctx.sbuf("pwA", [128, 2 * ND * 512], BF16)

    xn_fm_t = ctx.sbuf("xn_fm_t", [128, ND * 128], BF16)
    for tt in range(5):
        f1 = emit_ln_to_fm(ctx, "ln", xsib, xn_fm_t, ident_sb, n_tt=1, x_row0=tt * 128,
                           fm_stride=128, wait_extra=[("fmst", 16 * tt)])
        ctx.dma("sync", xnfm_d[:, tt * 128:(tt + 1) * 128].rearrange("(dc p) f -> p dc f", p=128),
                xn_fm_t[:].rearrange("p (dc f) -> p dc f", dc=ND), "fmst", waits=[f1])
    for tt in range(NT):
        f1 = emit_ln_to_fm(ctx, "ln", x, xn_fm_t, ident_sb, n_tt=1, x_row0=tt * 128,
                           fm_stride=128, wait_extra=[("fmst", 16 * (5 + tt))])
        ctx.dma("sync", xnfm_d[:, 640 + tt * 128:640 + (tt + 1) * 128]
                .rearrange("(dc p) f -> p dc f", p=128),
                xn_fm_t[:].rearrange("p (dc f) -> p dc f", dc=ND), "fmst", waits=[f1])
    fmstv = ctx.v("fmst")

    xnc = ctx.sbuf("xnc", [128, 2 * ND * 512], BF16)
    chunks = [(0, 512), (512, 512), (1024, 512), (1536, 512), (2048, 512), (2560, 128)]
    prevs = []
    for ci, (c0, cw) in enumerate(chunks):
        par = ci % 2
        ctx.dma("sync", xnc[:, par * ND * 512:(par + 1) * ND * 512]
                .rearrange("p (dc f) -> p dc f", f=512)[:, :, 0:cw],
                xnfm_d[:, c0:c0 + cw].rearrange("(dc p) f -> p dc f", p=128),
                "xcl%d" % par,
                waits=[("fmst", fmstv)] + prevs)
        xcw = [("xcl%d" % par, 16 * (ci // 2 + 1))]
        fmr = (("xcl%d" % par, 16 * (ci // 2 + 1)))
        fb = par * ND * 512
        kfin = emit_proj(ctx, "pk", xnc, wk, ktd[:, c0:c0 + cw], n_tok=cw, fm_stride=512,
                         fm_base=fb, mode="fm", F=512, fm_ready=fmr, psA="pA", psB="pB",
                         wname="pwA", oname="poA", bias_fm_dram=bk,
                         wait_extra=prevs + [("pvev", ctx.v("pvev")), ("pkev", ctx.v("pkev"))])
        vfin = emit_proj(ctx, "pv", xnc, wv, vtd, n_tok=cw, fm_stride=512, fm_base=fb,
                         mode="tok", F=512, fm_ready=fmr, psA="pA", psB="pB",
                         wname="pwA", oname="poA", bias_dram=bv, ones_sb=ones_sb,
                         out_row0=c0, wait_extra=kfin + [("pkev", ctx.v("pkev"))])
        prevs = kfin + vfin
        if c0 >= 640 or True:
            pass
    prevq = prevs
    for ci in range(4):
        par = ci % 2
        c0 = 640 + ci * 512
        ctx.dma("sync", xnc[:, par * ND * 512:(par + 1) * ND * 512]
                .rearrange("p (dc f) -> p dc f", dc=ND),
                xnfm_d[:, c0:c0 + 512].rearrange("(dc p) f -> p dc f", p=128),
                "xq%d" % par, waits=[("fmst", fmstv)] + prevq)
        fmr = ("xq%d" % par, 16 * (ci // 2 + 1))
        qfin = emit_proj(ctx, "pq", xnc, wq, qtd[:, ci * 512:(ci + 1) * 512], n_tok=512,
                         fm_stride=512, fm_base=par * ND * 512, mode="fm", F=D,
                         fm_ready=fmr, psA="pA", psB="pB", wname="pwA", oname="poA",
                         bias_fm_dram=bq,
                         wait_extra=prevq + [("pqev", ctx.v("pqev")), ("pvev", ctx.v("pvev"))])
        prevq = qfin
    projfin = prevq + prevs

    kcs = ctx.sbuf("kcs", [128, 2688], F32)
    ksn = ctx.sbuf("ksn", [128, 2688], F32)
    rbuf = ctx.sbuf("rbuf", [128, 2688], BF16)
    rtmp = ctx.sbuf("rtmp", [128, 2688], F32)
    rtm2 = ctx.sbuf("rtm2", [128, 2688], F32)
    rtm2s = ctx.sbuf("ktr", [128, 4 * 2688], BF16)[:].bitcast(F32)[:, 0:2688]
    ctx.dma("sync", kcs[:], kcosd, "rld", waits=projfin)
    ctx.dma("sync", ksn[:], ksind, "rld")

    def rope_pass(pre, src, T, nheads, sem):
        rldv = ctx.v(sem)
        for hh in range(nheads):
            ctx.dma("sync", rbuf[:, 0:T], src[hh * 128:(hh + 1) * 128, 0:T], pre + "l",
                    waits=[(sem, rldv), (pre + "w", ctx.v(pre + "w"))])
            lv = ctx.v(pre + "l")

            def rstep(fn, extra=()):
                ctx.emit("vector", fn, inc=(pre + "c", 1),
                         waits=[(pre + "l", lv), (pre + "c", ctx.v(pre + "c"))] + list(extra))
            rstep(lambda e: e.tensor_mul(rtmp[:, 0:T], rbuf[:, 0:T], kcs[:, 0:T]))
            rstep(lambda e: e.tensor_mul(rtm2[:, 0:T], rbuf[:, 0:T], ksn[:, 0:T]),
                  extra=[(pre + "x", ctx.v(pre + "x"))])
            cv = ctx.v(pre + "c")
            ctx.dma("gpsimd", rtm2s[0:64, 0:T], rtm2[64:128, 0:T], pre + "x",
                    waits=[(pre + "c", cv)])
            ctx.dma("gpsimd", rtm2s[64:128, 0:T], rtm2[0:64, 0:T], pre + "x",
                    waits=[(pre + "c", cv)])
            xv = ctx.v(pre + "x")
            rstep(lambda e: e.tensor_sub(rtmp[0:64, 0:T], rtmp[0:64, 0:T],
                                         rtm2s[0:64, 0:T]), extra=[(pre + "x", xv)])
            rstep(lambda e: e.tensor_add(rtmp[64:128, 0:T], rtmp[64:128, 0:T],
                                         rtm2s[64:128, 0:T]))
            rstep(lambda e: e.tensor_copy(out=rbuf[:, 0:T], in_=rtmp[:, 0:T]))
            ctx.dma("gpsimd", src[hh * 128:(hh + 1) * 128, 0:T], rbuf[:, 0:T], pre + "w",
                    waits=[(pre + "c", ctx.v(pre + "c"))])
        return (pre + "w", ctx.v(pre + "w"))

    rk = rope_pass("rk", ktd, 2688, 4, "rld")
    ctx.dma("sync", kcs[:, 0:2048], qcosd, "rlq", waits=[rk])
    ctx.dma("sync", ksn[:, 0:2048], qsind, "rlq")
    rq = rope_pass("rq", qtd, 2048, 16, "rlq")

    ktr = ctx.sbuf("ktr", [128, 4 * 2688], BF16)
    vtr = ctx.sbuf("vtr", [128, 21 * 512], BF16)
    mskb = ctx.sbuf("mskb", [128, 16 * 648], BF16)
    qh = ctx._sbufs["rtmp"][:].bitcast(BF16)
    _kcsb = None
    psb = ctx._sbufs["kcs"][:].bitcast(BF16)[:, 0:768]
    ptb = ctx._sbufs["kcs"][:].bitcast(BF16)[:, 1024:1792]
    osb = ctx._sbufs["rtm2"][:].bitcast(BF16)[:, 0:2048]
    lst = ctx.sbuf("lst", [128, 8], F32)
    pS = ctx.psum("pS", [128, 512])
    pS2 = ctx.psum("pS2", [128, 136])
    pO = ctx.psum("pO", [128, 128])
    pTt = ctx.psum("pTt", [128, 1024], BF16)
    for g4 in range(4):
        ctx.dma("sync", ktr[:, g4 * 2688:(g4 + 1) * 2688],
                ktd[g4 * 128:(g4 + 1) * 128, :], "akl", waits=[rq])
    for vt_ in range(21):
        ctx.dma("sync", vtr[:, vt_ * 512:(vt_ + 1) * 512],
                vtd[vt_ * 128:(vt_ + 1) * 128, :], "akl")
    for j in range(16):
        ctx.dma("sync", mskb[:, j * 648:(j + 1) * 648], msk[j], "akl")
    aklv = ctx.v("akl")
    for lane in ("tensor", "vector", "scalar"):
        ctx.wait(lane, [("akl", aklv)])

    for j in range(16):
        for h in range(16):
            u = j * 16 + h
            up = u % 2
            g4 = h // 4
            ctx.dma("gpsimd", qh[:, up * 2048:up * 2048 + 128],
                    qtd[h * 128:(h + 1) * 128, j * 128:(j + 1) * 128], "aql%d" % up,
                    waits=[("amm", max(0, ctx.v("amm") - 1))])
            aqv = ("aql%d" % up, ctx.v("aql%d" % up))
            w0 = 640 + (j - 4) * 128

            def smm(e, g4=g4, up=up, w0=w0):
                e.matmul(pS[:], qh[:, up * 2048:up * 2048 + 128],
                         ktr[:, g4 * 2688 + w0:g4 * 2688 + w0 + 512],
                         start=True, stop=True)
                e.matmul(pS2[:, 0:128], qh[:, up * 2048:up * 2048 + 128],
                         ktr[:, g4 * 2688 + w0 + 512:g4 * 2688 + w0 + 640],
                         start=True, stop=False)
                return e.matmul(pS2[:, 128:136] if False else pS2[:, 0:136],
                                qh[:, up * 2048:up * 2048 + 128],
                                ktr[:, g4 * 2688 + 0:g4 * 2688 + 8],
                                start=False, stop=True)
            def smm2(e, g4=g4, up=up, w0=w0):
                e.matmul(pS[:], qh[:, up * 2048:up * 2048 + 128],
                         ktr[:, g4 * 2688 + w0:g4 * 2688 + w0 + 512],
                         start=True, stop=True)
                e.matmul(pS2[:, 0:128], qh[:, up * 2048:up * 2048 + 128],
                         ktr[:, g4 * 2688 + w0 + 512:g4 * 2688 + w0 + 640],
                         start=True, stop=True)
                return e.matmul(pS2[:, 128:136], qh[:, up * 2048:up * 2048 + 128],
                                ktr[:, g4 * 2688 + 0:g4 * 2688 + 8],
                                start=True, stop=True)
            ctx.emit("tensor", smm2, inc=("amm", 1),
                     waits=[aqv, ("aex", ctx.v("aex"))])
            amv = ctx.v("amm")

            def astep(fn, lane="vector"):
                ctx.emit(lane, fn, inc=("aex", 1), waits=[("aex", ctx.v("aex")),
                                                          ("amm", amv)])
            ctx.emit("vector", lambda e, j=j: e.tensor_add(
                pS[:], pS[:], mskb[:, j * 648:j * 648 + 512]),
                inc=("aex", 1), waits=[("aex", ctx.v("aex")), ("amm", amv),
                                       ("aoc", ctx.v("aoc"))])
            astep(lambda e, j=j: e.tensor_add(pS2[:, 0:136], pS2[:, 0:136],
                                              mskb[:, j * 648 + 512:(j + 1) * 648]))
            astep(lambda e: e.reduce_max(out=lst[:, 0:1], in_=pS[:], axis=AX.X))
            astep(lambda e: e.reduce_max(out=lst[:, 1:2], in_=pS2[:, 0:136], axis=AX.X))
            astep(lambda e: e.tensor_max(lst[:, 0:1], lst[:, 0:1], lst[:, 1:2]))
            astep(lambda e: e.tensor_scalar_mul(lst[:, 1:2], lst[:, 0:1], -0.08838834764831845))
            astep(lambda e: e.activation(out=psb[:, 0:512], in_=pS[:], func=AF.Exp,
                                         bias=lst[:, 1:2], scale=0.08838834764831845,
                                         accum_out=lst[:, 2:3]), "scalar")
            astep(lambda e: e.activation(out=psb[:, 512:648], in_=pS2[:, 0:136],
                                         func=AF.Exp, bias=lst[:, 1:2], scale=0.08838834764831845,
                                         accum_out=lst[:, 3:4]), "scalar")
            astep(lambda e: e.tensor_add(lst[:, 2:3], lst[:, 2:3], lst[:, 3:4]))
            astep(lambda e: e.reciprocal(out=lst[:, 4:5], in_=lst[:, 2:3]))
            exv = ctx.v("aex")

            def tpfn(e):
                for s5 in range(5):
                    e.transpose(pTt[:, s5 * 128:(s5 + 1) * 128],
                                psb[:, s5 * 128:(s5 + 1) * 128], ident_sb[:])
                return e.transpose(pTt[0:8, 640:768], psb[:, 640:648], ident_sb[:])
            ctx.emit("tensor", tpfn, inc=("atp", 1), waits=[("aex", exv)])
            tpv = ctx.v("atp")
            ctx.emit("vector", lambda e: e.tensor_copy(out=ptb[:, 0:640], in_=pTt[:, 0:640]),
                     inc=("aex", 1), waits=[("aex", ctx.v("aex")), ("atp", tpv),
                                            ("apm", ctx.v("apm"))])
            ctx.emit("vector", lambda e: e.tensor_copy(out=ptb[0:8, 640:768],
                                                       in_=pTt[0:8, 640:768]),
                     inc=("aex", 1), waits=[("aex", ctx.v("aex"))])

            def pvfn(e, j=j, g4=g4, w0=w0):
                base = (w0 // 128)
                for s5 in range(5):
                    e.matmul(pO[:], ptb[:, s5 * 128:(s5 + 1) * 128],
                             vtr[:, (base + s5) * 512 + g4 * 128:(base + s5) * 512 + (g4 + 1) * 128],
                             start=(s5 == 0), stop=False)
                return e.matmul(pO[:], ptb[0:8, 640:768], vtr[0:8, 0 * 512 + g4 * 128:0 * 512 + (g4 + 1) * 128],
                                start=False, stop=True)
            ctx.emit("tensor", pvfn, inc=("apm", 1),
                     waits=[("aex", ctx.v("aex")), ("aoc", ctx.v("aoc"))])
            ctx.emit("vector", lambda e, h=h: e.tensor_scalar_mul(
                osb[:, h * 128:(h + 1) * 128], pO[:], lst[:, 4:5]),
                inc=("aoc", 1), waits=[("apm", ctx.v("apm")), ("ast", 16 * j)])
        ctx.dma("gpsimd", atd[j * 128:(j + 1) * 128, :], osb[:], "ast",
                waits=[("aoc", ctx.v("aoc"))])
    astv = ctx.v("ast")

    ctx._sbufs["le_y"] = ctx._sbufs["kcs"][:, 0:2048]
    ctx._sbufs["le_x"] = ctx._sbufs["ksn"][:, 0:2048]
    ctx._sbufs["le_oo"] = ctx._sbufs["rtmp"][:, 0:2048]
    ctx._sbufs["le_junk"] = ctx._sbufs["rtm2"][:, 0:2048]
    cfm = ctx._sbufs["mskb"]
    ctx._sbufs["p_b"] = ctx._sbufs["ktr"][0:1, 0:2048]
    ctx._sbufs["p_oF"] = ctx._sbufs["vtr"][:].bitcast(F32)
    ctx._psums["ct_tp"] = ctx._psums["ln_tp"]
    ctx._psums["ct_tq"] = ctx._psums["ln_tq"]
    pfin = []
    for c in range(4):
        ctr = emit_cast_transpose(ctx, "ct", atd, cfm, ident_sb, n_tt=4, nd=ND,
                                  src_row0=c * 512, fm_stride=512,
                                  wait_extra=[("ast", astv), ("pmm", ctx.v("pmm"))],
                                  src_bf=True)
        pfin = emit_proj(ctx, "p", cfm, wo, ao, n_tok=512, F=D, bias_dram=bo,
                         ones_sb=ones_sb, fm_ready=ctr, psA="pA", psB="pB",
                         wname="pwA", oname="p_oF", out_stage_dtype=F32, out_row0=c * 512)
    lfin = emit_ln_add(ctx, "le", ao, x, x4, wait_extra=pfin)
    ctx.wait("gpsimd", [lfin])
    return ctx


def build_l2c(ctx=None, io=None):
    ctx = ctx or Ctx()
    io = io or {}
    x = _io(ctx, io, "in", "xin", [TPC, D])
    wct = _io(ctx, io, "in", "wct", [D, 4096])
    tau = _io(ctx, io, "in", "tau", [1])
    genes = _io(ctx, io, "in", "genes", [4096, 256])
    wdt = _io(ctx, io, "in", "wdt", [256, D])
    wut = _io(ctx, io, "in", "wut", [256, D])
    nw = _io(ctx, io, "in", "norm_w", [D])
    sclr = _io(ctx, io, "in", "scl", [1])
    x5 = _io(ctx, io, "out", "X5", [TPC, D])
    xnfm_d = ctx.dram_tmp("XNFM", [D, TPC], BF16)
    xn32_d = ctx.dram_tmp("XN32", [TPC, D])
    lg_d = ctx.dram_tmp("LG", [TPC, 4096])
    phi_d = ctx.dram_tmp("PHI", [TPC, 4096], BF16)
    rz_d = ctx.dram_tmp("RZ", [TPC, 1])
    ex_d = ctx.dram_tmp("EX", [TPC, 256])
    ident_sb, ones_sb = setup_common(ctx)
    onesf = ctx.sbuf("onesf", [1, 128], F32)
    onesfd = ctx.const("onesf_l2c", np.ones((1, 128), dtype=np.float32))
    ctx.dma("sync", onesf[:], onesfd, "lof")
    F32EPS = float(np.finfo(np.float32).eps)

    tsb = ctx.sbuf("tsb", [1, 2], F32)
    tcol = ctx.sbuf("tcol", [128, 2], F32)
    tps = ctx.psum("tps", [128, 512])
    ctx.dma("sync", tsb[0:1, 0:1], tau.rearrange("(o f) -> o f", o=1), "lof")
    ctx.dma("sync", tsb[0:1, 1:2], sclr.rearrange("(o f) -> o f", o=1), "lof")
    ctx.emit("vector", lambda e: e.tensor_scalar_max(tsb[0:1, 0:1], tsb[0:1, 0:1], 0.1),
             inc=("tpp", 1), waits=[("lof", 48)])
    ctx.emit("vector", lambda e: e.reciprocal(out=tsb[0:1, 0:1], in_=tsb[0:1, 0:1]),
             inc=("tpp", 1), waits=[("tpp", ctx.v("tpp"))])
    ctx.emit("tensor", lambda e: e.matmul(tps[:, 0:2], onesf[:], tsb[0:1, 0:2],
                                          start=True, stop=True),
             inc=("tpp", 1), waits=[("tpp", ctx.v("tpp"))])
    ctx.emit("vector", lambda e: e.tensor_copy(out=tcol[:], in_=tps[:, 0:2]),
             inc=("tpp", 1), waits=[("tpp", ctx.v("tpp"))])
    tppv = ctx.v("tpp")

    nwb = ctx.sbuf("nwb", [128, D], F32)
    nwr = ctx.sbuf("nwr", [1, D], F32)
    ctx.dma("sync", nwr[:], nw.rearrange("(o f) -> o f", o=1), "lnw")
    for i in range(4):
        ctx.emit("tensor", lambda e, i=i: e.matmul(tps[:, 0:512], onesf[:],
                                                   nwr[0:1, i * 512:(i + 1) * 512],
                                                   start=True, stop=True),
                 inc=("nwm", 1), waits=[("lnw", 16), ("tpp", tppv), ("nwe", ctx.v("nwe"))])
        ctx.emit("vector", lambda e, i=i: e.tensor_copy(out=nwb[:, i * 512:(i + 1) * 512],
                                                        in_=tps[:, 0:512]),
                 inc=("nwe", 1), waits=[("nwm", ctx.v("nwm"))])
    nwev = ctx.v("nwe")
    nwfm = ctx.sbuf("nwfm", [128, ND], F32)
    ctx.dma("gpsimd", nwfm[:], nw.rearrange("(dc p) -> p dc", p=128), "lnw2",
            allow_slow_non_contiguous=True)
    lnwv = ctx.v("lnw2")

    xn_fm_t = ctx.sbuf("xn_fm_t", [128, ND * 128], BF16)
    for tt in range(NT):
        f1 = emit_ln_to_fm(ctx, "ln", x, xn_fm_t, ident_sb, n_tt=1, x_row0=tt * 128,
                           fm_stride=128, rms=True, eps=F32EPS, xn32_dram=xn32_d,
                           wait_extra=[("fmst", 16 * tt), ("lno3", 16 * tt)])
        ctx.dma("sync", xnfm_d[:, tt * 128:(tt + 1) * 128].rearrange("(dc p) f -> p dc f", p=128),
                xn_fm_t[:].rearrange("p (dc f) -> p dc f", dc=ND), "fmst", waits=[f1])
    fmstv = ctx.v("fmst")
    lno3v = ctx.v("lno3")

    xnc = ctx.sbuf("xnc", [128, 2 * ND * 512], BF16)
    prevs = []
    for ci in range(4):
        par = ci % 2
        ctx.dma("sync", xnc[:, par * ND * 512:(par + 1) * ND * 512]
                .rearrange("p (dc f) -> p dc f", dc=ND),
                xnfm_d[:, ci * 512:(ci + 1) * 512].rearrange("(dc p) f -> p dc f", p=128),
                "xcl%d" % par, waits=[("fmst", fmstv)] + prevs)
        fmr = ("xcl%d" % par, 16 * (ci // 2 + 1))
        lfin = emit_proj(ctx, "pl", xnc, wct, lg_d, n_tok=512, fm_stride=512,
                         fm_base=par * ND * 512, mode="tok", F=4096, fm_ready=fmr,
                         psA="pA", psB="pB", wname="pwA", oname="poA",
                         out_stage_dtype=F32, out_row0=ci * 512,
                         w_scale_fm=nwfm, w_scale_rdy=("lnw2", lnwv), wait_extra=prevs)
        prevs = lfin
    lgfin = prevs

    at_ = ctx.sbuf("at_", [128, 4096], F32)
    ap_ = ctx.sbuf("ap_", [128, 4096], F32)
    gbuf = ctx.sbuf("gbuf", [128, 32], F32)
    phib = ctx.sbuf("phib", [128, 4096], BF16)
    sl2 = ctx.sbuf("sl2", [128, 8], F32)
    for tt in range(NT):
        ctx.dma("sync", at_[:], lg_d[tt * 128:(tt + 1) * 128, :], "tld",
                waits=[("tphi", ctx.v("tphi"))] + (lgfin if tt < 2 else []))
        tldv = ctx.v("tld")

        def ts(lane, fn):
            ctx.emit(lane, fn, inc=("tsq", 1), waits=[("tsq", ctx.v("tsq")),
                                                      ("tld", tldv), ("tpp", tppv)])
        src_buf = at_
        for rd in range(4):
            ts("vector", lambda e, rd=rd, sb=src_buf: e.max(out=gbuf[:, rd * 8:(rd + 1) * 8],
                                                            in_=sb[:]))
            ts("vector", lambda e, rd=rd, sb=src_buf: e.match_replace(
                out=ap_[:], in_to_replace=gbuf[:, rd * 8:(rd + 1) * 8],
                in_values=sb[:], imm_value=-BIG))
            src_buf = ap_
        ts("vector", lambda e: e.scalar_tensor_tensor(
            out=sl2[:, 0:1], in0=gbuf[:, 0:1], scalar=-1.0, in1=tcol[:, 0:1],
            op0=ALU.mult, op1=ALU.mult))
        ts("vector", lambda e: e.tensor_tensor(out=phib[:], in0=at_[:], in1=ap_[:],
                                               op=ALU.not_equal))
        ts("scalar", lambda e: e.activation(out=at_[:], in_=at_[:], func=AF.Exp,
                                            bias=sl2[:, 0:1], scale=tcol[:, 0:1]))
        ts("vector", lambda e: e.tensor_mul(phib[:], phib[:], at_[:]))
        ts("scalar", lambda e: e.activation(out=gbuf[:], in_=gbuf[:], func=AF.Exp,
                                            bias=sl2[:, 0:1], scale=tcol[:, 0:1],
                                            accum_out=sl2[:, 1:2]))
        ts("vector", lambda e: e.reciprocal(out=sl2[:, 2:3], in_=sl2[:, 1:2]))
        tsv = ctx.v("tsq")
        ctx.dma("gpsimd", phi_d[tt * 128:(tt + 1) * 128, :], phib[:], "tphi",
                waits=[("tsq", tsv)])
        ctx.dma("gpsimd", rz_d[tt * 128:(tt + 1) * 128, :], sl2[:, 2:3], "tphi",
                waits=[("tsq", tsv)])
    tphv = ctx.v("tphi")

    cfm = ctx.sbuf("cfm", [128, 32 * 512], BF16)
    gsb = ctx.sbuf("gsb", [128, 32 * 256], BF16)
    rzc = ctx.sbuf("rzc", [128, 4], F32)
    exs = ctx.sbuf("exs", [128, 256], F32)
    gpsA = ctx.psum("pA", [128, 512])
    ctx.dma("gpsimd", gsb[:].rearrange("p (dc f) -> p dc f", dc=32),
            genes.rearrange("(dc p) f -> p dc f", p=128), "ggl")
    for c in range(4):
        ctr = emit_cast_transpose(ctx, "ct", phi_d, cfm, ident_sb, n_tt=4, nd=32,
                                  src_row0=c * 512, fm_stride=512,
                                  wait_extra=[("tphi", tphv), ("gmm", ctx.v("gmm"))],
                                  src_bf=True)
        for st4 in range(4):
            ctx.dma("sync", rzc[:, st4:st4 + 1],
                    rz_d[c * 512 + st4 * 128:c * 512 + (st4 + 1) * 128, :], "rzl",
                    waits=[("gev", ctx.v("gev"))])
        rzlv = ctx.v("rzl")
        for st4 in range(4):
            def gfn(e, st4=st4):
                ins = None
                for g in range(32):
                    ins = e.matmul(gpsA[:, 0:256],
                                   cfm[:, g * 512 + st4 * 128:g * 512 + (st4 + 1) * 128],
                                   gsb[:, g * 256:(g + 1) * 256],
                                   start=(g == 0), stop=(g == 31))
                return ins
            ctx.emit("tensor", gfn, inc=("gmm", 1),
                     waits=[ctr, ("ggl", 16), ("gev", ctx.v("gev"))])
            ctx.emit("vector", lambda e, st4=st4: e.tensor_scalar_mul(
                exs[:], gpsA[:, 0:256], rzc[:, st4:st4 + 1]),
                inc=("gev", 1), waits=[("gmm", ctx.v("gmm")), ("rzl", rzlv),
                                       ("gst", 16 * (c * 4 + st4))])
            ctx.dma("gpsimd", ex_d[c * 512 + st4 * 128:c * 512 + (st4 + 1) * 128, :],
                    exs[:], "gst", waits=[("gev", ctx.v("gev"))])
    gstv = ctx.v("gst")

    wdsb = ctx._sbufs["xnc"][:, 0:4096]
    wusb = ctx._sbufs["xnc"][:, 4096:8192]
    ctx.dma("gpsimd", wdsb.rearrange("p (dc f) -> p dc f", dc=2),
            wdt.rearrange("(dc p) f -> p dc f", p=128), "lwd", waits=lgfin)
    ctx.dma("gpsimd", wusb.rearrange("p (dc f) -> p dc f", dc=2),
            wut.rearrange("(dc p) f -> p dc f", p=128), "lwd")
    exb = ctx._sbufs["phib"][:, 0:256]
    exf = ctx._sbufs["phib"][:, 256:512]
    dsb = ctx._sbufs["at_"][:, 0:2048]
    usb = ctx._sbufs["ap_"][:, 0:2048]
    ya = ctx.sbuf("ln_x", [128, D], F32)
    yb = ctx.sbuf("ln_junk", [128, D], F32)
    yst = ctx.sbuf("yst", [128, 8], F32)
    pTb2 = ctx.psum("ln_tp", [128, 1024], BF16)
    pB2 = ctx.psum("pB", [128, 512])
    for tt in range(NT):
        sq0 = ctx.v("fsq")
        ctx.dma("gpsimd", exb, ex_d[tt * 128:(tt + 1) * 128, :], "fldg",
                waits=[("fsq", sq0), ("gst", gstv), ("lwd", 32)])
        ctx.dma("sync", ya[:], xn32_d[tt * 128:(tt + 1) * 128, :], "fld",
                waits=[("lno3", lno3v), ("fso", 16 * tt)])
        fldv = ctx.v("fld")
        fldgv = ctx.v("fldg")

        def fs(lane, fn):
            ctx.emit(lane, fn, inc=("fsq", 1), waits=[("fsq", ctx.v("fsq")),
                                                      ("fld", fldv), ("fldg", fldgv),
                                                      ("nwe", nwev)])
        fs("tensor", lambda e: (e.transpose(pTb2[:, 0:128], exb[:, 0:128], ident_sb[:]),
                                e.transpose(pTb2[:, 128:256], exb[:, 128:256], ident_sb[:]))[-1])
        fs("vector", lambda e: e.tensor_copy(out=exf, in_=pTb2[:, 0:256]))
        for fc in range(4):
            fs("tensor", lambda e, fc=fc: (
                e.matmul(pB2[:, 0:512], exf[:, 0:128],
                         wdsb[:, 0 * D + fc * 512:0 * D + (fc + 1) * 512],
                         start=True, stop=False),
                e.matmul(pB2[:, 0:512], exf[:, 128:256],
                         wdsb[:, 1 * D + fc * 512:1 * D + (fc + 1) * 512],
                         start=False, stop=True))[-1])
            fs("scalar", lambda e, fc=fc: e.activation(out=dsb[:, fc * 512:(fc + 1) * 512],
                                                       in_=pB2[:, 0:512], func=AF.Sigmoid,
                                                       bias=0.0, scale=2.0))
            fs("vector", lambda e, fc=fc: e.tensor_scalar(dsb[:, fc * 512:(fc + 1) * 512],
                                                          dsb[:, fc * 512:(fc + 1) * 512],
                                                          2.0, -1.0, op0=ALU.mult, op1=ALU.add))
            fs("tensor", lambda e, fc=fc: (
                e.matmul(pB2[:, 0:512], exf[:, 0:128],
                         wusb[:, 0 * D + fc * 512:0 * D + (fc + 1) * 512],
                         start=True, stop=False),
                e.matmul(pB2[:, 0:512], exf[:, 128:256],
                         wusb[:, 1 * D + fc * 512:1 * D + (fc + 1) * 512],
                         start=False, stop=True))[-1])
            fs("vector", lambda e, fc=fc: e.tensor_copy(out=usb[:, fc * 512:(fc + 1) * 512],
                                                        in_=pB2[:, 0:512]))
        fs("vector", lambda e: e.tensor_mul(ya[:], ya[:], nwb[:]))
        fs("vector", lambda e: e.tensor_mul(ya[:], ya[:], dsb[:]))
        fs("vector", lambda e: e.tensor_mul(ya[:], ya[:], usb[:]))
        fs("vector", lambda e: e.tensor_scalar_mul(ya[:], ya[:], tcol[:, 1:2]))
        fs("scalar", lambda e: e.activation(out=yb[:], in_=ya[:], func=AF.Square,
                                            bias=0.0, scale=1.0, accum_out=yst[:, 1:2]))
        fs("vector", lambda e: e.reduce_sum(out=yst[:, 0:1], in_=ya[:], axis=AX.X))
        fs("vector", lambda e: e.tensor_scalar_mul(yst[:, 2:3], yst[:, 0:1], 1.0 / D))
        fs("vector", lambda e: e.scalar_tensor_tensor(out=yst[:, 3:4], in0=yst[:, 2:3],
                                                      scalar=-1.0, in1=yst[:, 2:3],
                                                      op0=ALU.mult, op1=ALU.mult))
        fs("vector", lambda e: e.scalar_tensor_tensor(out=yst[:, 3:4], in0=yst[:, 1:2],
                                                      scalar=1.0 / D, in1=yst[:, 3:4],
                                                      op0=ALU.mult, op1=ALU.add))
        fs("vector", lambda e: e.tensor_scalar_add(yst[:, 3:4], yst[:, 3:4], LN_EPS))
        fs("scalar", lambda e: e.activation(out=yst[:, 4:5], in_=yst[:, 3:4], func=AF.Sqrt,
                                            bias=0.0, scale=1.0))
        fs("vector", lambda e: e.reciprocal(out=yst[:, 4:5], in_=yst[:, 4:5]))
        fs("vector", lambda e: e.scalar_tensor_tensor(out=yst[:, 5:6], in0=yst[:, 2:3],
                                                      scalar=-1.0, in1=yst[:, 4:5],
                                                      op0=ALU.mult, op1=ALU.mult))
        ctx.dma("sync", yb[:], x[tt * 128:(tt + 1) * 128, :], "fxl",
                waits=[("fsq", ctx.v("fsq"))])
        fs("vector", lambda e: e.scalar_tensor_tensor(
            out=ya[:], in0=ya[:], scalar=yst[:, 4:5],
            in1=yst[:, 5:6].to_broadcast([128, D]), op0=ALU.mult, op1=ALU.add))
        ctx.emit("vector", lambda e: e.tensor_add(ya[:], ya[:], yb[:]),
                 inc=("fsq", 1), waits=[("fsq", ctx.v("fsq")), ("fxl", ctx.v("fxl"))])
        ctx.dma("gpsimd", x5[tt * 128:(tt + 1) * 128, :], ya[:], "fso",
                waits=[("fsq", ctx.v("fsq"))])
        ctx.wait("sync", [("fso", ctx.v("fso"))]) if False else None
    ctx.wait("gpsimd", [("fso", ctx.v("fso"))])
    return ctx


def build_l3a(ctx=None, io=None):
    ctx = ctx or Ctx()
    io = io or {}
    x = _io(ctx, io, "in", "xin", [TPC, D])
    wq = _io(ctx, io, "in", "wq", [D, D])
    bq = _io(ctx, io, "in", "bq", [D])
    wk = _io(ctx, io, "in", "wk", [D, D])
    bk = _io(ctx, io, "in", "bk", [D])
    wv = _io(ctx, io, "in", "wv", [D, D])
    bv = _io(ctx, io, "in", "bv", [D])
    phiq_d = _io(ctx, io, "out", "PHIQ", [TPC, D])
    kvc_d = _io(ctx, io, "out", "KVC", [TPC, D])
    car_d = _io(ctx, io, "out", "CARRY", [1, D])
    qp_d = ctx.dram_tmp("QP", [TPC, D])
    kp_d = ctx.dram_tmp("KP", [TPC, D])
    vp_d = ctx.dram_tmp("VP", [TPC, D])
    xnfm_d = ctx.dram_tmp("XNFM", [D, TPC], BF16)
    ident_sb, ones_sb = setup_common(ctx)
    utd = ctx.const("l3_ut", np.triu(np.ones((128, 128), dtype=ml_dtypes.bfloat16)))
    uts = ctx.sbuf("uts", [128, 128], BF16)
    onesfd = ctx.const("onesf_l3a", np.ones((1, 128), dtype=np.float32))
    onesf = ctx.sbuf("onesf", [1, 128], F32)
    ctx.dma("sync", uts[:], utd, "lut")
    ctx.dma("sync", onesf[:], onesfd, "lut")

    xn_fm_t = ctx.sbuf("xn_fm_t", [128, ND * 128], BF16)
    for tt in range(NT):
        f1 = emit_ln_to_fm(ctx, "ln", x, xn_fm_t, ident_sb, n_tt=1, x_row0=tt * 128,
                           fm_stride=128, wait_extra=[("fmst", 16 * tt)])
        ctx.dma("sync", xnfm_d[:, tt * 128:(tt + 1) * 128].rearrange("(dc p) f -> p dc f", p=128),
                xn_fm_t[:].rearrange("p (dc f) -> p dc f", dc=ND), "fmst", waits=[f1])
    fmstv = ctx.v("fmst")

    xnc = ctx.sbuf("xnc", [128, 2 * ND * 512], BF16)
    prevs = []
    for ci in range(4):
        par = ci % 2
        ctx.dma("sync", xnc[:, par * ND * 512:(par + 1) * ND * 512]
                .rearrange("p (dc f) -> p dc f", dc=ND),
                xnfm_d[:, ci * 512:(ci + 1) * 512].rearrange("(dc p) f -> p dc f", p=128),
                "xcl%d" % par, waits=[("fmst", fmstv)] + prevs)
        fmr = ("xcl%d" % par, 16 * (ci // 2 + 1))
        prevs2 = []
        for pre_, wd, bd_, od in [("pq", wq, bq, qp_d), ("pk", wk, bk, kp_d),
                                  ("pv", wv, bv, vp_d)]:
            fin = emit_proj(ctx, pre_, xnc, wd, od, n_tok=512, fm_stride=512,
                            fm_base=par * ND * 512, mode="tok", F=D, fm_ready=fmr,
                            psA="pA", psB="pB", wname="pwA", oname="poA",
                            bias_dram=bd_, ones_sb=ones_sb, out_stage_dtype=F32,
                            out_row0=ci * 512,
                            wait_extra=prevs + prevs2 + [("pqev", ctx.v("pqev")),
                                                         ("pkev", ctx.v("pkev")),
                                                         ("pvev", ctx.v("pvev"))])
            prevs2 = prevs2 + fin
        prevs = prevs2
    pjf = prevs

    qa = ctx.sbuf("qa", [128, D], F32)
    ka = ctx.sbuf("ka", [128, D], F32)
    va = ctx.sbuf("va", [128, D], F32)
    tb = ctx.sbuf("tb", [128, D], F32)
    kvb = ctx.sbuf("kvb", [128, D], BF16)
    cum = ctx.sbuf("cum", [128, D], F32)
    crow = ctx.sbuf("crow", [1, D], F32)
    cps = ctx.psum("pA", [128, 512])
    ctx.emit("vector", lambda e: e.memset(crow[:], 0.0), inc=("csq", 1), waits=[])
    for tt in range(NT):
        ctx.dma("sync", qa[:], qp_d[tt * 128:(tt + 1) * 128, :], "cld",
                waits=[("csq", ctx.v("csq")), ("cst", ctx.v("cst"))] + (pjf if tt < 2 else []))
        ctx.dma("sync", ka[:], kp_d[tt * 128:(tt + 1) * 128, :], "cld")
        ctx.dma("sync", va[:], vp_d[tt * 128:(tt + 1) * 128, :], "cld")
        cldv = ctx.v("cld")

        def cs(lane, fn):
            ctx.emit(lane, fn, inc=("csq", 1), waits=[("csq", ctx.v("csq")),
                                                      ("cld", cldv), ("lut", 32)])
        for buf, outd in [(qa, phiq_d), (ka, None)]:
            cs("vector", lambda e, b=buf: e.tensor_scalar_min(tb[:], b[:], 0.0))
            cs("scalar", lambda e: e.activation(out=tb[:], in_=tb[:], func=AF.Exp,
                                                bias=0.0, scale=1.0))
            cs("vector", lambda e, b=buf: e.tensor_scalar_max(b[:], b[:], 0.0))
            cs("vector", lambda e, b=buf: e.tensor_add(b[:], b[:], tb[:]))
            if outd is not None:
                ctx.dma("gpsimd", outd[tt * 128:(tt + 1) * 128, :], buf[:], "cst",
                        waits=[("csq", ctx.v("csq"))])
        cs("vector", lambda e: e.tensor_mul(va[:], ka[:], va[:]))
        cs("vector", lambda e: e.tensor_copy(out=kvb[:], in_=va[:]))
        for fc in range(4):
            cs("tensor", lambda e, fc=fc: (
                e.matmul(cps[:, 0:512], uts[:], kvb[:, fc * 512:(fc + 1) * 512],
                         start=True, stop=False),
                e.matmul(cps[:, 0:512], onesf[:], crow[0:1, fc * 512:(fc + 1) * 512],
                         start=False, stop=True))[-1])
            cs("vector", lambda e, fc=fc: e.tensor_copy(out=cum[:, fc * 512:(fc + 1) * 512],
                                                        in_=cps[:, 0:512]))
        ctx.dma("gpsimd", kvc_d[tt * 128:(tt + 1) * 128, :], cum[:], "cst",
                waits=[("csq", ctx.v("csq"))])
        ctx.dma("gpsimd", crow[:], cum[127:128, :], "ccr",
                waits=[("csq", ctx.v("csq"))])
        ctx.wait("tensor", [("ccr", ctx.v("ccr"))])
        ctx.wait("vector", [("ccr", ctx.v("ccr"))])
    ctx.dma("gpsimd", car_d, crow[:], "cst", waits=[("ccr", ctx.v("ccr"))])
    ctx.wait("gpsimd", [("cst", ctx.v("cst"))])
    return ctx


def build_l3b(ctx=None, io=None):
    ctx = ctx or Ctx()
    io = io or {}
    x = _io(ctx, io, "in", "xin", [TPC, D])
    phiq_d = _io(ctx, io, "in", "PHIQ", [TPC, D])
    kvc_d = _io(ctx, io, "in", "KVC", [TPC, D])
    scar = _io(ctx, io, "in", "SCAR", [1, D])
    wo = _io(ctx, io, "in", "wo", [D, D])
    bo = _io(ctx, io, "in", "bo", [D])
    x6 = _io(ctx, io, "out", "X6", [TPC, D])
    ctx_d = ctx.dram_tmp("CTX", [TPC, D])
    ao = ctx.dram_tmp("AO", [TPC, D])
    ident_sb, ones_sb = setup_common(ctx)
    onesfd = ctx.const("onesf_l3b", np.ones((1, 128), dtype=np.float32))
    onesf = ctx.sbuf("onesf", [1, 128], F32)
    scr = ctx.sbuf("scr", [1, D], F32)
    scb = ctx.sbuf("scb", [128, D], F32)
    sps = ctx.psum("pA", [128, 512])
    ctx.dma("sync", onesf[:], onesfd, "lsc")
    ctx.dma("sync", scr[:], scar, "lsc")
    for i in range(4):
        ctx.emit("tensor", lambda e, i=i: e.matmul(sps[:, 0:512], onesf[:],
                                                   scr[0:1, i * 512:(i + 1) * 512],
                                                   start=True, stop=True),
                 inc=("scm", 1), waits=[("lsc", 32), ("sce", ctx.v("sce"))])
        ctx.emit("vector", lambda e, i=i: e.tensor_copy(out=scb[:, i * 512:(i + 1) * 512],
                                                        in_=sps[:, 0:512]),
                 inc=("sce", 1), waits=[("scm", ctx.v("scm"))])
    scev = ctx.v("sce")
    SCALE = float(128 ** (-0.5))
    pa = ctx.sbuf("pa", [128, D], F32)
    kb = ctx.sbuf("kb", [128, D], F32)
    for tt in range(NT):
        ctx.dma("sync", pa[:], phiq_d[tt * 128:(tt + 1) * 128, :], "bld",
                waits=[("bst", 16 * tt)])
        ctx.dma("sync", kb[:], kvc_d[tt * 128:(tt + 1) * 128, :], "bld")
        bldv = ctx.v("bld")
        ctx.emit("vector", lambda e: e.tensor_add(kb[:], kb[:], scb[:]),
                 inc=("bsq", 1), waits=[("bld", bldv), ("sce", scev),
                                        ("bsq", ctx.v("bsq"))])
        ctx.emit("vector", lambda e: e.tensor_mul(pa[:], pa[:], kb[:]),
                 inc=("bsq", 1), waits=[("bsq", ctx.v("bsq"))])
        ctx.emit("vector", lambda e: e.tensor_scalar_mul(pa[:], pa[:], SCALE),
                 inc=("bsq", 1), waits=[("bsq", ctx.v("bsq"))])
        ctx.dma("gpsimd", ctx_d[tt * 128:(tt + 1) * 128, :], pa[:], "bst",
                waits=[("bsq", ctx.v("bsq"))])
    bstv = ctx.v("bst")
    cfm = ctx.sbuf("cfm", [128, ND * 512], BF16)
    pfin = []
    for c in range(4):
        ctr = emit_cast_transpose(ctx, "ct", ctx_d, cfm, ident_sb, n_tt=4, nd=ND,
                                  src_row0=c * 512, fm_stride=512,
                                  wait_extra=[("bst", bstv), ("pmm", ctx.v("pmm"))])
        pfin = emit_proj(ctx, "p", cfm, wo, ao, n_tok=512, F=D, bias_dram=bo,
                         ones_sb=ones_sb, fm_ready=ctr, psA="pA", psB="pB",
                         out_stage_dtype=F32, out_row0=c * 512)
    lfin = emit_ln_add(ctx, "le", ao, x, x6, wait_extra=pfin)
    ctx.wait("gpsimd", [lfin])
    return ctx


def build_l3c(ctx=None, io=None):
    ctx = ctx or Ctx()
    io = io or {}
    HP = 5632
    NREAL = 5461 * 2048
    x = _io(ctx, io, "in", "xin", [TPC, D])
    w1t = _io(ctx, io, "in", "w1t", [D, HP])
    w3t = _io(ctx, io, "in", "w3t", [D, HP])
    w2tt = _io(ctx, io, "in", "w2tt", [HP, D])
    x7 = _io(ctx, io, "out", "X7", [TPC, D])
    wq_d = [ctx.dram_tmp("W1Q", [D, HP], BF16), ctx.dram_tmp("W3Q", [D, HP], BF16),
            ctx.dram_tmp("W2Q", [HP, D], BF16)]
    xnfm_d = ctx.dram_tmp("XNFM", [D, TPC], BF16)
    h1_d = ctx.dram_tmp("H1", [TPC, HP])
    h3_d = ctx.dram_tmp("H3", [TPC, HP])
    g_d = ctx.dram_tmp("G", [TPC, HP], BF16)
    ao = ctx.dram_tmp("AO", [TPC, D])
    ident_sb, ones_sb = setup_common(ctx)
    onesfd = ctx.const("onesf_l3c", np.ones((1, 128), dtype=np.float32))
    onesf = ctx.sbuf("onesf", [1, 128], F32)
    onescd = ctx.const("onesc_l3c", np.ones((128, 1), dtype=np.float32))
    onesc = ctx.sbuf("onesc", [128, 1], F32)
    ctx.dma("sync", onesf[:], onesfd, "lof")
    ctx.dma("sync", onesc[:], onescd, "lof")

    wf = ctx.sbuf("wf", [128, HP], F32)
    wb = ctx.sbuf("wb", [128, HP], BF16)
    wtmp = ctx.sbuf("wtmp", [128, HP], BF16)
    acol = ctx.sbuf("acol", [128, 4], F32)
    scol = ctx.sbuf("scol", [128, 8], F32)
    qps = ctx.psum("pA", [128, 512])

    def quant_matrix(mi, wsrc, wdst, rows, width, ready=()):
        pre = "q%d" % mi
        ntt_ = rows // 128
        ctx.emit("vector", lambda e: e.memset(acol[:, 0:1], 0.0), inc=(pre + "s", 1),
                 waits=[(pre + "s", ctx.v(pre + "s"))] + list(ready))
        for tt in range(ntt_):
            ctx.dma("sync", wf[:, 0:width], wsrc[tt * 128:(tt + 1) * 128, :], pre + "l",
                    waits=[(pre + "s", ctx.v(pre + "s"))] + list(ready))
            lv = ctx.v(pre + "l")
            ctx.emit("scalar", lambda e, width=width: e.activation(
                out=wb[:, 0:width], in_=wf[:, 0:width], func=AF.Abs, bias=0.0, scale=1.0,
                accum_out=acol[:, 1:2]),
                inc=(pre + "s", 1), waits=[(pre + "l", lv), (pre + "s", ctx.v(pre + "s"))])
            ctx.emit("vector", lambda e: e.tensor_add(acol[:, 0:1], acol[:, 0:1], acol[:, 1:2]),
                     inc=(pre + "s", 1), waits=[(pre + "s", ctx.v(pre + "s"))])
        ctx.emit("tensor", lambda e: e.matmul(qps[0:1, 0:1], acol[:, 0:1], onesc[:],
                                              start=True, stop=True),
                 inc=(pre + "s", 1), waits=[(pre + "s", ctx.v(pre + "s")), ("lof", 32)])
        ctx.emit("vector", lambda e: e.tensor_copy(out=acol[:, 2:3][0:1, :], in_=qps[0:1, 0:1]),
                 inc=(pre + "s", 1), waits=[(pre + "s", ctx.v(pre + "s"))])
        ctx.emit("tensor", lambda e: e.matmul(qps[:, 0:1], onesf[:], acol[0:1, 2:3],
                                              start=True, stop=True),
                 inc=(pre + "s", 1), waits=[(pre + "s", ctx.v(pre + "s"))])
        ctx.emit("vector", lambda e, mi=mi: e.tensor_scalar_mul(
            scol[:, 2 * mi:2 * mi + 1], qps[:, 0:1], 0.5 / NREAL),
            inc=(pre + "s", 1), waits=[(pre + "s", ctx.v(pre + "s"))])
        ctx.emit("vector", lambda e, mi=mi: e.tensor_scalar_mul(
            scol[:, 2 * mi + 1:2 * mi + 2], qps[:, 0:1], 1.0 / NREAL),
            inc=(pre + "s", 1), waits=[(pre + "s", ctx.v(pre + "s"))])
        ctx.emit("vector", lambda e, mi=mi: e.tensor_scalar_mul(
            acol[:, 3:4], scol[:, 2 * mi:2 * mi + 1], -1.0),
            inc=(pre + "s", 1), waits=[(pre + "s", ctx.v(pre + "s"))])
        for tt in range(ntt_):
            ctx.dma("sync", wf[:, 0:width], wsrc[tt * 128:(tt + 1) * 128, :], pre + "l",
                    waits=[(pre + "s", ctx.v(pre + "s")), (pre + "w", ctx.v(pre + "w"))])
            lv = ctx.v(pre + "l")
            ctx.emit("vector", lambda e, width=width, mi=mi: e.tensor_scalar(
                wb[:, 0:width], wf[:, 0:width], scol[:, 2 * mi:2 * mi + 1], None,
                op0=ALU.is_gt),
                inc=(pre + "s", 1), waits=[(pre + "l", lv), (pre + "s", ctx.v(pre + "s")),
                                           (pre + "w", ctx.v(pre + "w"))])
            ctx.emit("vector", lambda e, width=width: e.tensor_scalar(
                wtmp[:, 0:width], wf[:, 0:width], acol[:, 3:4], None, op0=ALU.is_lt),
                inc=(pre + "s", 1), waits=[(pre + "s", ctx.v(pre + "s"))])
            ctx.emit("vector", lambda e, width=width: e.tensor_sub(
                wb[:, 0:width], wb[:, 0:width], wtmp[:, 0:width]),
                inc=(pre + "s", 1), waits=[(pre + "s", ctx.v(pre + "s"))])
            ctx.dma("gpsimd", wdst[tt * 128:(tt + 1) * 128, :], wb[:, 0:width], pre + "w",
                    waits=[(pre + "s", ctx.v(pre + "s"))])
        return (pre + "w", ctx.v(pre + "w"))

    qf1 = quant_matrix(0, w1t, wq_d[0], D, HP)
    qf3 = quant_matrix(1, w3t, wq_d[1], D, HP, ready=[qf1])
    qf2 = quant_matrix(2, w2tt, wq_d[2], HP, D, ready=[qf3])

    xn_fm_t = ctx.sbuf("xn_fm_t", [128, ND * 128], BF16)
    for tt in range(NT):
        f1 = emit_ln_to_fm(ctx, "ln", x, xn_fm_t, ident_sb, n_tt=1, x_row0=tt * 128,
                           fm_stride=128, wait_extra=[("fmst", 16 * tt)])
        ctx.dma("sync", xnfm_d[:, tt * 128:(tt + 1) * 128].rearrange("(dc p) f -> p dc f", p=128),
                xn_fm_t[:].rearrange("p (dc f) -> p dc f", dc=ND), "fmst", waits=[f1])
    fmstv = ctx.v("fmst")

    xnc = ctx.sbuf("xnc", [128, 2 * ND * 512], BF16)
    prevs = [qf1, qf3, qf2]
    for ci in range(4):
        par = ci % 2
        ctx.dma("sync", xnc[:, par * ND * 512:(par + 1) * ND * 512]
                .rearrange("p (dc f) -> p dc f", dc=ND),
                xnfm_d[:, ci * 512:(ci + 1) * 512].rearrange("(dc p) f -> p dc f", p=128),
                "xcl%d" % par, waits=[("fmst", fmstv)] + prevs)
        fmr = ("xcl%d" % par, 16 * (ci // 2 + 1))
        p1 = emit_proj(ctx, "p1", xnc, wq_d[0], h1_d, n_tok=512, fm_stride=512,
                       fm_base=par * ND * 512, mode="tok", F=HP, fm_ready=fmr,
                       psA="pA", psB="pB", wname="pwA", oname="poA",
                       out_stage_dtype=F32, out_row0=ci * 512,
                       scale_ap=scol[:, 1:2], scale_ap_rdy=("q0s", ctx.v("q0s")),
                       wait_extra=prevs + [("p1ev", ctx.v("p1ev")), ("p3ev", ctx.v("p3ev"))])
        p3 = emit_proj(ctx, "p3", xnc, wq_d[1], h3_d, n_tok=512, fm_stride=512,
                       fm_base=par * ND * 512, mode="tok", F=HP, fm_ready=fmr,
                       psA="pA", psB="pB", wname="pwA", oname="poA",
                       out_stage_dtype=F32, out_row0=ci * 512,
                       scale_ap=scol[:, 3:4], scale_ap_rdy=("q1s", ctx.v("q1s")),
                       wait_extra=p1 + [("p3ev", ctx.v("p3ev"))])
        prevs = p1 + p3
    hfin = prevs

    ga = ctx.sbuf("wf", None, None) if False else None
    ha = ctx._sbufs["wf"][:, 0:HP]
    hb = ctx._sbufs["wb"][:, 0:HP]
    hg = ctx._sbufs["wtmp"][:, 0:HP]
    for tt in range(NT):
        ctx.dma("sync", ha, h1_d[tt * 128:(tt + 1) * 128, :], "gld",
                waits=[("gsq", ctx.v("gsq"))] + (hfin if tt < 2 else []))
        ctx.dma("gpsimd", hb, h3_d[tt * 128:(tt + 1) * 128, :], "gldg",
                waits=[("gsq", ctx.v("gsq"))] + (hfin if tt < 2 else []))
        glv, glgv = ctx.v("gld"), ctx.v("gldg")
        ctx.emit("scalar", lambda e: e.activation(out=hg, in_=ha, func=AF.Silu,
                                                  bias=0.0, scale=1.0),
                 inc=("gsq", 1), waits=[("gld", glv), ("gsq", ctx.v("gsq")),
                                        ("gst", 16 * tt)])
        ctx.emit("vector", lambda e: e.tensor_mul(hg, hg, hb),
                 inc=("gsq", 1), waits=[("gldg", glgv), ("gsq", ctx.v("gsq"))])
        ctx.dma("gpsimd", g_d[tt * 128:(tt + 1) * 128, :], hg, "gst",
                waits=[("gsq", ctx.v("gsq"))])
    gstv = ctx.v("gst")

    cfm = ctx.sbuf("cfm", [128, 44 * 512], BF16)
    pw2 = ctx._sbufs["xnc"][:, 0:22 * 512]
    ctx._psums["ct_tp"] = ctx._psums["ln_tp"]
    ctx._psums["ct_tq"] = ctx._psums["ln_tq"]
    po2 = ctx._sbufs["xnc"][:].bitcast(F32)[:, 5632:5632 + 4 * 512]
    ps4 = [ctx.psum("w2p%d" % i, [128, 512]) for i in range(4)]
    q2sv = ctx.v("q2s")
    for c in range(4):
        ctr = emit_cast_transpose(ctx, "ct", g_d, cfm, ident_sb, n_tt=4, nd=44,
                                  src_row0=c * 512, fm_stride=512,
                                  wait_extra=[("gst", gstv), ("p2m", ctx.v("p2m"))],
                                  src_bf=True)
        for fc in range(4):
            for half in range(2):
                nds = 22
                ctx.dma("gpsimd", pw2.rearrange("p (dc f) -> p dc f", f=512),
                        wq_d[2][half * 22 * 128:(half + 1) * 22 * 128,
                                fc * 512:(fc + 1) * 512]
                        .rearrange("(dc p) f -> p dc f", p=128),
                        "p2l", waits=[("p2m", ctx.v("p2m"))])
                p2lv = ctx.v("p2l")

                def w2fn(e, c=c, fc=fc, half=half):
                    ins = None
                    for tt in range(4):
                        for d_ in range(22):
                            gcol = (half * 22 + d_) * 512 + tt * 128
                            ins = e.matmul(ps4[tt][:],
                                           cfm[:, gcol:gcol + 128],
                                           pw2[:, d_ * 512:(d_ + 1) * 512],
                                           start=(half == 0 and d_ == 0),
                                           stop=(half == 1 and d_ == 21))
                    return ins
                ctx.emit("tensor", w2fn, inc=("p2m", 1),
                         waits=[("p2l", p2lv), ctr, ("p2e", ctx.v("p2e"))])
            for tt in range(4):
                ctx.emit("scalar", lambda e, tt=tt: e.activation(
                    out=po2[:, tt * 512:(tt + 1) * 512], in_=ps4[tt][:],
                    func=AF.Identity, bias=0.0, scale=scol[:, 5:6]),
                    inc=("p2e", 1), waits=[("p2m", ctx.v("p2m")), ("q2s", q2sv),
                                           ("p2st", 16 * (c * 4 * 4 + fc * 4))])
                ctx.dma("sync", ao[c * 512 + tt * 128:c * 512 + (tt + 1) * 128,
                                  fc * 512:(fc + 1) * 512],
                        po2[:, tt * 512:(tt + 1) * 512], "p2st",
                        waits=[("p2e", ctx.v("p2e"))])
    pfin = [("p2st", ctx.v("p2st"))]
    ctx._sbufs["le_y"] = ctx._sbufs["wf"][:, 0:2048]
    ctx._sbufs["le_x"] = ctx._sbufs["wf"][:, 2048:4096]
    ctx._sbufs["le_oo"] = ctx._sbufs["wb"][:].bitcast(F32)[:, 0:2048]
    ctx._sbufs["le_junk"] = ctx._sbufs["cfm"][:].bitcast(F32)[:, 0:2048]
    lfin = emit_ln_add(ctx, "le", ao, x, x7, wait_extra=pfin)
    ctx.wait("gpsimd", [lfin])
    return ctx



_BUILD_CACHE = {}


def ceil_div(a, b):
    return -(-a // b)


def emit_cast_transpose(ctx, pre, src_dram, fm_sb, ident_sb, n_tt=NT, nd=ND,
                        src_row0=0, fm_col0=0, fm_stride=None, wait_extra=(),
                        src_bf=False):
    if fm_stride is None:
        fm_stride = n_tt * 128
    Dw = nd * 128
    xnb = ctx.sbuf(pre + "_nb", [128, Dw], BF16)
    tp = ctx.psum(pre + "_tp", [128, 1024], BF16)
    tp2 = ctx.psum(pre + "_tq", [128, 1024], BF16)
    banks = [tp, tp2]
    for tt in range(n_tt):
        ctx.dma("gpsimd" if not src_bf else "sync", xnb[:],
                src_dram[src_row0 + tt * 128:src_row0 + (tt + 1) * 128, :],
                pre + "ld", waits=[(pre + "tp", ctx.v(pre + "tp"))] + list(wait_extra))
        ldv = ctx.v(pre + "ld")
        for g in range(ceil_div(nd, 8)):
            gsz = min(8, nd - g * 8)
            gidx = ctx.v(pre + "tp")
            bank = banks[gidx % 2]
            gw = [(pre + "ld", ldv)]
            if gidx >= 2:
                gw.append((pre + "ge", gidx - 1))
            for i in range(gsz):
                d = g * 8 + i
                ctx.emit("tensor",
                         lambda e, d=d, i=i, bank=bank: e.transpose(
                             bank[:, i * 128:(i + 1) * 128],
                             xnb[:, d * 128:(d + 1) * 128], ident_sb[:]),
                         inc=(pre + "tp", 1) if i == gsz - 1 else None,
                         waits=(gw if i == 0 else []))
            tpv = ctx.v(pre + "tp")
            for i in range(gsz):
                d = g * 8 + i
                col = d * fm_stride + fm_col0 + tt * 128
                ctx.emit("vector",
                         lambda e, i=i, col=col, bank=bank: e.tensor_copy(
                             out=fm_sb[:, col:col + 128], in_=bank[:, i * 128:(i + 1) * 128]),
                         inc=(pre + "ge", 1) if i == gsz - 1 else None,
                         waits=([(pre + "tp", tpv)] if i == 0 else []))
    return (pre + "ge", ctx.v(pre + "ge"))


def build_l0c(ctx=None, io=None):
    ctx = ctx or Ctx()
    io = io or {}
    x = _io(ctx, io, "in", "xin", [TPC, D])
    rw = _io(ctx, io, "in", "router_w", [D, 8])
    rb = _io(ctx, io, "in", "router_b", [8])
    ew = _io(ctx, io, "in", "expert_w", [8, D, D])
    eb = _io(ctx, io, "in", "expert_b", [8, D])
    pw = _io(ctx, io, "in", "proj_w", [D, D])
    pb = _io(ctx, io, "in", "proj_b", [D])
    nw = _io(ctx, io, "in", "norm_w", [D])
    x2 = _io(ctx, io, "out", "X2", [TPC, D])
    xnfm_d = ctx.dram_tmp("XNFM", [D, TPC], BF16)
    xn32_d = ctx.dram_tmp("XN32", [TPC, D])
    ewb = ctx.dram_tmp("EWB", [8, D, D], BF16)
    cmb_d = ctx.dram_tmp("CMB", [TPC, D])
    mo_d = ctx.dram_tmp("MO", [TPC, D])
    ident_sb, ones_sb = setup_common(ctx)

    stg = ctx.sbuf("stg", [128, 2048], BF16)
    for i in range(128):
        e8, hh = divmod(i, 16)
        ctx.dma("gpsimd", stg[:], ew[e8, hh * 128:(hh + 1) * 128, :],
                "pc", waits=[("pcs", 16 * i)])
        ctx.dma("sync", ewb[e8, hh * 128:(hh + 1) * 128, :], stg[:],
                "pcs", waits=[("pc", 16 * (i + 1))])
    pcsv = ctx.v("pcs")

    xn_fm_t = ctx.sbuf("xn_fm_t", [128, ND * 128], BF16)
    for tt in range(NT):
        fmr1 = emit_ln_to_fm(ctx, "ln", x, xn_fm_t, ident_sb, n_tt=1, x_row0=tt * 128,
                             fm_stride=128, xn32_dram=xn32_d,
                             wait_extra=[("fmst", 16 * tt), ("lno3", 16 * tt)])
        ctx.dma("sync", xnfm_d[:, tt * 128:(tt + 1) * 128].rearrange("(dc p) f -> p dc f", p=128),
                xn_fm_t[:].rearrange("p (dc f) -> p dc f", dc=ND), "fmst", waits=[fmr1])
    fmstv = ctx.v("fmst")
    lno3v = ctx.v("lno3")

    rwsb = ctx.sbuf("rwsb", [128, ND * 8], BF16)
    rbsb = ctx.sbuf("rbsb", [1, 8], BF16)
    w8 = ctx.sbuf("w8", [128, NT * 8], F32)
    rlg = ctx.sbuf("rlg", [128, 8], F32)
    rl2 = ctx.sbuf("rl2", [128, 8], F32)
    req = ctx.sbuf("req", [128, 8], F32)
    rtmp = ctx.sbuf("rtmp", [128, 8], F32)
    rst = ctx.sbuf("rst", [128, 8], F32)
    rps = ctx.psum("rps", [128, 512])
    ctx.dma("gpsimd", rwsb[:].rearrange("p (dc f) -> p dc f", dc=ND),
            rw.rearrange("(dc p) f -> p dc f", p=128), "lrw")
    ctx.dma("gpsimd", rbsb[:], rb.rearrange("(o f) -> o f", o=1), "lrw")
    xfm_r = ctx.sbuf("xn_fm_t", [128, ND * 128], BF16)
    for tt in range(NT):
        ctx.dma("sync", xfm_r[:].rearrange("p (dc f) -> p dc f", dc=ND),
                xnfm_d[:, tt * 128:(tt + 1) * 128].rearrange("(dc p) f -> p dc f", p=128),
                "rld", waits=[("fmst", fmstv), ("rdn", ctx.v("rdn"))])

        def rfn(e, tt=tt):
            for d in range(ND):
                e.matmul(rps[:, 0:8], xfm_r[:, d * 128:(d + 1) * 128],
                         rwsb[:, d * 8:(d + 1) * 8], start=(d == 0), stop=False)
            return e.matmul(rps[:, 0:8], ones_sb[0:1, 0:128], rbsb[0:1, :],
                            start=False, stop=True)
        ctx.emit("tensor", rfn, inc=("rmm", 1),
                 waits=[("lrw", 32), ("rld", ctx.v("rld")), ("rcp", ctx.v("rcp"))])
        ctx.emit("vector", lambda e: e.tensor_copy(out=rlg[:], in_=rps[:, 0:8]),
                 inc=("rcp", 1), waits=[("rmm", ctx.v("rmm")), ("rdn", ctx.v("rdn"))])

        def step(fn, lane="vector"):
            ctx.emit(lane, fn, inc=("rdn", 1),
                     waits=[("rcp", ctx.v("rcp")), ("rdn", ctx.v("rdn"))])
        step(lambda e: e.reduce_max(out=rst[:, 0:1], in_=rlg[:], axis=AX.X))
        step(lambda e: e.tensor_scalar(req[:], rlg[:], rst[:, 0:1], None, op0=ALU.is_equal))
        step(lambda e: e.scalar_tensor_tensor(out=rl2[:], in0=req[:], scalar=-BIG, in1=rlg[:],
                                              op0=ALU.mult, op1=ALU.add))
        step(lambda e: e.reduce_max(out=rst[:, 1:2], in_=rl2[:], axis=AX.X))
        step(lambda e: e.tensor_sub(rst[:, 2:3], rst[:, 0:1], rst[:, 1:2]))
        step(lambda e: e.activation(out=rst[:, 3:4], in_=rst[:, 2:3],
                                    func=AF.Sigmoid, bias=0.0, scale=1.0), lane="scalar")
        step(lambda e: e.tensor_scalar(rst[:, 4:5], rst[:, 3:4], -1.0, 1.0,
                                       op0=ALU.mult, op1=ALU.add))
        step(lambda e: e.tensor_scalar(rtmp[:], rlg[:], rst[:, 1:2], None, op0=ALU.is_equal))
        step(lambda e: e.tensor_scalar_mul(req[:], req[:], rst[:, 3:4]))
        step(lambda e, tt=tt: e.scalar_tensor_tensor(out=w8[:, tt * 8:(tt + 1) * 8],
                                                     in0=rtmp[:], scalar=rst[:, 4:5],
                                                     in1=req[:], op0=ALU.mult, op1=ALU.add))
    w8v = ctx.v("rdn")

    CH = 512
    nch = TPC // CH
    ntt_c = CH // 128
    acc = ctx.sbuf("acc", [128, ntt_c * D], F32)
    xfm_c = ctx.sbuf("xfm_c", [128, 2 * ND * CH], BF16)
    ewsb = ctx.sbuf("ewsb", [128, 2 * ND * 512], BF16)
    ebsb = ctx.sbuf("ebsb", [1, 2 * D], BF16)
    ssb = ctx.sbuf("ssb", [128, 2 * 512], BF16)
    eps_ = [ctx.psum("ep0", [128, 512]), ctx.psum("ep1", [128, 512])]
    for c in range(nch):
        for e8 in range(8):
            li = c * 8 + e8
            par = li % 2
            ctx.dma("sync", xfm_c[:, par * ND * CH:(par + 1) * ND * CH]
                    .rearrange("p (dc f) -> p dc f", dc=ND),
                    xnfm_d[:, c * CH:(c + 1) * CH].rearrange("(dc p) f -> p dc f", p=128),
                    "xcl%d" % par,
                    waits=[("fmst", fmstv), ("emm", max(0, (li - 1) * 16))])
            ctx.dma("gpsimd", ebsb[0:1, par * D:(par + 1) * D],
                    eb[e8:e8 + 1, :], "ebl%d" % par,
                    waits=[("emm", max(0, (li - 1) * 16))])
            for fc in range(4):
                wi = li * 4 + fc
                wpar = wi % 2
                ctx.dma("gpsimd",
                        ewsb[:, wpar * ND * 512:(wpar + 1) * ND * 512]
                        .rearrange("p (dc f) -> p dc f", dc=ND),
                        ewb[e8, :, fc * 512:(fc + 1) * 512].rearrange("(dc p) f -> p dc f", p=128),
                        "ewl%d" % wpar,
                        waits=[("pcs", pcsv), ("emm", max(0, (wi - 1) * 4))])
                for tt in range(ntt_c):
                    job = wi * 4 + tt
                    bank = job % 2

                    def efn(e, par=par, wpar=wpar, fc=fc, tt=tt, bank=bank):
                        for d in range(ND):
                            e.matmul(eps_[bank][:],
                                     xfm_c[:, par * ND * CH + d * CH + tt * 128:
                                           par * ND * CH + d * CH + (tt + 1) * 128],
                                     ewsb[:, wpar * ND * 512 + d * 512:
                                          wpar * ND * 512 + (d + 1) * 512],
                                     start=(d == 0), stop=False)
                        return e.matmul(eps_[bank][:], ones_sb[0:1, 0:128],
                                        ebsb[0:1, par * D + fc * 512:par * D + (fc + 1) * 512],
                                        start=False, stop=True)
                    ctx.emit("tensor", efn, inc=("emm", 1),
                             waits=[("xcl%d" % par, 16 * (li // 2 + 1)),
                                    ("ebl%d" % par, 16 * (li // 2 + 1)),
                                    ("ewl%d" % wpar, 16 * (wi // 2 + 1)),
                                    ("esl", max(0, job - 1)), ("rdn", w8v)])
                    ctx.emit("scalar", lambda e, bank=bank: e.activation(
                        out=ssb[:, bank * 512:(bank + 1) * 512], in_=eps_[bank][:],
                        func=AF.Silu, bias=0.0, scale=1.0),
                        inc=("esl", 1), waits=[("emm", ctx.v("emm")),
                                               ("eac", max(0, job - 1))])
                    accsl = acc[:, tt * D + fc * 512: tt * D + (fc + 1) * 512]
                    wcol = w8[:, (c * ntt_c + tt) * 8 + e8:(c * ntt_c + tt) * 8 + e8 + 1]
                    if e8 == 0:
                        ctx.emit("vector", lambda e, bank=bank, accsl=accsl, wcol=wcol:
                                 e.tensor_scalar_mul(accsl, ssb[:, bank * 512:(bank + 1) * 512],
                                                     wcol),
                                 inc=("eac", 1), waits=[("esl", ctx.v("esl")),
                                                        ("cst", ctx.v("cst"))])
                    else:
                        ctx.emit("vector", lambda e, bank=bank, accsl=accsl, wcol=wcol:
                                 e.scalar_tensor_tensor(
                                     out=accsl, in0=ssb[:, bank * 512:(bank + 1) * 512],
                                     scalar=wcol, in1=accsl, op0=ALU.mult, op1=ALU.add),
                                 inc=("eac", 1), waits=[("esl", ctx.v("esl"))])
        eacv = ctx.v("eac")
        for tt in range(ntt_c):
            ctx.dma("gpsimd", cmb_d[c * CH + tt * 128:c * CH + (tt + 1) * 128, :],
                    acc[:, tt * D:(tt + 1) * D], "cst", waits=[("eac", eacv)])
    cstv = ctx.v("cst")

    cfm = ctx.sbuf("xfm_c", [128, 2 * ND * CH], BF16)
    pfin = []
    for c in range(nch):
        ctr = emit_cast_transpose(ctx, "ct", cmb_d, cfm, ident_sb, n_tt=ntt_c,
                                  src_row0=c * CH, fm_stride=CH,
                                  wait_extra=[("cst", cstv), ("pmm", ctx.v("pmm"))])
        pfin = emit_proj(ctx, "p", cfm, pw, mo_d, n_tok=CH, F=D, bias_dram=pb,
                         ones_sb=ones_sb, fm_ready=ctr, psA="ep0", psB="ep1",
                         out_stage_dtype=F32, out_row0=c * CH)

    nwb = ctx.sbuf("nwb", [128, D], F32)
    nwsb = ctx.sbuf("nwsb", [1, D], F32)
    onesfh = ctx.sbuf("onesfh", [1, 128], F32)
    onesfd = ctx.const("onesf_l0c", np.ones((1, 128), dtype=np.float32))
    ctx.dma("sync", nwsb[:], nw.rearrange("(o f) -> o f", o=1), "nwl")
    ctx.dma("sync", onesfh[:], onesfd, "nwl")
    for i in range(4):
        ctx.emit("tensor", lambda e, i=i: e.matmul(rps[:, 0:512], onesfh[:],
                                                   nwsb[0:1, i * 512:(i + 1) * 512],
                                                   start=True, stop=True),
                 inc=("nwm", 1), waits=[("nwl", 32), ("rcp", ctx.v("rcp")),
                                        ("nwe", ctx.v("nwe"))])
        ctx.emit("vector", lambda e, i=i: e.tensor_copy(out=nwb[:, i * 512:(i + 1) * 512],
                                                        in_=rps[:, 0:512]),
                 inc=("nwe", 1), waits=[("nwm", ctx.v("nwm"))])
    nwev = ctx.v("nwe")

    ya = ctx.sbuf("ln_x", [128, D], F32)
    yb = ctx.sbuf("ln_junk", [128, D], F32)
    yst = ctx.sbuf("yst", [128, 8], F32)
    yjunk = yb
    for tt in range(NT):
        w0 = [("yso", 16 * tt), ("lno3", lno3v)]
        if tt == 0:
            w0 += pfin
        ctx.dma("sync", ya[:], xn32_d[tt * 128:(tt + 1) * 128, :], "yld", waits=w0)
        ctx.dma("sync", yb[:], mo_d[tt * 128:(tt + 1) * 128, :], "yld")
        ylv = ctx.v("yld")

        def ystep(fn, lane="vector"):
            ctx.emit(lane, fn, inc=("yad", 1), waits=[("yad", ctx.v("yad"))])
        ctx.emit("vector", lambda e: e.tensor_add(ya[:], ya[:], yb[:]),
                 inc=("yad", 1), waits=[("yld", ylv), ("yad", ctx.v("yad"))])
        ystep(lambda e: e.activation(out=yjunk[:], in_=ya[:], func=AF.Square,
                                     bias=0.0, scale=1.0, accum_out=yst[:, 1:2]), "scalar")
        ystep(lambda e: e.tensor_scalar(yst[:, 3:4], yst[:, 1:2], 1.0 / D, 1e-6,
                                        op0=ALU.mult, op1=ALU.add))
        ystep(lambda e: e.activation(out=yst[:, 4:5], in_=yst[:, 3:4], func=AF.Sqrt,
                                     bias=0.0, scale=1.0), "scalar")
        ystep(lambda e: e.reciprocal(out=yst[:, 4:5], in_=yst[:, 4:5]))
        ystep(lambda e: e.tensor_scalar_mul(ya[:], ya[:], yst[:, 4:5]))
        ctx.emit("vector", lambda e: e.tensor_mul(ya[:], ya[:], nwb[:]),
                 inc=("yad", 1), waits=[("yad", ctx.v("yad")), ("nwe", nwev)])
        ystep(lambda e: e.activation(out=yjunk[:], in_=ya[:], func=AF.Square,
                                     bias=0.0, scale=1.0, accum_out=yst[:, 1:2]), "scalar")
        ystep(lambda e: e.reduce_sum(out=yst[:, 0:1], in_=ya[:], axis=AX.X))
        ystep(lambda e: e.tensor_scalar_mul(yst[:, 2:3], yst[:, 0:1], 1.0 / D))
        ystep(lambda e: e.scalar_tensor_tensor(out=yst[:, 3:4], in0=yst[:, 2:3],
                                               scalar=-1.0, in1=yst[:, 2:3],
                                               op0=ALU.mult, op1=ALU.mult))
        ystep(lambda e: e.scalar_tensor_tensor(out=yst[:, 3:4], in0=yst[:, 1:2],
                                               scalar=1.0 / D, in1=yst[:, 3:4],
                                               op0=ALU.mult, op1=ALU.add))
        ystep(lambda e: e.tensor_scalar_add(yst[:, 3:4], yst[:, 3:4], LN_EPS))
        ystep(lambda e: e.activation(out=yst[:, 4:5], in_=yst[:, 3:4], func=AF.Sqrt,
                                     bias=0.0, scale=1.0), "scalar")
        ystep(lambda e: e.reciprocal(out=yst[:, 4:5], in_=yst[:, 4:5]))
        ystep(lambda e: e.scalar_tensor_tensor(out=yst[:, 5:6], in0=yst[:, 2:3],
                                               scalar=-1.0, in1=yst[:, 4:5],
                                               op0=ALU.mult, op1=ALU.mult))
        ctx.dma("sync", yb[:], x[tt * 128:(tt + 1) * 128, :], "yxl",
                waits=[("yad", ctx.v("yad"))])
        ctx.emit("vector", lambda e: e.scalar_tensor_tensor(
            out=ya[:], in0=ya[:], scalar=yst[:, 4:5],
            in1=yst[:, 5:6].to_broadcast([128, D]), op0=ALU.mult, op1=ALU.add),
            inc=("yad", 1), waits=[("yad", ctx.v("yad"))])
        ctx.emit("vector", lambda e: e.tensor_add(ya[:], ya[:], yb[:]),
                 inc=("yad", 1), waits=[("yad", ctx.v("yad")), ("yxl", ctx.v("yxl"))])
        ctx.dma("gpsimd", x2[tt * 128:(tt + 1) * 128, :], ya[:], "yso",
                waits=[("yad", ctx.v("yad"))])
    ctx.wait("gpsimd", [("yso", ctx.v("yso"))])
    return ctx


def l0_masks():
    m = np.zeros((4, 128, 512), dtype=np.float32)
    k = np.arange(128)[:, None]
    q = np.arange(512)[None, :]
    for j in range(4):
        m[j] = np.where(k <= q - 128 * j, 0.0, -BIG)
    return m


def build_l0b(ctx=None, io=None):
    ctx = ctx or Ctx()
    io = io or {}
    qt = _io(ctx, io, "in", "QT", [D, TPC], BF16)
    kto = _io(ctx, io, "in", "KTo", [D, TPC], BF16)
    vo = _io(ctx, io, "in", "Vo", [TPC, D], BF16)
    kts = _io(ctx, io, "in", "KTs", [D, TPC], BF16)
    vs = _io(ctx, io, "in", "Vs", [TPC, D], BF16)
    wo = _io(ctx, io, "in", "wo", [D, D])
    x = _io(ctx, io, "in", "xin", [TPC, D])
    sibmask = _io(ctx, io, "in", "sibmask", [128, 1])
    masksd = ctx.const("l0masks", l0_masks())
    x1 = _io(ctx, io, "out", "X1", [TPC, D])
    ot = ctx.dram_tmp("OT", [D, TPC], BF16)
    ao = ctx.dram_tmp("AO", [TPC, D])
    ident_sb, ones_sb = setup_common(ctx)

    fl_fin = emit_l0_flash(ctx, "fl", qt, kto, kts, vo, vs, ot, sibmask, masksd)
    ot_fm = ctx.sbuf("ot_fm", [128, ND * TPC], BF16)
    for dd in range(ND):
        ctx.dma("sync", ot_fm[:, dd * TPC:(dd + 1) * TPC], ot[dd * 128:(dd + 1) * 128, :],
                "otld", waits=[fl_fin])
    fmr = ("otld", ctx.v("otld"))
    pfin = emit_proj(ctx, "wo", ot_fm, wo, ao, mode="tok", F=D, fm_ready=fmr,
                     psA="fl_psO", psB="fl_psL", out_stage_dtype=F32)
    lfin = emit_ln_add(ctx, "l2", ao, x, x1, wait_extra=pfin)
    ctx.wait("gpsimd", [lfin])
    return ctx


AG_PAIRS = [[0, 1], [2, 3], [4, 5], [6, 7]]


def build_mega():
    nc = bass.Bass("TRN2", target_bir_lowering=False, debug=False)
    mega_consts = {}

    def ein(name, shape, dtype=F32):
        return nc.dram_tensor(name, list(shape), dtype, kind="ExternalInput").ap()

    def internal(name, shape, dtype=F32):
        return nc.dram_tensor(name, list(shape), dtype, kind="Internal").ap()

    xin = ein("xin", [TPC, D])
    sibmask = ein("sibmask", [128, 1])
    grol = ein("grol", [128, 1])
    s0z = ein("S0z", [6, 256, 512])
    msk = ein("l2masks", [16, 128, 648], BF16)
    qcos = ein("qcos", [128, TPC])
    qsin = ein("qsin", [128, TPC])
    kcos = ein("kcos", [128, 2688])
    ksin = ein("ksin", [128, 2688])
    wn = {}
    for name, shape in [
            ("l0_wq", [D, D]), ("l0_wk", [D, D]), ("l0_wv", [D, D]), ("l0_wo", [D, D]),
            ("l0f_rw", [D, 8]), ("l0f_rb", [8]), ("l0f_ew", [8, D, D]),
            ("l0f_eb", [8, D]), ("l0f_pw", [D, D]), ("l0f_pb", [D]), ("l0f_nw", [D]),
            ("l1_wq", [D, 1536]), ("l1_wk0", [D, 1536]), ("l1_wk1", [D, 1536]),
            ("l1_wv0", [D, 3072]), ("l1_wv1", [D, 3072]), ("l1_wbab", [D, 18]),
            ("l1_dtb18", [18]), ("l1_wg", [D, 3072]), ("l1_cqw", [1536, 4]),
            ("l1_ckw0", [1536, 4]), ("l1_ckw1", [1536, 4]), ("l1_cvw0", [3072, 4]),
            ("l1_cvw1", [3072, 4]), ("l1_alog", [6]), ("l1_onw", [512]),
            ("l1_wo", [3072, D]), ("l1f_wt", [D, D]), ("l1f_sw", [D]), ("l1f_bias", [D]),
            ("l2_wq", [D, D]), ("l2_bq", [D]), ("l2_wk", [D, 512]), ("l2_bk", [512]),
            ("l2_wv", [D, 512]), ("l2_bv", [512]), ("l2_wo", [D, D]), ("l2_bo", [D]),
            ("l2f_wct", [D, 4096]), ("l2f_tau", [1]), ("l2f_genes", [4096, 256]),
            ("l2f_wdt", [256, D]), ("l2f_wut", [256, D]), ("l2f_nw", [D]),
            ("l2f_scl", [1]), ("l3_wq", [D, D]), ("l3_bq", [D]), ("l3_wk", [D, D]),
            ("l3_bk", [D]), ("l3_wv", [D, D]), ("l3_bv", [D]), ("l3_wo", [D, D]),
            ("l3_bo", [D]), ("l3f_w1t", [D, 5632]), ("l3f_w3t", [D, 5632]),
            ("l3f_w2tt", [5632, D])]:
        wn[name] = ein(name, shape)

    QT = internal("QT", [D, TPC], BF16)
    KT = internal("KT", [D, TPC], BF16)
    V = internal("V", [TPC, D], BF16)
    KTg = internal("KTg", [2 * D, TPC], BF16)
    Vg = internal("Vg", [2 * TPC, D], BF16)
    X1 = internal("X1", [TPC, D])
    X2 = internal("X2", [TPC, D])
    XBg = internal("XBg", [6, D])
    xbnd = internal("xbnd", [128, D])
    QFN = internal("QFN", [1536, TPC], BF16)
    KFN0 = internal("KFN0", [1536, TPC], BF16)
    KFN1 = internal("KFN1", [1536, TPC], BF16)
    V0T = internal("V0T", [TPC, 3072], BF16)
    V1T = internal("V1T", [TPC, 3072], BF16)
    B0 = internal("B0", [TPC, 6])
    B1 = internal("B1", [TPC, 6])
    G = internal("G", [TPC, 6])
    GG = internal("GG", [TPC, 3072])
    OD1 = internal("OD1", [TPC, 3072])
    SF1 = internal("SF1", [6, 256, 512])
    SFg = internal("SFg", [12, 256, 512])
    S0b = internal("S0b", [6, 256, 512])
    OD2 = internal("OD2", [TPC, 3072])
    SF2 = internal("SF2", [6, 256, 512])
    X3 = internal("X3", [TPC, D])
    X3b = internal("X3b", [TPC, D])
    XSa = internal("XSa", [8, D])
    XSb = internal("XSb", [1024, D])
    xsib = internal("xsib", [640, D])
    X4 = internal("X4", [TPC, D])
    X5 = internal("X5", [TPC, D])
    PHIQ = internal("PHIQ", [TPC, D])
    KVC = internal("KVC", [TPC, D])
    CARRY = internal("CARRY", [1, D])
    Cg = internal("Cg", [2, D])
    SCAR = internal("SCAR", [1, D])
    X6 = internal("X6", [TPC, D])
    X7 = nc.dram_tensor("X7", [TPC, D], F32, kind="ExternalOutput").ap()

    _ninstr = [0]

    def section(tag, fn):
        with nc.cleanup_on_exit():
            ctx = Ctx(nc=nc, tag=tag + "_")
            fn(ctx)
            ctx.wait("gpsimd", [(s, v) for s, v in sorted(ctx.semv.items()) if v > 0])
            ctx.finalize(close=False)
            nc.all_engine_barrier()
            mega_consts.update(ctx.consts)
            _ninstr[0] += sum(len(v) for v in ctx.lanes.values())

    def ag(tag, pairs):
        def fn(ctx):
            n = 0
            for (src, dst) in pairs:
                ctx.emit("gpsimd", lambda e, src=src, dst=dst: e.collective_compute(
                    "AllGather", ALU.bypass, replica_groups=AG_PAIRS,
                    ins=[src], outs=[dst]),
                    inc=("cd", 1), waits=[])
                n += 1
            ctx.wait("gpsimd", [("cd", n)])
        section(tag, fn)

    section("s00", lambda ctx: build_l0a(ctx=ctx, io={
        "xin": xin, "wq": wn["l0_wq"], "wk": wn["l0_wk"], "wv": wn["l0_wv"],
        "QT": QT, "KT": KT, "V": V}))
    ag("s01", [(KT, KTg), (V, Vg)])
    section("s02", lambda ctx: build_l0b(ctx=ctx, io={
        "QT": QT, "KTo": KT, "Vo": V, "KTs": KTg[0:D, :], "Vs": Vg[0:TPC, :],
        "wo": wn["l0_wo"], "xin": xin, "sibmask": sibmask, "X1": X1}))
    section("s03", lambda ctx: build_l0c(ctx=ctx, io={
        "xin": X1, "router_w": wn["l0f_rw"], "router_b": wn["l0f_rb"],
        "expert_w": wn["l0f_ew"], "expert_b": wn["l0f_eb"], "proj_w": wn["l0f_pw"],
        "proj_b": wn["l0f_pb"], "norm_w": wn["l0f_nw"], "X2": X2}))
    ag("s04", [(X2[TPC - 3:TPC, :], XBg)])

    def glue_xbnd(ctx):
        xb = ctx.sbuf("xb", [128, D], F32)
        ctx.emit("vector", lambda e: e.memset(xb[:], 0.0), inc=("ms", 1), waits=[])
        ctx.dma("sync", xb[125:128, :], XBg[0:3, :], "ld", waits=[("ms", 1)])
        gb = ctx.sbuf("gb", [128, 1], F32)
        ctx.dma("sync", gb[:], grol, "lg")
        ctx.emit("vector", lambda e: e.tensor_scalar_mul(xb[:], xb[:], gb[:, 0:1]),
                 inc=("mu", 1), waits=[("ld", 16), ("lg", 16)])
        ctx.dma("gpsimd", xbnd, xb[:], "st", waits=[("mu", 1)])
        ctx.wait("gpsimd", [("st", 16)])
    section("s05", glue_xbnd)
    section("s06", lambda ctx: build_l1a(ctx=ctx, io={
        "xin": X2, "xbnd": xbnd, "wq": wn["l1_wq"], "wk0": wn["l1_wk0"],
        "wk1": wn["l1_wk1"], "wv0": wn["l1_wv0"], "wv1": wn["l1_wv1"],
        "wbab": wn["l1_wbab"], "dtb18": wn["l1_dtb18"], "wg": wn["l1_wg"],
        "cqw": wn["l1_cqw"], "ckw0": wn["l1_ckw0"], "ckw1": wn["l1_ckw1"],
        "cvw0": wn["l1_cvw0"], "cvw1": wn["l1_cvw1"], "a_log": wn["l1_alog"],
        "QFN": QFN, "KFN0": KFN0, "KFN1": KFN1, "V0T": V0T, "V1T": V1T,
        "B0": B0, "B1": B1, "G": G, "GG": GG}))
    l1b_io = {"QFN": QFN, "KFN0": KFN0, "KFN1": KFN1, "V0T": V0T, "V1T": V1T,
              "B0": B0, "B1": B1, "G": G}
    section("s07", lambda ctx: build_l1b(ctx=ctx, io=dict(
        l1b_io, S0=s0z, OD=OD1, SF=SF1), emit_out=False))
    ag("s08", [(SF1, SFg)])

    def glue_s0(ctx):
        buf = ctx.sbuf("sb", [128, 512], F32)
        gb = ctx.sbuf("gb", [128, 1], F32)
        ctx.dma("sync", gb[:], grol, "lg")
        src = SFg[0:6].rearrange("h k v -> (h k) v")
        dst = S0b.rearrange("h k v -> (h k) v")
        for i in range(12):
            ctx.dma("sync", buf[:], src[i * 128:(i + 1) * 128, :], "ld",
                    waits=[("mu", ctx.v("mu")), ("st", 16 * i)])
            ctx.emit("vector", lambda e: e.tensor_scalar_mul(buf[:], buf[:], gb[:, 0:1]),
                     inc=("mu", 1), waits=[("ld", ctx.v("ld")), ("lg", 16)])
            ctx.dma("gpsimd", dst[i * 128:(i + 1) * 128, :], buf[:], "st",
                    waits=[("mu", ctx.v("mu"))])
        ctx.wait("gpsimd", [("st", 16 * 12)])
    section("s09", glue_s0)
    section("s10", lambda ctx: build_l1b(ctx=ctx, io=dict(
        l1b_io, S0=S0b, OD=OD2, SF=SF2)))
    section("s11", lambda ctx: build_l1c(ctx=ctx, io={
        "OD": OD2, "GG": GG, "onw": wn["l1_onw"], "wo": wn["l1_wo"],
        "xin": X2, "X3": X3}))
    section("s12", lambda ctx: build_l1d(ctx=ctx, io={
        "xin": X3, "wt": wn["l1f_wt"], "sw": wn["l1f_sw"], "bias": wn["l1f_bias"],
        "XO": X3b}))
    ag("s13", [(X3b[0:4, :], XSa), (X3b[TPC - 512:TPC, :], XSb)])

    def glue_xsib(ctx):
        zb = ctx.sbuf("zb", [128, D], F32)
        ctx.emit("vector", lambda e: e.memset(zb[:], 0.0), inc=("ms", 1), waits=[])
        ctx.dma("gpsimd", xsib[4:128, :], zb[0:124, :], "st", waits=[("ms", 1)])
        ctx.dma("sync", xsib[0:4, :], XSa[0:4, :], "st2")
        ctx.dma("sync", xsib[128:640, :], XSb[0:512, :], "st2")
        ctx.wait("gpsimd", [("st", 16), ("st2", 32)])
    section("s14", glue_xsib)
    section("s15", lambda ctx: build_l2a(ctx=ctx, io={
        "xin": X3b, "xsib": xsib, "wq": wn["l2_wq"], "bq": wn["l2_bq"],
        "wk": wn["l2_wk"], "bk": wn["l2_bk"], "wv": wn["l2_wv"], "bv": wn["l2_bv"],
        "wo": wn["l2_wo"], "bo": wn["l2_bo"], "l2masks": msk, "qcos": qcos,
        "qsin": qsin, "kcos": kcos, "ksin": ksin, "X4": X4}))
    section("s16", lambda ctx: build_l2c(ctx=ctx, io={
        "xin": X4, "wct": wn["l2f_wct"], "tau": wn["l2f_tau"],
        "genes": wn["l2f_genes"], "wdt": wn["l2f_wdt"], "wut": wn["l2f_wut"],
        "norm_w": wn["l2f_nw"], "scl": wn["l2f_scl"], "X5": X5}))
    section("s17", lambda ctx: build_l3a(ctx=ctx, io={
        "xin": X5, "wq": wn["l3_wq"], "bq": wn["l3_bq"], "wk": wn["l3_wk"],
        "bk": wn["l3_bk"], "wv": wn["l3_wv"], "bv": wn["l3_bv"],
        "PHIQ": PHIQ, "KVC": KVC, "CARRY": CARRY}))
    ag("s18", [(CARRY, Cg)])

    def glue_scar(ctx):
        row = ctx.sbuf("row", [1, D], F32)
        gb = ctx.sbuf("gb", [128, 1], F32)
        ctx.dma("sync", gb[:], grol, "lg")
        ctx.dma("sync", row[:], Cg[0:1, :], "lg")
        ctx.emit("vector", lambda e: e.tensor_scalar_mul(row[:], row[:], gb[0:1, 0:1]),
                 inc=("mu", 1), waits=[("lg", 32)])
        ctx.dma("gpsimd", SCAR, row[:], "st", waits=[("mu", 1)])
        ctx.wait("gpsimd", [("st", 16)])
    section("s19", glue_scar)
    section("s20", lambda ctx: build_l3b(ctx=ctx, io={
        "xin": X5, "PHIQ": PHIQ, "KVC": KVC, "SCAR": SCAR, "wo": wn["l3_wo"],
        "bo": wn["l3_bo"], "X6": X6}))
    section("s21", lambda ctx: build_l3c(ctx=ctx, io={
        "xin": X6, "w1t": wn["l3f_w1t"], "w3t": wn["l3f_w3t"],
        "w2tt": wn["l3f_w2tt"], "X7": X7}))
    print("[mega] total emitted thunks:", _ninstr[0])
    return nc, mega_consts


def _get(name, builder):
    if name not in _BUILD_CACHE:
        ctx = builder()
        nc = ctx.finalize()
        _BUILD_CACHE[name] = (nc, ctx.consts)
    return _BUILD_CACHE[name]


def _run(name, builder, per_core_inputs):
    nc, consts = _get(name, builder)
    in_maps = []
    for c in range(NCORES):
        m = dict(per_core_inputs[c])
        m.update(consts)
        in_maps.append(m)
    res = run_bass_kernel_spmd(nc, in_maps, core_ids=list(range(NCORES)))
    return res.results


_STAGE_DUMP = None


def _dump(name, cores, X):
    if _STAGE_DUMP is None:
        return
    full = np.empty((4, S, D), np.float32)
    for c, (b, r) in enumerate(cores):
        full[b, r * TPC:(r + 1) * TPC] = np.asarray(X[c], np.float32)
    _STAGE_DUMP[name] = full


def _run_mega(per_core_inputs):
    if "mega" not in _BUILD_CACHE:
        _BUILD_CACHE["mega"] = build_mega()
    nc, consts = _BUILD_CACHE["mega"]
    in_maps = []
    for c in range(NCORES):
        m = dict(per_core_inputs[c])
        m.update(consts)
        in_maps.append(m)
    return run_bass_kernel_spmd(nc, in_maps, core_ids=list(range(NCORES))).results


_FUSED = True


def kernel(**inputs):
    if not _FUSED:
        return _kernel_unfused(**inputs)
    pc, cores = _mega_inmaps(inputs)
    rr = _run_mega(pc)
    out = np.empty((4, S, D), np.float32)
    for c, (b, r) in enumerate(cores):
        out[b, r * TPC:(r + 1) * TPC] = rr[c]["X7"]
    return out


def _mega_inmaps(inputs):
    f32 = np.float32
    ct = np.ascontiguousarray
    x = np.asarray(inputs["L0_attn_x"], f32)
    cores = [(c // 2, c % 2) for c in range(NCORES)]
    xs = [ct(x[b, r * TPC:(r + 1) * TPC]) for b, r in cores]
    Wb = inputs["L1_attn_Wb"]
    HP = 5632
    w1 = np.asarray(inputs["L3_ffn_w1"], f32)
    w3 = np.asarray(inputs["L3_ffn_w3"], f32)
    w2 = np.asarray(inputs["L3_ffn_w2"], f32)
    wmap = {
        "l0_wq": inputs["L0_attn_w_q"], "l0_wk": inputs["L0_attn_w_k"],
        "l0_wv": inputs["L0_attn_w_v"], "l0_wo": inputs["L0_attn_w_o"],
        "l0f_rw": inputs["L0_ffn_router_w"], "l0f_rb": inputs["L0_ffn_router_b"],
        "l0f_ew": inputs["L0_ffn_expert_w"], "l0f_eb": inputs["L0_ffn_expert_b"],
        "l0f_pw": inputs["L0_ffn_proj_w"], "l0f_pb": inputs["L0_ffn_proj_b"],
        "l0f_nw": inputs["L0_ffn_norm_w"],
        "l1_wq": inputs["L1_attn_Wq"],
        "l1_wk0": inputs["L1_attn_Wk"][0], "l1_wk1": inputs["L1_attn_Wk"][1],
        "l1_wv0": inputs["L1_attn_Wv"][0], "l1_wv1": inputs["L1_attn_Wv"][1],
        "l1_wbab": ct(np.concatenate([Wb[0], Wb[1], inputs["L1_attn_Wa"]], axis=1)),
        "l1_dtb18": ct(np.concatenate([np.zeros(12, f32),
                                       np.asarray(inputs["L1_attn_dt_bias"], f32)])),
        "l1_wg": inputs["L1_attn_Wg"], "l1_cqw": inputs["L1_attn_conv_q_w"],
        "l1_ckw0": inputs["L1_attn_conv_k_w"][0],
        "l1_ckw1": inputs["L1_attn_conv_k_w"][1],
        "l1_cvw0": inputs["L1_attn_conv_v_w"][0],
        "l1_cvw1": inputs["L1_attn_conv_v_w"][1],
        "l1_alog": inputs["L1_attn_A_log"], "l1_onw": inputs["L1_attn_o_norm_w"],
        "l1_wo": inputs["L1_attn_Wo"],
        "l1f_wt": ct(np.asarray(inputs["L1_ffn_weight"], f32).T),
        "l1f_sw": inputs["L1_ffn_scale_weight"], "l1f_bias": inputs["L1_ffn_bias"],
        "l2_wq": inputs["L2_attn_Wq"], "l2_bq": inputs["L2_attn_bq"],
        "l2_wk": inputs["L2_attn_Wk"], "l2_bk": inputs["L2_attn_bk"],
        "l2_wv": inputs["L2_attn_Wv"], "l2_bv": inputs["L2_attn_bv"],
        "l2_wo": inputs["L2_attn_Wo"], "l2_bo": inputs["L2_attn_bo"],
        "l2f_wct": ct(np.asarray(inputs["L2_ffn_Wc"], f32).T),
        "l2f_tau": np.asarray(inputs["L2_ffn_temperature"], f32).reshape(1),
        "l2f_genes": inputs["L2_ffn_genes"],
        "l2f_wdt": ct(np.asarray(inputs["L2_ffn_Wd"], f32).T),
        "l2f_wut": ct(np.asarray(inputs["L2_ffn_Wu"], f32).T),
        "l2f_nw": inputs["L2_ffn_norm_w"],
        "l2f_scl": np.asarray(inputs["L2_ffn_scale"], f32).reshape(1),
        "l3_wq": ct(np.asarray(inputs["L3_attn_Wq"], f32).T),
        "l3_bq": inputs["L3_attn_bq"],
        "l3_wk": ct(np.asarray(inputs["L3_attn_Wk"], f32).T),
        "l3_bk": inputs["L3_attn_bk"],
        "l3_wv": ct(np.asarray(inputs["L3_attn_Wv"], f32).T),
        "l3_bv": inputs["L3_attn_bv"],
        "l3_wo": ct(np.asarray(inputs["L3_attn_Wo"], f32).T),
        "l3_bo": inputs["L3_attn_bo"],
        "l3f_w1t": ct(np.pad(w1.T, ((0, 0), (0, HP - w1.shape[0])))),
        "l3f_w3t": ct(np.pad(w3.T, ((0, 0), (0, HP - w3.shape[0])))),
        "l3f_w2tt": ct(np.pad(w2.T, ((0, HP - w2.shape[1]), (0, 0)))),
    }
    rope_q = [l2_rope_tables(r * TPC + np.arange(TPC)) for r in range(2)]
    rope_k = [l2_rope_tables(l2_klocal_pos(r)) for r in range(2)]
    msks = [l2_masks(r) for r in range(2)]
    sm0 = np.full((128, 1), -BIG, f32)
    sm1 = np.zeros((128, 1), f32)
    ZS = np.zeros((6, 256, 512), f32)
    pc = []
    for c, (b, r) in enumerate(cores):
        m = dict(wmap)
        m.update({"xin": xs[c], "sibmask": sm1 if r else sm0,
                  "grol": np.full((128, 1), float(r), f32), "S0z": ZS,
                  "l2masks": msks[r], "qcos": rope_q[r][0], "qsin": rope_q[r][1],
                  "kcos": rope_k[r][0], "ksin": rope_k[r][1]})
        pc.append(m)
    return pc, cores


def _kernel_unfused(**inputs):
    f32 = np.float32
    bfd = ml_dtypes.bfloat16
    ct = np.ascontiguousarray
    x = np.asarray(inputs["L0_attn_x"], f32)
    cores = [(c // 2, c % 2) for c in range(NCORES)]
    xs = [ct(x[b, r * TPC:(r + 1) * TPC]) for b, r in cores]

    pc = [{"xin": xs[c], "wq": inputs["L0_attn_w_q"], "wk": inputs["L0_attn_w_k"],
           "wv": inputs["L0_attn_w_v"]} for c in range(NCORES)]
    r0 = _run("l0a", build_l0a, pc)
    Z2 = np.zeros((D, TPC), bfd)
    Z3 = np.zeros((TPC, D), bfd)
    sm0 = np.full((128, 1), -BIG, f32)
    sm1 = np.zeros((128, 1), f32)
    pc = []
    for c, (b, r) in enumerate(cores):
        sib = 2 * b
        pc.append({"QT": r0[c]["QT"], "KTo": r0[c]["KT"], "Vo": r0[c]["V"],
                   "KTs": r0[sib]["KT"] if r else Z2,
                   "Vs": r0[sib]["V"] if r else Z3,
                   "wo": inputs["L0_attn_w_o"], "xin": xs[c],
                   "sibmask": sm1 if r else sm0})
    rr = _run("l0b", build_l0b, pc)
    X = [rr[c]["X1"] for c in range(NCORES)]
    _dump("x1", cores, X)

    pc = [{"xin": X[c], "router_w": inputs["L0_ffn_router_w"],
           "router_b": inputs["L0_ffn_router_b"],
           "expert_w": inputs["L0_ffn_expert_w"],
           "expert_b": inputs["L0_ffn_expert_b"],
           "proj_w": inputs["L0_ffn_proj_w"], "proj_b": inputs["L0_ffn_proj_b"],
           "norm_w": inputs["L0_ffn_norm_w"]} for c in range(NCORES)]
    rr = _run("l0c", build_l0c, pc)
    X = [rr[c]["X2"] for c in range(NCORES)]
    _dump("x2", cores, X)

    Wb = inputs["L1_attn_Wb"]
    wbab = ct(np.concatenate([Wb[0], Wb[1], inputs["L1_attn_Wa"]], axis=1))
    dtb18 = ct(np.concatenate([np.zeros(12, f32),
                               np.asarray(inputs["L1_attn_dt_bias"], f32)]))
    pc = []
    for c, (b, r) in enumerate(cores):
        xbnd = np.zeros((128, D), f32)
        if r:
            xbnd[125:128] = X[2 * b][TPC - 3:TPC]
        pc.append({"xin": X[c], "xbnd": xbnd, "wq": inputs["L1_attn_Wq"],
                   "wk0": inputs["L1_attn_Wk"][0], "wk1": inputs["L1_attn_Wk"][1],
                   "wv0": inputs["L1_attn_Wv"][0], "wv1": inputs["L1_attn_Wv"][1],
                   "wbab": wbab, "dtb18": dtb18, "wg": inputs["L1_attn_Wg"],
                   "cqw": inputs["L1_attn_conv_q_w"],
                   "ckw0": inputs["L1_attn_conv_k_w"][0],
                   "ckw1": inputs["L1_attn_conv_k_w"][1],
                   "cvw0": inputs["L1_attn_conv_v_w"][0],
                   "cvw1": inputs["L1_attn_conv_v_w"][1],
                   "a_log": inputs["L1_attn_A_log"]})
    r1a = _run("l1a", build_l1a, pc)
    ZS = np.zeros((6, 256, 512), f32)
    pc = [{"QFN": r1a[c]["QFN"], "KFN0": r1a[c]["KFN0"], "KFN1": r1a[c]["KFN1"],
           "V0T": r1a[c]["V0T"], "V1T": r1a[c]["V1T"], "B0": r1a[c]["B0"],
           "B1": r1a[c]["B1"], "G": r1a[c]["G"], "S0": ZS} for c in range(NCORES)]
    r1b = _run("l1b", build_l1b, pc)
    pc2 = []
    for c, (b, r) in enumerate(cores):
        m = dict(pc[c])
        m["S0"] = r1b[2 * b]["SF"] if r else ZS
        pc2.append(m)
    r1b = _run("l1b", build_l1b, pc2)
    pc = [{"OD": r1b[c]["OD"], "GG": r1a[c]["GG"],
           "onw": inputs["L1_attn_o_norm_w"], "wo": inputs["L1_attn_Wo"],
           "xin": X[c]} for c in range(NCORES)]
    rr = _run("l1c", build_l1c, pc)
    X = [rr[c]["X3"] for c in range(NCORES)]
    _dump("x3", cores, X)

    w1t = ct(np.asarray(inputs["L1_ffn_weight"], f32).T)
    pc = [{"xin": X[c], "wt": w1t, "sw": inputs["L1_ffn_scale_weight"],
           "bias": inputs["L1_ffn_bias"]} for c in range(NCORES)]
    rr = _run("l1d", build_l1d, pc)
    X = [rr[c]["XO"] for c in range(NCORES)]
    _dump("x3b", cores, X)

    rope_q = [l2_rope_tables(r * TPC + np.arange(TPC)) for r in range(2)]
    rope_k = [l2_rope_tables(l2_klocal_pos(r)) for r in range(2)]
    msks = [l2_masks(r) for r in range(2)]
    pc = []
    for c, (b, r) in enumerate(cores):
        xsib = np.zeros((640, D), f32)
        xsib[0:4] = X[2 * b][0:4]
        if r:
            xsib[128:640] = X[2 * b][1536:2048]
        pc.append({"xin": X[c], "xsib": xsib, "wq": inputs["L2_attn_Wq"],
                   "bq": inputs["L2_attn_bq"], "wk": inputs["L2_attn_Wk"],
                   "bk": inputs["L2_attn_bk"], "wv": inputs["L2_attn_Wv"],
                   "bv": inputs["L2_attn_bv"], "wo": inputs["L2_attn_Wo"],
                   "bo": inputs["L2_attn_bo"], "l2masks": msks[r],
                   "qcos": rope_q[r][0], "qsin": rope_q[r][1],
                   "kcos": rope_k[r][0], "ksin": rope_k[r][1]})
    rr = _run("l2a", build_l2a, pc)
    X = [rr[c]["X4"] for c in range(NCORES)]
    _dump("x4", cores, X)

    tau = np.asarray(inputs["L2_ffn_temperature"], f32).reshape(1)
    scl = np.asarray(inputs["L2_ffn_scale"], f32).reshape(1)
    wct = ct(np.asarray(inputs["L2_ffn_Wc"], f32).T)
    wdt = ct(np.asarray(inputs["L2_ffn_Wd"], f32).T)
    wut = ct(np.asarray(inputs["L2_ffn_Wu"], f32).T)
    pc = [{"xin": X[c], "wct": wct, "tau": tau, "genes": inputs["L2_ffn_genes"],
           "wdt": wdt, "wut": wut, "norm_w": inputs["L2_ffn_norm_w"],
           "scl": scl} for c in range(NCORES)]
    rr = _run("l2c", build_l2c, pc)
    X = [rr[c]["X5"] for c in range(NCORES)]
    _dump("x5", cores, X)

    wqT = ct(np.asarray(inputs["L3_attn_Wq"], f32).T)
    wkT = ct(np.asarray(inputs["L3_attn_Wk"], f32).T)
    wvT = ct(np.asarray(inputs["L3_attn_Wv"], f32).T)
    woT = ct(np.asarray(inputs["L3_attn_Wo"], f32).T)
    pc = [{"xin": X[c], "wq": wqT, "bq": inputs["L3_attn_bq"],
           "wk": wkT, "bk": inputs["L3_attn_bk"],
           "wv": wvT, "bv": inputs["L3_attn_bv"]}
          for c in range(NCORES)]
    r3a = _run("l3a", build_l3a, pc)
    Z1 = np.zeros((1, D), f32)
    pc = []
    for c, (b, r) in enumerate(cores):
        pc.append({"xin": X[c], "PHIQ": r3a[c]["PHIQ"], "KVC": r3a[c]["KVC"],
                   "SCAR": r3a[2 * b]["CARRY"] if r else Z1,
                   "wo": woT, "bo": inputs["L3_attn_bo"]})
    rr = _run("l3b", build_l3b, pc)
    X = [rr[c]["X6"] for c in range(NCORES)]
    _dump("x6", cores, X)

    HP = 5632
    w1 = np.asarray(inputs["L3_ffn_w1"], f32)
    w3 = np.asarray(inputs["L3_ffn_w3"], f32)
    w2 = np.asarray(inputs["L3_ffn_w2"], f32)
    w1t = ct(np.pad(w1.T, ((0, 0), (0, HP - w1.shape[0]))))
    w3t = ct(np.pad(w3.T, ((0, 0), (0, HP - w3.shape[0]))))
    w2tt = ct(np.pad(w2.T, ((0, HP - w2.shape[1]), (0, 0))))
    pc = [{"xin": X[c], "w1t": w1t, "w3t": w3t, "w2tt": w2tt}
          for c in range(NCORES)]
    rr = _run("l3c", build_l3c, pc)

    out = np.empty((4, S, D), f32)
    for c, (b, r) in enumerate(cores):
        out[b, r * TPC:(r + 1) * TPC] = rr[c]["X7"]
    return out
